# version 05
# baseline (speedup 1.0000x reference)
import contextlib
import numpy as np

import concourse.bass as bass
import concourse.mybir as mybir
from concourse.bass_utils import run_bass_kernel_spmd

F32 = mybir.dt.float32
I32 = mybir.dt.int32
AF = mybir.ActivationFunctionType
OP = mybir.AluOpType
AX = mybir.AxisListType

T = 2048
NT = T // 128
D = 1024
NI = D // 128
H = 16
DH = 64
EPS_LN = 1e-5
EPS_L2 = 1e-12
FF = 512
HID = 3072
HC = 3
NHG = (HID // 128) // HC
CS = 1024
NCH = T // CS
LOG_N = float(np.log(8192.0))
QSCALE = 0.125


QK_MODE = "full"


class Builder:
    def __init__(self, tap_names=(), stop_after=None):
        self.nc = bass.Bass("TRN2", target_bir_lowering=False,
                            detect_race_conditions=False)
        self.tap_names = set(tap_names)
        self.stop_after = stop_after
        self.stack = contextlib.ExitStack()
        self.substacks = []

    GP_NAMES = ("sto", "stoS", "sp", "ccl", "outd")

    def sems(self, st, names):
        out, i = {}, 0
        for n in names:
            if n in self.GP_NAMES:
                out[n] = self.stack.enter_context(self.nc.semaphore(self._uniq(n)))
            else:
                out[n] = self.pool[i]
                i += 1
        return out

    def clear_block(self):
        nc = self.nc
        with nc.Block() as blk:
            @blk.sync
            def _(sync):
                for sem in self.pool:
                    sync.sem_clear(sem)

    _n = 0

    def _uniq(self, n):
        Builder._n += 1
        return f"{n}_{Builder._n}"

    def sb(self, st, name, shape, dtype=F32):
        return st.enter_context(self.nc.sbuf_tensor(self._uniq(name), shape, dtype))

    def ps(self, st, name, shape):
        return st.enter_context(self.nc.psum_tensor(self._uniq(name), shape, F32))

    def build(self):
        nc = self.nc
        big = self.stack

        dt_ = nc.dram_tensor
        self.x_in = dt_("x_in", [D, T], F32, kind="ExternalInput")
        self.l0_wqkv = dt_("l0_wqkv", [D, 3 * D], F32, kind="ExternalInput")
        self.l0_bqkv = dt_("l0_bqkv", [1, 3 * D], F32, kind="ExternalInput")
        self.l0_wo = dt_("l0_wo", [D, D], F32, kind="ExternalInput")
        self.l0_bo = dt_("l0_bo", [128, 8], F32, kind="ExternalInput")
        self.l0_m = dt_("l0_m", [1, H], F32, kind="ExternalInput")
        self.l0_SR = dt_("l0_SR", [4 * FF, FF], F32, kind="ExternalInput")
        self.l0_TR = dt_("l0_TR", [4 * FF, FF], F32, kind="ExternalInput")
        self.l0_SI = dt_("l0_SI", [4 * FF, FF], F32, kind="ExternalInput")
        self.l0_TI = dt_("l0_TI", [4 * FF, FF], F32, kind="ExternalInput")
        self.l0_brbi = dt_("l0_brbi", [128, 8], F32, kind="ExternalInput")
        self.l1_wqkvT = dt_("l1_wqkvT", [D, 3 * D], F32, kind="ExternalInput")
        self.l1_woutT = dt_("l1_woutT", [D, D], F32, kind="ExternalInput")
        self.l1_bout = dt_("l1_bout", [128, 8], F32, kind="ExternalInput")
        self.l1_lam1 = dt_("l1_lam1", [2, 64], F32, kind="ExternalInput")
        self.l1_lam2T = dt_("l1_lam2T", [64, H], F32, kind="ExternalInput")
        self.l1_lam2b = dt_("l1_lam2b", [1, H], F32, kind="ExternalInput")
        self.l1_wgT = dt_("l1_wgT", [D, HID], I32, kind="ExternalInput")
        self.l1_wuT = dt_("l1_wuT", [D, HID], I32, kind="ExternalInput")
        self.l1_wdT = dt_("l1_wdT", [HID, D], I32, kind="ExternalInput")
        self.ident_in = dt_("ident_in", [128, 128], F32, kind="ExternalInput")
        self.posb_in = dt_("posb_in", [1, 1], F32, kind="ExternalInput")
        self.y_out = dt_("y_out", [T, D], F32, kind="ExternalOutput")

        self.xspill = dt_("xspill", [D, T], F32)
        self.q_tok = dt_("q_tok", [T, D], F32)
        self.k_tok = dt_("k_tok", [T, D], F32)
        self.v_tok = dt_("v_tok", [T, D], F32)
        self.qhat_f = dt_("qhat_f", [D, T], F32)
        self.s0_f = dt_("s0_f", [D, T], F32)
        self.s_f = dt_("s_f", [D, T], F32)
        self.x2spill = dt_("x2spill", [D, T], F32)
        self.xn1spill = dt_("xn1spill", [D, T], F32)
        self.qf_sp = dt_("qf_sp", [D, T], F32)
        self.kf_sp = dt_("kf_sp", [D, T], F32)
        self.vtok_sp = dt_("vtok_sp", [T, D], F32)
        self.attnf_sp = dt_("attnf_sp", [D, T], F32)
        self.s1_sp = dt_("s1_sp", [D, T], F32)
        self.cc_in = dt_("cc_in", [64, D], F32)
        self.cc_out = dt_("cc_out", [64, D], F32)
        self.xn0spill = dt_("xn0spill", [D, T], F32)

        self._tappable = {
            "xn0": (self.xn0spill, [D, T]), "q_tok": (self.q_tok, [T, D]),
            "k_tok": (self.k_tok, [T, D]), "v_tok": (self.v_tok, [T, D]),
            "qhat_f": (self.qhat_f, [D, T]), "kv": (self.cc_out, [64, D]),
            "s0_f": (self.s0_f, [D, T]), "s_f": (self.s_f, [D, T]),
            "x2": (self.x2spill, [D, T]), "xn1": (self.xn1spill, [D, T]),
            "qf": (self.qf_sp, [D, T]), "kf": (self.kf_sp, [D, T]),
            "vtok": (self.vtok_sp, [T, D]), "attnf": (self.attnf_sp, [D, T]),
            "s1": (self.s1_sp, [D, T]),
        }
        self.taps = {}
        for name in self.tap_names:
            src, shape = self._tappable[name]
            self.taps[name] = (src, dt_("tap_" + name, shape, F32,
                                        kind="ExternalOutput"))

        self.pool = [big.enter_context(nc.semaphore(f"sp{i}")) for i in range(26)]

        self.ident = self.sb(big, "ident", [128, 128])
        self.ones_col = self.sb(big, "ones_col", [128, 1])
        self.ones2 = self.sb(big, "ones2", [128, 2])
        self.ones_row = self.sb(big, "ones_row", [1, 512])
        self.ones_r1 = self.sb(big, "ones_r1", [1, 128])
        self.bo_cols = self.sb(big, "bo_cols", [128, 8])
        self.brbi_cols = self.sb(big, "brbi_cols", [128, 8])
        self.bout_cols = self.sb(big, "bout_cols", [128, 8])
        self.m_row = self.sb(big, "m_row", [1, H])
        self.posb = self.sb(big, "posb", [1, 1])
        self.xt = [self.sb(big, f"xt{i}", [128, T]) for i in range(NI)]
        self.xnt = [self.sb(big, f"xnt{i}", [128, T]) for i in range(NI)]

        def stop(name):
            return self.stop_after is not None and name == self.stop_after

        done = False
        self.p_load()
        if not stop("load"):
            self.clear_block()
            self.p_ln("ln0", self.xt, self.xnt,
                      tap_spill=self.xn0spill if "xn0" in self.tap_names else None)
        else:
            done = True
        if not done and not stop("ln0"):
            self.clear_block()
            self.p_l0_qkv()
        else:
            done = True
        if not done and not stop("qkv"):
            self.clear_block()
            self.p_l0_kv()
        else:
            done = True
        if not done and not stop("kv"):
            self.clear_block()
            self.p_l0_cc_qhat()
        else:
            done = True
        if not done and not stop("cc"):
            self.clear_block()
            self.p_l0_s0()
            self.kv_stack.close()
        else:
            done = True
        if not done and not stop("s0"):
            self.clear_block()
            self.p_wtbuild()
            self.clear_block()
            self.p_l0_oproj_ffn()
            self.wt_stack.close()
        else:
            done = True
        if not done and not stop("oproj"):
            self.clear_block()
            self.p_ln_reload("ln1", self.x2spill, self.xt, self.xnt,
                             tap_spill=self.xn1spill)
        else:
            done = True
        if not done and not stop("ln1"):
            self.clear_block()
            self.p_l1_qk()
        else:
            done = True
        if not done and not stop("qk"):
            self.clear_block()
            self.p_l1_lam_bound()
        else:
            done = True
        if not done and not stop("lam"):
            self.clear_block()
            self.p_l1_v()
        else:
            done = True
        if not done and not stop("v"):
            self.clear_block()
            self.p_l1_attn()
            self.l1a_stack.close()
        else:
            done = True
        if not done and not stop("attn"):
            self.clear_block()
            self.p_l1_outproj()
        else:
            done = True
        if not done and not stop("outproj"):
            self.clear_block()
            self.p_l1_ffn_final()
        else:
            done = True
        self.clear_block()
        self.p_taps()

        for stk in reversed(self.substacks):
            stk.close()
        self.stack.close()
        return self.nc

    def p_load(self):
        nc = self.nc
        with contextlib.ExitStack() as st:
            s = self.sems(st, ["ld", "sto"])
            n = [0]
            with nc.Block() as blk:
                @blk.sync
                def _(sync):
                    for i in range(NI):
                        sync.dma_start(out=self.xt[i][:],
                                       in_=self.x_in.ap()[128 * i:128 * (i + 1), :]).then_inc(s["ld"], 16)
                        n[0] += 16
                    for dst, src in [(self.ident, self.ident_in),
                                     (self.bo_cols, self.l0_bo),
                                     (self.brbi_cols, self.l0_brbi),
                                     (self.bout_cols, self.l1_bout),
                                     (self.m_row, self.l0_m),
                                     (self.posb, self.posb_in)]:
                        sync.dma_start(out=dst[:], in_=src.ap()[:, :]).then_inc(s["ld"], 16)
                        n[0] += 16

                @blk.gpsimd
                def _(gpsimd):
                    gpsimd.dma_start(out=self.xspill.ap()[:, :],
                                     in_=self.x_in.ap()[:, :]).then_inc(s["sto"], 16)

                @blk.vector
                def _(vector):
                    vector.memset(self.ones_col[:], 1.0)
                    vector.memset(self.ones2[:], 1.0)
                    vector.memset(self.ones_row[:], 1.0)
                    vector.memset(self.ones_r1[:], 1.0)
                    vector.wait_ge(s["ld"], n[0])

    def _emit_ln(self, tag, src_tiles, dst_tiles, tap_spill):
        nc = self.nc
        with contextlib.ExitStack() as st:
            s = self.sems(st, ["sq", "mm", "ev", "bc", "sqr", "sto"])
            with contextlib.ExitStack() as st_stats:
                ps_sum = self.ps(st_stats, "ps_sum", [1, T])
                ps_sumsq = self.ps(st_stats, "ps_sumsq", [1, T])
                xsq = [self.sb(st, "xsq", [128, T]) for _ in range(2)]
                mu = self.sb(st, "mu", [1, T])
                rstd = self.sb(st, "rstd", [1, T])
                tmp1 = self.sb(st, "tmp1", [1, T])
                tmp2 = self.sb(st, "tmp2", [1, T])

                with nc.Block() as blk:
                    @blk.scalar
                    def _(scalar):
                        for i in range(NI):
                            if i >= 2:
                                scalar.wait_ge(s["mm"], i - 1)
                            scalar.activation(xsq[i % 2][:], src_tiles[i][:],
                                              AF.Square).then_inc(s["sq"], 1)
                        scalar.wait_ge(s["ev"], 1)
                        scalar.activation(tmp2[:], tmp1[:], AF.Sqrt).then_inc(s["sqr"], 1)

                    @blk.tensor
                    def _(tensor):
                        for i in range(NI):
                            for q in range(4):
                                sl = slice(512 * q, 512 * (q + 1))
                                tensor.matmul(ps_sum.ap()[:, sl], self.ones_col[:],
                                              src_tiles[i][:, sl],
                                              start=(i == 0), stop=(i == NI - 1))
                            tensor.wait_ge(s["sq"], i + 1)
                            for q in range(4):
                                sl = slice(512 * q, 512 * (q + 1))
                                mmi = tensor.matmul(ps_sumsq.ap()[:, sl], self.ones_col[:],
                                                    xsq[i % 2][:, sl],
                                                    start=(i == 0), stop=(i == NI - 1))
                            mmi.then_inc(s["mm"], 1)

                    @blk.vector
                    def _(vector):
                        vector.wait_ge(s["mm"], NI)
                        vector.tensor_scalar(mu[:], ps_sum.ap()[:, :], 1.0 / D, None, OP.mult)
                        vector.tensor_scalar(tmp1[:], ps_sumsq.ap()[:, :], 1.0 / D, None, OP.mult)
                        vector.tensor_mul(tmp2[:], mu[:], mu[:])
                        vector.tensor_sub(tmp1[:], tmp1[:], tmp2[:])
                        vector.tensor_scalar(tmp1[:], tmp1[:], EPS_LN, None,
                                             OP.add).then_inc(s["ev"], 1)
                        vector.wait_ge(s["sqr"], 1)
                        vector.reciprocal(rstd[:], tmp2[:]).then_inc(s["ev"], 1)

            ps_mu = self.ps(st, "ps_mu", [128, T])
            ps_rs = self.ps(st, "ps_rs", [128, T])
            with nc.Block() as blk:
                @blk.tensor
                def _(tensor):
                    for q in range(4):
                        sl = slice(512 * q, 512 * (q + 1))
                        tensor.matmul(ps_mu.ap()[:, sl], self.ones_r1[:],
                                      mu[:, sl], start=True, stop=True)
                        mmi = tensor.matmul(ps_rs.ap()[:, sl], self.ones_r1[:],
                                            rstd[:, sl], start=True, stop=True)
                    mmi.then_inc(s["bc"], 1)

                @blk.vector
                def _(vector):
                    vector.wait_ge(s["bc"], 1)
                    for i in range(NI):
                        for q in range(4):
                            sl = slice(512 * q, 512 * (q + 1))
                            vector.tensor_sub(dst_tiles[i][:, sl], src_tiles[i][:, sl],
                                              ps_mu.ap()[:, sl])
                            vector.tensor_mul(dst_tiles[i][:, sl], dst_tiles[i][:, sl],
                                              ps_rs.ap()[:, sl]).then_inc(s["ev"], 1)

                @blk.gpsimd
                def _(gpsimd):
                    if tap_spill is not None:
                        for i in range(NI):
                            gpsimd.wait_ge(s["ev"], 2 + 4 * (i + 1))
                            gpsimd.dma_start(out=tap_spill.ap()[128 * i:128 * (i + 1), :],
                                             in_=dst_tiles[i][:]).then_inc(s["sto"], 16)

    def p_ln(self, tag, src, dst, tap_spill=None):
        self._emit_ln(tag, src, dst, tap_spill)

    def p_ln_reload(self, tag, spill, src_tiles, dst_tiles, tap_spill=None):
        nc = self.nc
        with contextlib.ExitStack() as st:
            s = self.sems(st, ["ld"])
            with nc.Block() as blk:
                @blk.sync
                def _(sync):
                    for i in range(NI):
                        sync.dma_start(out=src_tiles[i][:],
                                       in_=spill.ap()[128 * i:128 * (i + 1), :]).then_inc(s["ld"], 16)

                @blk.vector
                def _(vector):
                    vector.wait_ge(s["ld"], 16 * NI)
        self.clear_block()
        self._emit_ln(tag, src_tiles, dst_tiles, tap_spill)

    def p_wtbuild(self):
        nc = self.nc
        self.wt_stack = contextlib.ExitStack()
        self.substacks.append(self.wt_stack)
        big = self.wt_stack
        self.wrT = [self.sb(big, "wrT", [128, FF]) for _ in range(4)]
        self.wiT = [self.sb(big, "wiT", [128, FF]) for _ in range(4)]
        self.wiTn = [self.sb(big, "wiTn", [128, FF]) for _ in range(4)]
        with contextlib.ExitStack() as st:
            s = self.sems(st, ["ld", "used"])
            sbuf = [self.sb(st, "stbuf", [128, FF]) for _ in range(4)]
            tmp = self.sb(st, "wtmp", [128, FF])
            jobs = []
            jobs.append((self.l0_SR, self.l0_TR, self.wrT))
            jobs.append((self.l0_SI, self.l0_TI, self.wiT))
            seq = []
            for Sd, Td, dest in jobs:
                for r in range(4):
                    for c in range(4):
                        seq.append((Sd, Td, dest, r, c))
            with nc.Block() as blk:
                @blk.sync
                def _(sync):
                    for j, (Sd, Td, dest, r, c) in enumerate(seq):
                        if j >= 2:
                            sync.wait_ge(s["used"], j - 1)
                        row = 512 * c + 128 * r
                        sync.dma_start(out=sbuf[2 * (j % 2)][:],
                                       in_=Sd.ap()[row:row + 128, :]).then_inc(s["ld"], 16)
                        sync.dma_start(out=sbuf[2 * (j % 2) + 1][:],
                                       in_=Td.ap()[row:row + 128, :]).then_inc(s["ld"], 16)

                @blk.vector
                def _(vector):
                    for j, (Sd, Td, dest, r, c) in enumerate(seq):
                        vector.wait_ge(s["ld"], 32 * (j + 1))
                        a, b = sbuf[2 * (j % 2)], sbuf[2 * (j % 2) + 1]
                        if c == 0:
                            vector.tensor_mul(dest[r][:], a[:], b[:]).then_inc(s["used"], 1)
                        else:
                            vector.tensor_mul(tmp[:], a[:], b[:]).then_inc(s["used"], 1)
                            vector.tensor_add(dest[r][:], dest[r][:], tmp[:])
                    for r in range(4):
                        vector.tensor_scalar(self.wiTn[r][:], self.wiT[r][:],
                                             -1.0, None, OP.mult)

    def p_l0_qkv(self):
        nc = self.nc
        NOB = 6
        dests = [(self.q_tok, 0), (self.q_tok, 512), (self.k_tok, 0),
                 (self.k_tok, 512), (self.v_tok, 0), (self.v_tok, 512)]
        with contextlib.ExitStack() as st:
            s = self.sems(st, ["wld", "wfree", "mm", "ev", "sto", "bq", "bev"])
            wbuf = [self.sb(st, "wbuf", [128, 8 * 512]) for _ in range(2)]
            obuf = [self.sb(st, "obuf", [128, 512]) for _ in range(2)]
            qkvb = self.sb(st, "qkvb", [128, 3 * D])
            bqrow = self.sb(st, "bqrow", [1, 3 * D])
            psm = [self.ps(st, "psqkv", [128, 512]) for _ in range(2)]
            psb = self.ps(st, "psbias", [128, 512])

            with nc.Block() as blk:
                @blk.sync
                def _(sync):
                    sync.dma_start(out=bqrow[:], in_=self.l0_bqkv.ap()[:, :]).then_inc(s["bq"], 16)
                    for ob in range(NOB):
                        if ob >= 2:
                            sync.wait_ge(s["ev"], NT * (ob - 1))
                        for i in range(NI):
                            sync.dma_start(
                                out=wbuf[ob % 2][:, 512 * i:512 * (i + 1)],
                                in_=self.l0_wqkv.ap()[128 * i:128 * (i + 1),
                                                      512 * ob:512 * (ob + 1)]).then_inc(s["wld"], 16)

                @blk.tensor
                def _(tensor):
                    tensor.wait_ge(s["bq"], 16)
                    for b6 in range(6):
                        if b6 >= 1:
                            tensor.wait_ge(s["bev"], b6)
                        tensor.matmul(psb.ap()[:, :], self.ones_r1[:],
                                      bqrow[:, 512 * b6:512 * (b6 + 1)],
                                      start=True, stop=True).then_inc(s["mm"], 1)
                    for ob in range(NOB):
                        tensor.wait_ge(s["wld"], 16 * 8 * (ob + 1))
                        for t in range(NT):
                            idx = ob * NT + t
                            if idx >= 2:
                                tensor.wait_ge(s["ev"], idx - 1)
                            for i in range(NI):
                                mmi = tensor.matmul(
                                    psm[idx % 2].ap()[:, :],
                                    self.xnt[i][:, 128 * t:128 * (t + 1)],
                                    wbuf[ob % 2][:, 512 * i:512 * (i + 1)],
                                    start=(i == 0), stop=(i == NI - 1))
                            mmi.then_inc(s["mm"], 1)

                @blk.vector
                def _(vector):
                    for b6 in range(6):
                        vector.wait_ge(s["mm"], b6 + 1)
                        vector.tensor_copy(qkvb[:, 512 * b6:512 * (b6 + 1)],
                                           psb.ap()[:, :]).then_inc(s["bev"], 1)
                    for ob in range(NOB):
                        for t in range(NT):
                            idx = ob * NT + t
                            vector.wait_ge(s["mm"], 6 + idx + 1)
                            if idx >= 2:
                                vector.wait_ge(s["sto"], 16 * (idx - 1))
                            vector.tensor_tensor(
                                obuf[idx % 2][:], psm[idx % 2].ap()[:, :],
                                qkvb[:, 512 * ob:512 * (ob + 1)],
                                OP.add).then_inc(s["ev"], 1)

                @blk.gpsimd
                def _(gpsimd):
                    for ob in range(NOB):
                        dstd, coff = dests[ob]
                        for t in range(NT):
                            idx = ob * NT + t
                            gpsimd.wait_ge(s["ev"], idx + 1)
                            gpsimd.dma_start(
                                out=bass.AP(dstd, 128 * t * D + coff,
                                            [[D, 128], [1, 512]]),
                                in_=obuf[idx % 2][:]).then_inc(s["sto"], 16)

    def p_l0_kv(self):
        nc = self.nc
        self.kv_stack = contextlib.ExitStack()
        self.substacks.append(self.kv_stack)
        big = self.kv_stack
        self.srecip = self.sb(big, "srecip", [1, H])
        with contextlib.ExitStack() as st:
            s = self.sems(st, ["ld", "sq", "vr", "sr", "kh", "mm", "ev", "sg", "rc",
                               "sto"])
            kt = [self.sb(st, "ktile", [128, D]) for _ in range(2)]
            vt = [self.sb(st, "vtile", [128, D]) for _ in range(2)]
            ksq = [self.sb(st, "ksq", [128, D]) for _ in range(2)]
            khat = [self.sb(st, "khat", [128, D]) for _ in range(2)]
            ssq = [self.sb(st, "ssq", [128, H]) for _ in range(2)]
            nrm = [self.sb(st, "nrm", [128, H]) for _ in range(2)]
            kvpart = self.sb(st, "kvpart", [64, D])
            sgr = self.sb(st, "sgr", [1, H])
            sclr = self.sb(st, "sclr", [1, H])
            pskv = self.ps(st, "pskv", [64, D])

            with nc.Block() as blk:
                @blk.sync
                def _(sync):
                    for t in range(NT):
                        if t >= 2:
                            sync.wait_ge(s["mm"], t - 1)
                        sync.dma_start(out=kt[t % 2][:],
                                       in_=self.k_tok.ap()[128 * t:128 * (t + 1), :]).then_inc(s["ld"], 16)
                        sync.dma_start(out=vt[t % 2][:],
                                       in_=self.v_tok.ap()[128 * t:128 * (t + 1), :]).then_inc(s["ld"], 16)

                @blk.scalar
                def _(scalar):
                    scalar.activation(sgr[:], self.m_row[:], AF.Sigmoid)
                    scalar.activation(sclr[:], sgr[:], AF.Exp, scale=LOG_N).then_inc(s["sg"], 1)
                    for t in range(NT):
                        scalar.wait_ge(s["ld"], 32 * (t + 1))
                        if t >= 2:
                            scalar.wait_ge(s["vr"], t - 1)
                        scalar.activation(ksq[t % 2][:], kt[t % 2][:],
                                          AF.Square).then_inc(s["sq"], 1)
                        scalar.wait_ge(s["vr"], t + 1)
                        scalar.activation(nrm[t % 2][:], ssq[t % 2][:],
                                          AF.Sqrt).then_inc(s["sr"], 1)

                @blk.vector
                def _(vector):
                    vector.wait_ge(s["sg"], 1)
                    vector.reciprocal(self.srecip[:], sclr[:])
                    for t in range(NT):
                        vector.wait_ge(s["sq"], t + 1)
                        v3 = bass.AP(ksq[t % 2], 0, [[D, 128], [DH, H], [1, DH]])
                        vector.tensor_reduce(ssq[t % 2][:], v3, AX.X,
                                             OP.add).then_inc(s["vr"], 1)
                        vector.wait_ge(s["sr"], t + 1)
                        vector.tensor_scalar_max(nrm[t % 2][:], nrm[t % 2][:], EPS_L2)
                        vector.reciprocal(nrm[t % 2][:], nrm[t % 2][:]).then_inc(s["rc"], 1)
                        vector.wait_ge(s["rc"], t + 1)
                        if t >= 2:
                            vector.wait_ge(s["mm"], t - 1)
                        for h in range(H):
                            vi = vector.tensor_scalar(
                                khat[t % 2][:, DH * h:DH * (h + 1)],
                                kt[t % 2][:, DH * h:DH * (h + 1)],
                                nrm[t % 2][:, h:h + 1], None, OP.mult)
                        vi.then_inc(s["kh"], 1)
                    vector.wait_ge(s["mm"], NT)
                    vector.tensor_copy(kvpart[:], pskv.ap()[:, :]).then_inc(s["ev"], 1)

                @blk.tensor
                def _(tensor):
                    for t in range(NT):
                        tensor.wait_ge(s["kh"], t + 1)
                        for h in range(H):
                            mmi = tensor.matmul(
                                pskv.ap()[:, DH * h:DH * (h + 1)],
                                khat[t % 2][:, DH * h:DH * (h + 1)],
                                vt[t % 2][:, DH * h:DH * (h + 1)],
                                start=(t == 0 and h % 8 == 0),
                                stop=(t == NT - 1 and h % 8 == 7),
                                skip_group_check=True)
                        mmi.then_inc(s["mm"], 1)

                @blk.gpsimd
                def _(gpsimd):
                    gpsimd.wait_ge(s["ev"], 1)
                    gpsimd.dma_start(out=self.cc_in.ap()[:, :],
                                     in_=kvpart[:]).then_inc(s["sto"], 16)
                    gpsimd.wait_ge(s["sto"], 16)

    def p_l0_cc_qhat(self):
        nc = self.nc
        big = self.kv_stack
        self.kvbar = self.sb(big, "kvbar", [64, D])
        with contextlib.ExitStack() as st:
            s = self.sems(st, ["cc", "ccl", "bc", "bev", "ld", "sq", "vr", "sr",
                              "qh", "tp", "ev", "rc", "sto"])
            qt = [self.sb(st, "qtile", [128, D]) for _ in range(2)]
            qsq = [self.sb(st, "qsq", [128, D]) for _ in range(2)]
            qhat = [self.sb(st, "qhat", [128, D]) for _ in range(2)]
            ssq = [self.sb(st, "ssq2", [128, H]) for _ in range(2)]
            nrm = [self.sb(st, "nrm2", [128, H]) for _ in range(2)]
            qtr = [self.sb(st, "qtr", [64, T]) for _ in range(2)]
            kvld = self.sb(st, "kvld", [64, D])
            srp = self.sb(st, "srp", [64, H])
            pssr = self.ps(st, "pssr", [64, 512])
            pstr = [self.ps(st, "pstr", [64, 512]) for _ in range(4)]

            with nc.Block() as blk:
                @blk.gpsimd
                def _(gpsimd):
                    gpsimd.collective_compute(
                        "AllReduce", OP.add,
                        ins=[self.cc_in.ap()[:, :]],
                        outs=[self.cc_out.ap()[:, :]],
                        replica_groups=[[0, 1, 2, 3], [4, 5, 6, 7]],
                    ).then_inc(s["cc"], 1)
                    gpsimd.wait_ge(s["cc"], 1)
                    gpsimd.dma_start(out=kvld[:], in_=self.cc_out.ap()[:, :]).then_inc(s["ccl"], 16)
                    for t in range(NT):
                        gpsimd.wait_ge(s["ev"], (t + 1) * H)
                        gpsimd.dma_start(
                            out=bass.AP(self.qhat_f, 128 * t,
                                        [[T, 64], [64 * T, H], [1, 128]]),
                            in_=bass.AP(qtr[t % 2], 0, [[T, 64], [128, H], [1, 128]]),
                        ).then_inc(s["sto"], 16)

                @blk.tensor
                def _(tensor):
                    tensor.matmul(pssr.ap()[:, 0:H], self.ones_r1[:, 0:64],
                                  self.srecip[:], start=True, stop=True).then_inc(s["bc"], 1)
                    for t in range(NT):
                        tensor.wait_ge(s["qh"], t + 1)
                        for h in range(H):
                            j = (t * H + h) % 4
                            if t * H + h >= 4:
                                tensor.wait_ge(s["ev"], t * H + h - 3)
                            tensor.transpose(pstr[j].ap()[:, 0:128],
                                             qhat[t % 2][:, DH * h:DH * (h + 1)],
                                             self.ident[:]).then_inc(s["tp"], 1)

                @blk.sync
                def _(sync):
                    for t in range(NT):
                        if t >= 2:
                            sync.wait_ge(s["tp"], (t - 1) * H)
                        sync.dma_start(out=qt[t % 2][:],
                                       in_=self.q_tok.ap()[128 * t:128 * (t + 1), :]).then_inc(s["ld"], 16)

                @blk.scalar
                def _(scalar):
                    for t in range(NT):
                        scalar.wait_ge(s["ld"], 16 * (t + 1))
                        if t >= 2:
                            scalar.wait_ge(s["vr"], t - 1)
                        scalar.activation(qsq[t % 2][:], qt[t % 2][:],
                                          AF.Square).then_inc(s["sq"], 1)
                        scalar.wait_ge(s["vr"], t + 1)
                        scalar.activation(nrm[t % 2][:], ssq[t % 2][:],
                                          AF.Sqrt).then_inc(s["sr"], 1)

                @blk.vector
                def _(vector):
                    vector.wait_ge(s["bc"], 1)
                    vector.tensor_copy(srp[:], pssr.ap()[:, 0:H]).then_inc(s["bev"], 1)
                    vector.wait_ge(s["bev"], 1)
                    vector.wait_ge(s["ccl"], 16)
                    for h in range(H):
                        vector.tensor_scalar(self.kvbar[:, DH * h:DH * (h + 1)],
                                             kvld[:, DH * h:DH * (h + 1)],
                                             srp[:, h:h + 1], None, OP.mult)
                    for t in range(NT):
                        vector.wait_ge(s["sq"], t + 1)
                        v3 = bass.AP(qsq[t % 2], 0, [[D, 128], [DH, H], [1, DH]])
                        vector.tensor_reduce(ssq[t % 2][:], v3, AX.X,
                                             OP.add).then_inc(s["vr"], 1)
                        vector.wait_ge(s["sr"], t + 1)
                        vector.tensor_scalar_max(nrm[t % 2][:], nrm[t % 2][:], EPS_L2)
                        vector.reciprocal(nrm[t % 2][:], nrm[t % 2][:]).then_inc(s["rc"], 1)
                        vector.wait_ge(s["rc"], t + 1)
                        if t >= 2:
                            vector.wait_ge(s["tp"], (t - 1) * H)
                        for h in range(H):
                            vi = vector.tensor_scalar(
                                qhat[t % 2][:, DH * h:DH * (h + 1)],
                                qt[t % 2][:, DH * h:DH * (h + 1)],
                                nrm[t % 2][:, h:h + 1], None, OP.mult)
                        vi.then_inc(s["qh"], 1)
                        if t >= 2:
                            vector.wait_ge(s["sto"], 16 * (t - 1))
                        for h in range(H):
                            vector.wait_ge(s["tp"], t * H + h + 1)
                            vector.tensor_copy(qtr[t % 2][:, 128 * h:128 * (h + 1)],
                                               pstr[(t * H + h) % 4].ap()[:, 0:128]).then_inc(s["ev"], 1)

    def p_l0_s0(self):
        nc = self.nc
        with contextlib.ExitStack() as st:
            s = self.sems(st, ["ld", "mm", "ev", "sto"])
            qhf = [self.sb(st, "qhf", [64, T]) for _ in range(2)]
            s0b = [self.sb(st, "s0b", [64, T]) for _ in range(2)]
            psb = [self.ps(st, "pss0", [64, 512]) for _ in range(2)]
            with nc.Block() as blk:
                @blk.sync
                def _(sync):
                    for h in range(H):
                        if h >= 2:
                            sync.wait_ge(s["mm"], 4 * (h - 1))
                        sync.dma_start(out=qhf[h % 2][:],
                                       in_=self.qhat_f.ap()[64 * h:64 * (h + 1), :]).then_inc(s["ld"], 16)

                @blk.tensor
                def _(tensor):
                    for h in range(H):
                        tensor.wait_ge(s["ld"], 16 * (h + 1))
                        for q in range(4):
                            idx = h * 4 + q
                            if idx >= 2:
                                tensor.wait_ge(s["ev"], idx - 1)
                            tensor.matmul(psb[idx % 2].ap()[:, :],
                                          self.kvbar[:, DH * h:DH * (h + 1)],
                                          qhf[h % 2][:, 512 * q:512 * (q + 1)],
                                          start=True, stop=True).then_inc(s["mm"], 1)

                @blk.vector
                def _(vector):
                    for h in range(H):
                        if h >= 2:
                            vector.wait_ge(s["sto"], 16 * (h - 1))
                        for q in range(4):
                            idx = h * 4 + q
                            vector.wait_ge(s["mm"], idx + 1)
                            vector.tensor_copy(s0b[h % 2][:, 512 * q:512 * (q + 1)],
                                               psb[idx % 2].ap()[:, :]).then_inc(s["ev"], 1)

                @blk.gpsimd
                def _(gpsimd):
                    for h in range(H):
                        gpsimd.wait_ge(s["ev"], 4 * (h + 1))
                        gpsimd.dma_start(out=self.s0_f.ap()[64 * h:64 * (h + 1), :],
                                         in_=s0b[h % 2][:]).then_inc(s["sto"], 16)

    def p_l0_oproj_ffn(self):
        nc = self.nc
        s0t = self.xt
        with contextlib.ExitStack() as st:
            s = self.sems(st, ["l0", "wld", "wfree", "mmS", "evS", "mmG", "gv",
                              "ldx", "ev2", "sto", "stoS"])
            wo_buf = [self.sb(st, "wo_buf", [128, D]) for _ in range(2)]
            schunk = self.sb(st, "schunk", [128, T])
            gchunk = self.sb(st, "gchunk", [128, T])
            xold = [self.sb(st, "xold", [128, T]) for _ in range(2)]
            x2b = [self.sb(st, "x2b", [128, 512]) for _ in range(2)]
            psS = [self.ps(st, "psS", [128, 512]) for _ in range(2)]
            psG = [self.ps(st, "psG", [128, 512]) for _ in range(2)]

            with nc.Block() as blk:
                @blk.sync
                def _(sync):
                    for i in range(NI):
                        sync.dma_start(out=s0t[i][:],
                                       in_=self.s0_f.ap()[128 * i:128 * (i + 1), :]).then_inc(s["l0"], 16)
                    for oc in range(NI):
                        if oc >= 2:
                            sync.wait_ge(s["evS"], 4 * (oc - 1))
                        for i in range(NI):
                            sync.dma_start(
                                out=wo_buf[oc % 2][:, 128 * i:128 * (i + 1)],
                                in_=self.l0_wo.ap()[128 * i:128 * (i + 1),
                                                    128 * oc:128 * (oc + 1)]).then_inc(s["wld"], 16)
                        if oc >= 2:
                            sync.wait_ge(s["ev2"], 4 * (oc - 1))
                        sync.dma_start(out=xold[oc % 2][:],
                                       in_=self.xspill.ap()[128 * oc:128 * (oc + 1), :]).then_inc(s["ldx"], 16)

                @blk.tensor
                def _(tensor):
                    tensor.wait_ge(s["l0"], 16 * NI)
                    for oc in range(NI):
                        tensor.wait_ge(s["wld"], 16 * 8 * (oc + 1))
                        for q in range(4):
                            idx = oc * 4 + q
                            if idx >= 2:
                                tensor.wait_ge(s["evS"], idx - 1)
                            for i in range(NI):
                                mmi = tensor.matmul(
                                    psS[idx % 2].ap()[:, :],
                                    wo_buf[oc % 2][:, 128 * i:128 * (i + 1)],
                                    s0t[i][:, 512 * q:512 * (q + 1)],
                                    start=(i == 0), stop=(i == NI - 1))
                            mmi.then_inc(s["mmS"], 1)
                            if idx >= 2:
                                tensor.wait_ge(s["gv"], idx - 1)
                            oc2 = oc % 4
                            if oc < 4:
                                pairs = [(self.wrT[ic], self.xnt[ic]) for ic in range(4)] + \
                                        [(self.wiTn[ic], self.xnt[4 + ic]) for ic in range(4)]
                            else:
                                pairs = [(self.wiT[ic], self.xnt[ic]) for ic in range(4)] + \
                                        [(self.wrT[ic], self.xnt[4 + ic]) for ic in range(4)]
                            for jj, (wt, xn) in enumerate(pairs):
                                mmi = tensor.matmul(
                                    psG[idx % 2].ap()[:, :],
                                    wt[:, 128 * oc2:128 * (oc2 + 1)],
                                    xn[:, 512 * q:512 * (q + 1)],
                                    start=(jj == 0), stop=(jj == 7))
                            mmi.then_inc(s["mmG"], 1)

                @blk.scalar
                def _(scalar):
                    for oc in range(NI):
                        for q in range(4):
                            idx = oc * 4 + q
                            scalar.wait_ge(s["mmG"], idx + 1)
                            if idx >= 1:
                                scalar.wait_ge(s["ev2"], idx)
                            scalar.activation(gchunk[:, 512 * q:512 * (q + 1)],
                                              psG[idx % 2].ap()[:, :], AF.Silu,
                                              bias=self.brbi_cols[:, oc:oc + 1]).then_inc(s["gv"], 1)

                @blk.vector
                def _(vector):
                    for oc in range(NI):
                        vector.wait_ge(s["ldx"], 16 * (oc + 1))
                        if oc >= 1:
                            vector.wait_ge(s["stoS"], 16 * oc)
                        for q in range(4):
                            idx = oc * 4 + q
                            sl = slice(512 * q, 512 * (q + 1))
                            vector.wait_ge(s["mmS"], idx + 1)
                            vector.tensor_scalar(schunk[:, sl], psS[idx % 2].ap()[:, :],
                                                 self.bo_cols[:, oc:oc + 1], None,
                                                 OP.add).then_inc(s["evS"], 1)
                            vector.wait_ge(s["gv"], idx + 1)
                            if idx >= 2:
                                vector.wait_ge(s["sto"], 16 * (idx - 1))
                            vector.tensor_mul(x2b[idx % 2][:],
                                              schunk[:, sl], gchunk[:, sl])
                            vector.tensor_add(x2b[idx % 2][:],
                                              x2b[idx % 2][:],
                                              xold[oc % 2][:, sl]).then_inc(s["ev2"], 1)

                @blk.gpsimd
                def _(gpsimd):
                    for oc in range(NI):
                        for q in range(4):
                            idx = oc * 4 + q
                            gpsimd.wait_ge(s["ev2"], idx + 1)
                            gpsimd.dma_start(
                                out=self.x2spill.ap()[128 * oc:128 * (oc + 1),
                                                      512 * q:512 * (q + 1)],
                                in_=x2b[idx % 2][:]).then_inc(s["sto"], 16)
                        gpsimd.wait_ge(s["evS"], 4 * (oc + 1))
                        gpsimd.dma_start(out=self.s_f.ap()[128 * oc:128 * (oc + 1), :],
                                         in_=schunk[:]).then_inc(s["stoS"], 16)

    def p_l1_qk(self):
        nc = self.nc
        self.l1a_stack = contextlib.ExitStack()
        self.substacks.append(self.l1a_stack)
        big = self.l1a_stack
        self.normq = self.sb(big, "normq", [128, 256])
        self.normk = self.sb(big, "normk", [128, 256])
        with contextlib.ExitStack() as st:
            s = self.sems(st, ["wld", "wfree", "mm", "sq", "nm", "ev", "sto"])
            wbuf = [self.sb(st, "wbuf1", [128, D]) for _ in range(4)]
            qkb = [self.sb(st, "qkb", [128, 512]) for _ in range(2)]
            nbounce = self.sb(st, "nbounce", [128, 512])
            sqb = [self.sb(st, "sqb", [128, 512]) for _ in range(2)]
            psm = [self.ps(st, "psqk", [128, 512]) for _ in range(2)]
            psnq = self.ps(st, "psnq", [128, 512])
            psnk = self.ps(st, "psnk", [128, 512])

            with nc.Block() as blk:
                @blk.sync
                def _(sync):
                    for oc in range(16):
                        if oc >= 4:
                            sync.wait_ge(s["ev"], 4 * (oc - 3))
                        for i in range(NI):
                            sync.dma_start(
                                out=wbuf[oc % 4][:, 128 * i:128 * (i + 1)],
                                in_=self.l1_wqkvT.ap()[128 * i:128 * (i + 1),
                                                       128 * oc:128 * (oc + 1)]).then_inc(s["wld"], 16)

                @blk.tensor
                def _(tensor):
                    for oc in range(16):
                        tensor.wait_ge(s["wld"], 16 * 8 * (oc + 1))
                        for q in range(4):
                            idx = oc * 4 + q
                            if idx >= 2:
                                tensor.wait_ge(s["ev"], idx - 1)
                                tensor.wait_ge(s["nm"], idx - 1)
                            for i in range(NI):
                                mmi = tensor.matmul(
                                    psm[idx % 2].ap()[:, :],
                                    wbuf[oc % 4][:, 128 * i:128 * (i + 1)],
                                    self.xnt[i][:, 512 * q:512 * (q + 1)],
                                    start=(i == 0), stop=(i == NI - 1))
                            mmi.then_inc(s["mm"], 1)
                            if QK_MODE == "nonorm":
                                tensor.nop().then_inc(s["nm"], 1)
                                continue
                            tensor.wait_ge(s["sq"], idx + 1)
                            pn = psnq if oc < 8 else psnk
                            for half in range(2):
                                hcol = (2 * (oc % 8) + half) * 16
                                for m in range(4):
                                    col = hcol + q * 4 + m
                                    mmi = tensor.matmul(
                                        pn.ap()[:, 2 * col:2 * col + 2],
                                        sqb[idx % 2][64 * half:64 * half + 64,
                                                     128 * m:128 * (m + 1)],
                                        self.ones2[64 * half:64 * half + 64, :],
                                        start=True, stop=True)
                            mmi.then_inc(s["nm"], 1)

                @blk.scalar
                def _(scalar):
                    if QK_MODE == "nonorm":
                        pass
                    else:
                        for oc in range(16):
                            for q in range(4):
                                idx = oc * 4 + q
                                scalar.wait_ge(s["mm"], idx + 1)
                                if idx >= 2:
                                    scalar.wait_ge(s["nm"], idx - 1)
                                    scalar.wait_ge(s["ev"], idx - 1)
                                scalar.activation(sqb[idx % 2][:], psm[idx % 2].ap()[:, :],
                                                  AF.Square).then_inc(s["sq"], 1)

                @blk.vector
                def _(vector):
                    for oc in range(16):
                        for q in range(4):
                            idx = oc * 4 + q
                            vector.wait_ge(s["sq"], idx + 1)
                            if idx >= 2:
                                vector.wait_ge(s["sto"], 16 * (idx - 1))
                            vector.tensor_copy(qkb[idx % 2][:],
                                               psm[idx % 2].ap()[:, :]).then_inc(s["ev"], 1)
                    vector.wait_ge(s["nm"], 64)
                    vector.tensor_copy(nbounce[:], psnq.ap()[:, 0:512])
                    vector.tensor_copy(self.normq[:],
                                       bass.AP(nbounce, 0, [[512, 128], [2, 256]]))
                    vector.tensor_copy(nbounce[:], psnk.ap()[:, 0:512])
                    vector.tensor_copy(self.normk[:],
                                       bass.AP(nbounce, 0, [[512, 128], [2, 256]]))

                @blk.gpsimd
                def _(gpsimd):
                    for oc in range(16):
                        dstd = self.qf_sp if oc < 8 else self.kf_sp
                        for q in range(4):
                            idx = oc * 4 + q
                            gpsimd.wait_ge(s["ev"], idx + 1)
                            gpsimd.dma_start(
                                out=dstd.ap()[128 * (oc % 8):128 * (oc % 8 + 1),
                                              512 * q:512 * (q + 1)],
                                in_=qkb[idx % 2][:]).then_inc(s["sto"], 16)

    def p_l1_lam_bound(self):
        nc = self.nc
        big = self.l1a_stack
        self.lam8t = [self.sb(big, "lam8t", [128, H]) for _ in range(NT)]
        self.negB = self.sb(big, "negB", [128, 32])
        with contextlib.ExitStack() as st:
            s = self.sems(st, ["io", "ln", "mm1", "rl", "mm2", "sgm", "l8",
                              "tp", "ev", "nb", "ld"])
            posrow = self.sb(st, "posrow", [1, T])
            logrow = self.sb(st, "logrow", [1, T])
            lamrhs = self.sb(st, "lamrhs", [2, T])
            h1 = self.sb(st, "h1", [64, T])
            lamf = self.sb(st, "lamf", [H, T])
            lam8f = self.sb(st, "lam8f", [H, T])
            lam1t = self.sb(st, "lam1t", [2, 64])
            lam2Tt = self.sb(st, "lam2Tt", [64, H])
            lam2bt = self.sb(st, "lam2bt", [1, H])
            mqcol = self.sb(st, "mqcol", [128, 2])
            mkcol = self.sb(st, "mkcol", [128, 2])
            mq2 = self.sb(st, "mq2", [1, 32])
            mk2 = self.sb(st, "mk2", [1, 32])
            brow = self.sb(st, "brow", [1, 32])
            ps1 = [self.ps(st, "pslam", [64, 512]) for _ in range(2)]
            ps2 = [self.ps(st, "pslam2", [H, 512]) for _ in range(2)]
            pst = [self.ps(st, "pslt", [128, 512]) for _ in range(2)]
            psr = self.ps(st, "pslr", [1, 128])
            psnb = self.ps(st, "psnb", [128, 32])

            with nc.Block() as blk:
                @blk.sync
                def _(sync):
                    sync.dma_start(out=lam1t[:], in_=self.l1_lam1.ap()[:, :]).then_inc(s["ld"], 16)
                    sync.dma_start(out=lam2Tt[:], in_=self.l1_lam2T.ap()[:, :]).then_inc(s["ld"], 16)
                    sync.dma_start(out=lam2bt[:], in_=self.l1_lam2b.ap()[:, :]).then_inc(s["ld"], 16)

                @blk.gpsimd
                def _(gpsimd):
                    gpsimd.iota(posrow[:], [[1, T]], channel_multiplier=0,
                                allow_small_or_imprecise_dtypes=True).then_inc(s["io"], 1)

                @blk.vector
                def _(vector):
                    vector.wait_ge(s["io"], 1)
                    vector.tensor_scalar(posrow[:], posrow[:], self.posb[:], 1.0,
                                         OP.add, OP.add).then_inc(s["io"], 1)
                    vector.wait_ge(s["ln"], 1)
                    vector.memset(lamrhs[:, :], 1.0)
                    vector.tensor_copy(lamrhs[0:1, :], logrow[:]).then_inc(s["io"], 1)
                    vector.wait_ge(s["sgm"], 4)
                    vector.tensor_scalar(lam8f[:], lamf[:], QSCALE, None,
                                         OP.mult).then_inc(s["l8"], 1)
                    for t in range(NT):
                        vector.wait_ge(s["tp"], t + 1)
                        vector.tensor_copy(self.lam8t[t][:], pst[t % 2].ap()[:, 0:H]).then_inc(s["ev"], 1)
                    for j, (src, dst) in enumerate([(self.normq, mqcol), (self.normk, mkcol)]):
                        for half in range(2):
                            vector.wait_ge(s["tp"], NT + 2 * j + half + 1)
                            vector.reduce_max(dst[:, half:half + 1],
                                              pst[(2 * j + half) % 2].ap()[:, 0:128],
                                              AX.X).then_inc(s["ev"], 1)
                    for j, (colt, dst) in enumerate([(mqcol, mq2), (mkcol, mk2)]):
                        for half in range(2):
                            vector.wait_ge(s["tp"], NT + 4 + 2 * j + half + 1)
                            v4 = bass.AP(psr, 0, [[128, 1], [16, 8], [8, 2], [1, 8]])
                            vector.reduce_max(dst[:, 16 * half:16 * (half + 1)],
                                              v4, AX.X).then_inc(s["ev"], 1)
                    vector.tensor_mul(brow[:], mq2[:], mk2[:]).then_inc(s["nb"], 1)
                    vector.wait_ge(s["nb"], 2)
                    vector.tensor_scalar(brow[:], brow[:], -QSCALE, None,
                                         OP.mult).then_inc(s["nb"], 1)
                    vector.wait_ge(s["nb"], 4)
                    vector.tensor_copy(self.negB[:], psnb.ap()[:, :])

                @blk.scalar
                def _(scalar):
                    scalar.wait_ge(s["io"], 2)
                    scalar.activation(logrow[:], posrow[:], AF.Ln).then_inc(s["ln"], 1)
                    for q in range(4):
                        scalar.wait_ge(s["mm1"], q + 1)
                        scalar.activation(h1[:, 512 * q:512 * (q + 1)],
                                          ps1[q % 2].ap()[:, :], AF.Relu).then_inc(s["rl"], 1)
                    for q in range(4):
                        scalar.wait_ge(s["mm2"], q + 1)
                        scalar.activation(lamf[:, 512 * q:512 * (q + 1)],
                                          ps2[q % 2].ap()[:, :], AF.Sigmoid).then_inc(s["sgm"], 1)
                    scalar.wait_ge(s["nb"], 1)
                    scalar.activation(brow[:], brow[:], AF.Sqrt).then_inc(s["nb"], 1)

                @blk.tensor
                def _(tensor):
                    tensor.wait_ge(s["io"], 3)
                    tensor.wait_ge(s["ld"], 48)
                    for q in range(4):
                        if q >= 2:
                            tensor.wait_ge(s["rl"], q - 1)
                        tensor.matmul(ps1[q % 2].ap()[:, :], lam1t[:],
                                      lamrhs[:, 512 * q:512 * (q + 1)],
                                      start=True, stop=True).then_inc(s["mm1"], 1)
                    for q in range(4):
                        if q >= 2:
                            tensor.wait_ge(s["sgm"], q - 1)
                        tensor.wait_ge(s["rl"], q + 1)
                        tensor.matmul(ps2[q % 2].ap()[:, :], lam2Tt[:],
                                      h1[:, 512 * q:512 * (q + 1)],
                                      start=True, stop=False)
                        tensor.matmul(ps2[q % 2].ap()[:, :], lam2bt[:],
                                      self.ones_row[:, 0:512],
                                      start=False, stop=True).then_inc(s["mm2"], 1)
                    tensor.wait_ge(s["l8"], 1)
                    for t in range(NT):
                        if t >= 2:
                            tensor.wait_ge(s["ev"], t - 1)
                        tensor.transpose(pst[t % 2].ap()[:, 0:H],
                                         lam8f[:, 128 * t:128 * (t + 1)],
                                         self.ident[0:H, 0:H]).then_inc(s["tp"], 1)
                    for j, src in enumerate([self.normq, self.normk]):
                        for half in range(2):
                            k = 2 * j + half
                            tensor.wait_ge(s["ev"], NT - 1 + k)
                            tensor.transpose(pst[k % 2].ap()[:, 0:128],
                                             src[:, 128 * half:128 * (half + 1)],
                                             self.ident[:]).then_inc(s["tp"], 1)
                    for j, colt in enumerate([mqcol, mkcol]):
                        for half in range(2):
                            k = 2 * j + half
                            tensor.wait_ge(s["ev"], NT + 2 + k)
                            tensor.transpose(psr.ap()[:, :], colt[:, half:half + 1],
                                             self.ident[:]).then_inc(s["tp"], 1)
                            tensor.wait_ge(s["ev"], NT + 4 + k + 1)
                    tensor.wait_ge(s["nb"], 3)
                    tensor.matmul(psnb.ap()[:, :], self.ones_r1[:], brow[:],
                                  start=True, stop=True).then_inc(s["nb"], 1)

    def p_l1_v(self):
        nc = self.nc
        with contextlib.ExitStack() as st:
            s = self.sems(st, ["wld", "mm", "ev", "sto"])
            wv = self.sb(st, "wvslab", [128, 8 * D])
            vb = [self.sb(st, "vb", [128, D]) for _ in range(2)]
            psm = [self.ps(st, "psv", [128, D]) for _ in range(2)]
            with nc.Block() as blk:
                @blk.sync
                def _(sync):
                    for i in range(NI):
                        sync.dma_start(
                            out=wv[:, D * i:D * (i + 1)],
                            in_=self.l1_wqkvT.ap()[128 * i:128 * (i + 1), 2 * D:3 * D]).then_inc(s["wld"], 16)

                @blk.tensor
                def _(tensor):
                    tensor.wait_ge(s["wld"], 16 * NI)
                    for t in range(NT):
                        if t >= 2:
                            tensor.wait_ge(s["ev"], t - 1)
                        for i in range(NI):
                            for half in range(2):
                                mmi = tensor.matmul(
                                    psm[t % 2].ap()[:, 512 * half:512 * (half + 1)],
                                    self.xnt[i][:, 128 * t:128 * (t + 1)],
                                    wv[:, D * i + 512 * half:D * i + 512 * (half + 1)],
                                    start=(i == 0), stop=(i == NI - 1))
                        mmi.then_inc(s["mm"], 1)

                @blk.vector
                def _(vector):
                    for t in range(NT):
                        vector.wait_ge(s["mm"], t + 1)
                        if t >= 2:
                            vector.wait_ge(s["sto"], 16 * (t - 1))
                        vector.tensor_copy(vb[t % 2][:], psm[t % 2].ap()[:, :]).then_inc(s["ev"], 1)

                @blk.gpsimd
                def _(gpsimd):
                    for t in range(NT):
                        gpsimd.wait_ge(s["ev"], t + 1)
                        gpsimd.dma_start(out=self.vtok_sp.ap()[128 * t:128 * (t + 1), :],
                                         in_=vb[t % 2][:]).then_inc(s["sto"], 16)

    def p_l1_attn(self):
        nc = self.nc
        NKT = CS // 128
        with contextlib.ExitStack() as st:
            s = self.sems(st, ["sp", "ld", "vst", "mmS", "ex", "mmP", "evp",
                               "tp", "eva", "rc", "sto"])
            qfc = [self.xt[i][:, 0:CS] for i in range(NI)]
            kfc = [self.xt[i][:, CS:2 * CS] for i in range(NI)]
            vc = [self.xnt[kt] for kt in range(NI)]
            pt = [self.sb(st, "ptile", [128, CS]) for _ in range(NKT)]
            atk = [self.sb(st, "atk", [128, D]) for _ in range(NKT)]
            atf = [self.sb(st, "atf", [128, CS]) for _ in range(2)]
            rec = self.sb(st, "recip", [128, 1])
            psS = [self.ps(st, "psSc", [128, CS]) for _ in range(2)]
            psP = [self.ps(st, "psPV", [128, 512]) for _ in range(2)]
            psT = [self.ps(st, "psTr", [128, 512]) for _ in range(2)]
            NLD = 16 * (2 * NI + NKT)

            with nc.Block() as blk:
                @blk.gpsimd
                def _(gpsimd):
                    for i in range(NI):
                        gpsimd.dma_start(out=self.xn1spill.ap()[128 * i:128 * (i + 1), :],
                                         in_=self.xnt[i][:, :]).then_inc(s["sp"], 16)
                    for ch in range(NCH):
                        for fc in range(NI):
                            gpsimd.wait_ge(s["eva"], ch * NI * NKT + (fc + 1) * NKT)
                            gpsimd.dma_start(
                                out=self.attnf_sp.ap()[128 * fc:128 * (fc + 1),
                                                       CS * ch:CS * (ch + 1)],
                                in_=atf[fc % 2][:]).then_inc(s["sto"], 16)

                @blk.sync
                def _(sync):
                    sync.wait_ge(s["sp"], 16 * NI)
                    for ch in range(NCH):
                        if ch >= 1:
                            sync.wait_ge(s["mmP"], ch * H * NKT)
                        for i in range(NI):
                            sync.dma_start(out=qfc[i],
                                           in_=self.qf_sp.ap()[128 * i:128 * (i + 1),
                                                               CS * ch:CS * (ch + 1)]).then_inc(s["ld"], 16)
                            sync.dma_start(out=kfc[i],
                                           in_=self.kf_sp.ap()[128 * i:128 * (i + 1),
                                                               CS * ch:CS * (ch + 1)]).then_inc(s["ld"], 16)
                        for kt in range(NKT):
                            sync.dma_start(
                                out=bass.AP(vc[kt], 0, [[T, 128], [68, H], [1, DH]]),
                                in_=bass.AP(self.vtok_sp, (CS * ch + 128 * kt) * D,
                                            [[D, 128], [DH, H], [1, DH]])).then_inc(s["ld"], 16)

                @blk.vector
                def _(vector):
                    for ch in range(NCH):
                        vector.wait_ge(s["ld"], NLD * (ch + 1))
                        for kt in range(NKT):
                            vector.memset(bass.AP(vc[kt], DH, [[T, 128], [68, H], [1, 4]]), 0.0)
                            vi = vector.memset(bass.AP(vc[kt], DH, [[T, 128], [68, H]]), 1.0)
                        vi.then_inc(s["vst"], 1)
                        for h in range(H):
                            for qc in range(NKT):
                                idx = (ch * H + h) * NKT + qc
                                vector.wait_ge(s["mmP"], idx + 1)
                                vector.reciprocal(rec[:], psP[idx % 2].ap()[:, 64:65]).then_inc(s["rc"], 1)
                                vector.wait_ge(s["rc"], idx + 1)
                                vector.tensor_scalar(
                                    atk[qc][:, DH * h:DH * (h + 1)],
                                    psP[idx % 2].ap()[:, 0:DH],
                                    rec[:], None, OP.mult).then_inc(s["evp"], 1)
                        for fc in range(NI):
                            if ch * NI + fc >= 2:
                                vector.wait_ge(s["sto"], 16 * (ch * NI + fc - 1))
                            for qc in range(NKT):
                                tidx = (ch * NI + fc) * NKT + qc
                                vector.wait_ge(s["tp"], tidx + 1)
                                vector.tensor_copy(atf[fc % 2][:, 128 * qc:128 * (qc + 1)],
                                                   psT[tidx % 2].ap()[:, 0:128]).then_inc(s["eva"], 1)

                @blk.scalar
                def _(scalar):
                    for ch in range(NCH):
                        for h in range(H):
                            for kt in range(NKT):
                                sidx = (ch * H + h) * NKT + kt
                                scalar.wait_ge(s["mmS"], sidx + 1)
                                gt = ch * NKT + kt
                                scalar.activation(
                                    pt[kt][:], psS[sidx % 2].ap()[:, :], AF.Exp,
                                    scale=self.lam8t[gt][:, h:h + 1],
                                    bias=self.negB[:, 2 * h + ch:2 * h + ch + 1]).then_inc(s["ex"], 1)

                @blk.tensor
                def _(tensor):
                    for ch in range(NCH):
                        tensor.wait_ge(s["ld"], NLD * (ch + 1))
                        tensor.wait_ge(s["vst"], ch + 1)
                        for h in range(H):
                            oc, half = h // 2, h % 2
                            qsl = slice(64 * half, 64 * half + 64)
                            for kt in range(NKT):
                                sidx = (ch * H + h) * NKT + kt
                                if sidx >= 2:
                                    tensor.wait_ge(s["ex"], sidx - 1)
                                for h2 in range(2):
                                    mmi = tensor.matmul(
                                        psS[sidx % 2].ap()[:, 512 * h2:512 * (h2 + 1)],
                                        kfc[oc][qsl, 128 * kt:128 * (kt + 1)],
                                        qfc[oc][qsl, 512 * h2:512 * (h2 + 1)],
                                        start=True, stop=True)
                                mmi.then_inc(s["mmS"], 1)
                            tensor.wait_ge(s["ex"], (ch * H + h + 1) * NKT)
                            for qc in range(NKT):
                                idx = (ch * H + h) * NKT + qc
                                if idx >= 2:
                                    tensor.wait_ge(s["evp"], idx - 1)
                                for kt in range(NKT):
                                    mmi = tensor.matmul(
                                        psP[idx % 2].ap()[:, 0:66],
                                        pt[kt][:, 128 * qc:128 * (qc + 1)],
                                        bass.AP(vc[kt], 68 * h, [[T, 128], [1, 66]]),
                                        start=(kt == 0), stop=(kt == NKT - 1))
                                mmi.then_inc(s["mmP"], 1)
                        tensor.wait_ge(s["evp"], (ch + 1) * H * NKT)
                        for fc in range(NI):
                            for qc in range(NKT):
                                tidx = (ch * NI + fc) * NKT + qc
                                if tidx >= 2:
                                    tensor.wait_ge(s["eva"], tidx - 1)
                                tensor.transpose(psT[tidx % 2].ap()[:, 0:128],
                                                 atk[qc][:, 128 * fc:128 * (fc + 1)],
                                                 self.ident[:]).then_inc(s["tp"], 1)

    def p_l1_outproj(self):
        nc = self.nc
        atf = self.xt
        with contextlib.ExitStack() as st:
            s = self.sems(st, ["l0", "wld", "wfree", "mm", "ev", "sto"])
            wbuf = [self.sb(st, "wbuf2", [128, D]) for _ in range(2)]
            s1b = [self.sb(st, "s1b", [128, 512]) for _ in range(2)]
            psm = [self.ps(st, "pss1", [128, 512]) for _ in range(2)]
            with nc.Block() as blk:
                @blk.sync
                def _(sync):
                    for i in range(NI):
                        sync.dma_start(out=atf[i][:],
                                       in_=self.attnf_sp.ap()[128 * i:128 * (i + 1), :]).then_inc(s["l0"], 16)
                    for oc in range(NI):
                        if oc >= 2:
                            sync.wait_ge(s["ev"], 4 * (oc - 1))
                        for i in range(NI):
                            sync.dma_start(
                                out=wbuf[oc % 2][:, 128 * i:128 * (i + 1)],
                                in_=self.l1_woutT.ap()[128 * i:128 * (i + 1),
                                                       128 * oc:128 * (oc + 1)]).then_inc(s["wld"], 16)

                @blk.tensor
                def _(tensor):
                    tensor.wait_ge(s["l0"], 16 * NI)
                    for oc in range(NI):
                        tensor.wait_ge(s["wld"], 16 * 8 * (oc + 1))
                        for q in range(4):
                            idx = oc * 4 + q
                            if idx >= 2:
                                tensor.wait_ge(s["ev"], idx - 1)
                            for i in range(NI):
                                mmi = tensor.matmul(
                                    psm[idx % 2].ap()[:, :],
                                    wbuf[oc % 2][:, 128 * i:128 * (i + 1)],
                                    atf[i][:, 512 * q:512 * (q + 1)],
                                    start=(i == 0), stop=(i == NI - 1))
                            mmi.then_inc(s["mm"], 1)

                @blk.vector
                def _(vector):
                    for idx in range(NI * 4):
                        vector.wait_ge(s["mm"], idx + 1)
                        if idx >= 2:
                            vector.wait_ge(s["sto"], 16 * (idx - 1))
                        oc = idx // 4
                        vector.tensor_scalar(s1b[idx % 2][:], psm[idx % 2].ap()[:, :],
                                             self.bout_cols[:, oc:oc + 1], None,
                                             OP.add).then_inc(s["ev"], 1)

                @blk.gpsimd
                def _(gpsimd):
                    for idx in range(NI * 4):
                        oc, q = idx // 4, idx % 4
                        gpsimd.wait_ge(s["ev"], idx + 1)
                        gpsimd.dma_start(
                            out=self.s1_sp.ap()[128 * oc:128 * (oc + 1),
                                                512 * q:512 * (q + 1)],
                            in_=s1b[idx % 2][:]).then_inc(s["sto"], 16)

    def p_l1_ffn_final(self):
        nc = self.nc
        HC2 = 2
        NHG2 = (HID // 128) // HC2
        SW = 256
        NS = T // SW
        dacc = self.xt
        xnl = self.xnt
        ytok = self.xnt
        with contextlib.ExitStack() as st:
            s = self.sems(st, ["xl", "wld", "cv", "tf", "mmGU", "sl", "hh",
                               "mmD", "evD", "g1", "lds", "fx", "tp", "evy",
                               "outd"])
            wi32 = self.sb(st, "wi32", [128, 2048], I32)
            wgF = self.sb(st, "wgF", [128, 2048])
            wuF = self.sb(st, "wuF", [128, 2048])
            wdF = self.sb(st, "wdF", [128, 2048])
            gsil = self.sb(st, "gsil", [128, SW])
            hbuf = [self.sb(st, "hbuf", [128, SW]) for _ in range(HC2)]
            g1b = self.sb(st, "g1b", [128, T])
            s1c = self.sb(st, "s1c", [128, T])
            x2c = self.sb(st, "x2c", [128, T])
            xfb = self.sb(st, "xfb", [128, T])
            psG = self.ps(st, "psGate", [128, 512])
            psU = self.ps(st, "psUp", [128, 512])
            psD = [self.ps(st, "psD", [128, 512]) for _ in range(4)]
            psY = [self.ps(st, "psY", [128, 512]) for _ in range(2)]
            WLDG = 16 * (8 + 8 + 2)

            with nc.Block() as blk:
                @blk.sync
                def _(sync):
                    for i in range(NI):
                        sync.dma_start(out=xnl[i][:],
                                       in_=self.xn1spill.ap()[128 * i:128 * (i + 1), :]).then_inc(s["xl"], 16)
                    for hg in range(NHG2):
                        if hg >= 1:
                            sync.wait_ge(s["cv"], 3 * hg)
                        for i in range(NI):
                            sync.dma_start(
                                out=bass.AP(wi32, 256 * i, [[2048, 128], [1, 256]]),
                                in_=self.l1_wgT.ap()[128 * i:128 * (i + 1),
                                                     256 * hg:256 * (hg + 1)]).then_inc(s["wld"], 16)
                        sync.wait_ge(s["cv"], 3 * hg + 1)
                        for i in range(NI):
                            sync.dma_start(
                                out=bass.AP(wi32, 256 * i, [[2048, 128], [1, 256]]),
                                in_=self.l1_wuT.ap()[128 * i:128 * (i + 1),
                                                     256 * hg:256 * (hg + 1)]).then_inc(s["wld"], 16)
                        sync.wait_ge(s["cv"], 3 * hg + 2)
                        for j in range(HC2):
                            sync.dma_start(
                                out=bass.AP(wi32, 1024 * j, [[2048, 128], [1, 1024]]),
                                in_=self.l1_wdT.ap()[256 * hg + 128 * j:256 * hg + 128 * (j + 1),
                                                     :]).then_inc(s["wld"], 16)
                    for oc in range(NI):
                        if oc >= 1:
                            sync.wait_ge(s["fx"], oc)
                        sync.dma_start(out=s1c[:],
                                       in_=self.s1_sp.ap()[128 * oc:128 * (oc + 1), :]).then_inc(s["lds"], 16)
                        sync.dma_start(out=x2c[:],
                                       in_=self.x2spill.ap()[128 * oc:128 * (oc + 1), :]).then_inc(s["lds"], 16)

                @blk.vector
                def _(vector):
                    for hg in range(NHG2):
                        if hg >= 1:
                            vector.wait_ge(s["mmD"], 2 * NS * hg)
                        vector.wait_ge(s["wld"], WLDG * hg + 16 * 8)
                        vector.tensor_copy(wgF[:], wi32[:]).then_inc(s["cv"], 1)
                        vector.wait_ge(s["wld"], WLDG * hg + 16 * 16)
                        vector.tensor_copy(wuF[:], wi32[:]).then_inc(s["cv"], 1)
                        vector.wait_ge(s["wld"], WLDG * hg + 16 * 18)
                        vector.tensor_copy(wdF[:], wi32[:]).then_inc(s["cv"], 1)
                        for sp in range(NS):
                            gsp = hg * NS + sp
                            if gsp >= 1:
                                vector.wait_ge(s["mmD"], 2 * gsp)
                            for hc in range(HC2):
                                j = gsp * HC2 + hc
                                vector.wait_ge(s["sl"], j + 1)
                                vector.tensor_tensor(hbuf[hc][:], gsil[:],
                                                     psU.ap()[:, 0:SW],
                                                     OP.mult).then_inc(s["hh"], 1)
                            for g4 in range(2):
                                didx = gsp * 2 + g4
                                vector.wait_ge(s["mmD"], didx + 1)
                                ssl = slice(SW * sp, SW * (sp + 1))
                                for oc in range(4 * g4, 4 * g4 + 4):
                                    if hg == 0:
                                        vi = vector.tensor_copy(dacc[oc][:, ssl],
                                                                psD[oc % 4].ap()[:, 0:SW])
                                    else:
                                        vi = vector.tensor_add(dacc[oc][:, ssl],
                                                               dacc[oc][:, ssl],
                                                               psD[oc % 4].ap()[:, 0:SW])
                                vi.then_inc(s["evD"], 1)
                    for oc in range(NI):
                        vector.wait_ge(s["lds"], 32 * (oc + 1))
                        vector.wait_ge(s["g1"], oc + 1)
                        if oc >= 1:
                            vector.wait_ge(s["tp"], oc * NT)
                        vector.tensor_mul(xfb[:], s1c[:], g1b[:])
                        vector.tensor_add(xfb[:], xfb[:], x2c[:]).then_inc(s["fx"], 1)
                        for t in range(NT):
                            tidx = oc * NT + t
                            vector.wait_ge(s["tp"], tidx + 1)
                            vector.tensor_copy(
                                ytok[t // 2][:, CS * (t % 2) + 128 * oc:
                                             CS * (t % 2) + 128 * (oc + 1)],
                                psY[tidx % 2].ap()[:, 0:128]).then_inc(s["evy"], 1)

                @blk.scalar
                def _(scalar):
                    for hg in range(NHG2):
                        for sp in range(NS):
                            for hc in range(HC2):
                                j = (hg * NS + sp) * HC2 + hc
                                scalar.wait_ge(s["mmGU"], j + 1)
                                if j >= 1:
                                    scalar.wait_ge(s["hh"], j)
                                scalar.activation(gsil[:], psG.ap()[:, 0:SW], AF.Silu,
                                                  scale=1.0 / 127.0).then_inc(s["sl"], 1)
                    for oc in range(NI):
                        scalar.wait_ge(s["evD"], 2 * NS * NHG2)
                        if oc >= 1:
                            scalar.wait_ge(s["fx"], oc)
                        scalar.activation(g1b[:], dacc[oc][:], AF.Silu,
                                          scale=1.0 / (127.0 * 127.0)).then_inc(s["g1"], 1)

                @blk.tensor
                def _(tensor):
                    tensor.wait_ge(s["xl"], 16 * NI)
                    for hg in range(NHG2):
                        for sp in range(NS):
                            gsp = hg * NS + sp
                            ssl = slice(SW * sp, SW * (sp + 1))
                            for hc in range(HC2):
                                j = gsp * HC2 + hc
                                tensor.wait_ge(s["cv"], 3 * hg + 2)
                                if j >= 1:
                                    tensor.wait_ge(s["sl"], j)
                                    tensor.wait_ge(s["hh"], j)
                                for i in range(NI):
                                    tensor.matmul(
                                        psG.ap()[:, 0:SW],
                                        wgF[:, 256 * i + 128 * hc:256 * i + 128 * (hc + 1)],
                                        xnl[i][:, ssl],
                                        start=(i == 0), stop=(i == NI - 1))
                                for i in range(NI):
                                    mmi = tensor.matmul(
                                        psU.ap()[:, 0:SW],
                                        wuF[:, 256 * i + 128 * hc:256 * i + 128 * (hc + 1)],
                                        xnl[i][:, ssl],
                                        start=(i == 0), stop=(i == NI - 1))
                                mmi.then_inc(s["mmGU"], 1)
                            tensor.wait_ge(s["cv"], 3 * hg + 3)
                            tensor.wait_ge(s["hh"], gsp * HC2 + HC2)
                            for g4 in range(2):
                                didx = gsp * 2 + g4
                                if didx >= 1:
                                    tensor.wait_ge(s["evD"], didx)
                                for oc in range(4 * g4, 4 * g4 + 4):
                                    for hc in range(HC2):
                                        mmi = tensor.matmul(
                                            psD[oc % 4].ap()[:, 0:SW],
                                            wdF[:, 1024 * hc + 128 * oc:1024 * hc + 128 * (oc + 1)],
                                            hbuf[hc][:],
                                            start=(hc == 0), stop=(hc == HC2 - 1))
                                mmi.then_inc(s["mmD"], 1)
                    for oc in range(NI):
                        tensor.wait_ge(s["fx"], oc + 1)
                        for t in range(NT):
                            tidx = oc * NT + t
                            if tidx >= 2:
                                tensor.wait_ge(s["evy"], tidx - 1)
                            tensor.transpose(psY[tidx % 2].ap()[:, 0:128],
                                             xfb[:, 128 * t:128 * (t + 1)],
                                             self.ident[:]).then_inc(s["tp"], 1)

                @blk.gpsimd
                def _(gpsimd):
                    gpsimd.wait_ge(s["evy"], NI * NT)
                    for t in range(NT):
                        gpsimd.dma_start(
                            out=self.y_out.ap()[128 * t:128 * (t + 1), :],
                            in_=ytok[t // 2][:, CS * (t % 2):CS * (t % 2) + D]).then_inc(s["outd"], 16)
                    gpsimd.wait_ge(s["outd"], 16 * NT)

    def p_taps(self):
        if not self.taps:
            return
        nc = self.nc
        with contextlib.ExitStack() as st:
            s = self.sems(st, ["sto"])
            n = [0]
            with nc.Block() as blk:
                @blk.gpsimd
                def _(gpsimd):
                    for name, (srcd, dst) in self.taps.items():
                        gpsimd.dma_start(out=dst.ap()[:, :], in_=srcd.ap()[:, :]).then_inc(s["sto"], 16)
                        n[0] += 16
                    gpsimd.wait_ge(s["sto"], n[0])


def build(tap_names=(), stop_after=None):
    return Builder(tap_names, stop_after=stop_after).build()


def _prep_shared(inputs):
    f = lambda k: np.asarray(inputs[k], dtype=np.float32)
    sh = {}
    sh["l0_wqkv"] = np.ascontiguousarray(
        np.concatenate([f("L0_attn_Wq"), f("L0_attn_Wk"), f("L0_attn_Wv")], axis=1))
    sh["l0_bqkv"] = np.concatenate(
        [f("L0_attn_bq"), f("L0_attn_bk"), f("L0_attn_bv")]).reshape(1, 3 * D)
    sh["l0_wo"] = np.ascontiguousarray(f("L0_attn_Wo"))
    sh["l0_bo"] = np.ascontiguousarray(f("L0_attn_bo").reshape(8, 128).T)
    sh["l0_m"] = f("L0_attn_m").reshape(1, H)
    G1R, G2R = f("L0_ffn_G1_R"), f("L0_ffn_G2_R")
    G1I, G2I = f("L0_ffn_G1_I"), f("L0_ffn_G2_I")

    def s_arr(G1):
        out = np.empty((4, FF, FF), dtype=np.float32)
        for c in range(4):
            g1c = G1[0, :, c, :]
            out[c] = np.broadcast_to(g1c.T[:, None, :, None],
                                     (32, 16, 32, 16)).reshape(FF, FF)
        return np.ascontiguousarray(out.reshape(4 * FF, FF))

    def t_arr(G2):
        out = np.empty((4, FF, FF), dtype=np.float32)
        for c in range(4):
            g2c = G2[c, :, 0, :]
            out[c] = np.broadcast_to(g2c.T[None, :, None, :],
                                     (32, 16, 32, 16)).reshape(FF, FF)
        return np.ascontiguousarray(out.reshape(4 * FF, FF))

    sh["l0_SR"], sh["l0_TR"] = s_arr(G1R), t_arr(G2R)
    sh["l0_SI"], sh["l0_TI"] = s_arr(G1I), t_arr(G2I)
    sh["l0_brbi"] = np.ascontiguousarray(
        np.concatenate([f("L0_ffn_bias_real"), f("L0_ffn_bias_imag")]).reshape(8, 128).T)
    sh["l1_wqkvT"] = np.ascontiguousarray(f("L1_attn_Wqkv").T)
    sh["l1_woutT"] = np.ascontiguousarray(f("L1_attn_Wout").T)
    sh["l1_bout"] = np.ascontiguousarray(f("L1_attn_bout").reshape(8, 128).T)
    sh["l1_lam1"] = np.ascontiguousarray(
        np.stack([f("L1_attn_Wl1")[:, 0], f("L1_attn_bl1")]))
    sh["l1_lam2T"] = np.ascontiguousarray(f("L1_attn_Wl2").T)
    sh["l1_lam2b"] = f("L1_attn_bl2").reshape(1, H)
    sh["l1_wgT"] = np.ascontiguousarray(np.asarray(inputs["L1_ffn_w_gate"]).T)
    sh["l1_wuT"] = np.ascontiguousarray(np.asarray(inputs["L1_ffn_w_up"]).T)
    sh["l1_wdT"] = np.ascontiguousarray(np.asarray(inputs["L1_ffn_w_down"]).T)
    sh["ident_in"] = np.eye(128, dtype=np.float32)
    return sh


def make_in_maps(inputs):
    shared = _prep_shared(inputs)
    maps = []
    for c in range(8):
        b, t0 = c // 4, (c % 4) * T
        d = dict(shared)
        x = np.asarray(inputs["L0_attn_x"][b, t0:t0 + T, :], dtype=np.float32)
        d["x_in"] = np.ascontiguousarray(x.T)
        d["posb_in"] = np.array([[float(t0)]], dtype=np.float32)
        maps.append(d)
    return maps


def kernel(**inputs):
    in_maps = make_in_maps(inputs)
    nc = build()
    res = run_bass_kernel_spmd(nc, in_maps, core_ids=list(range(8)))
    y = np.stack([np.asarray(res.results[c]["y_out"]) for c in range(8)])
    return np.ascontiguousarray(y.reshape(2, 4 * T, D)).astype(np.float32)


# version 06
# speedup vs baseline: 1.0849x; 1.0849x over previous
import contextlib
import numpy as np

import concourse.bass as bass
import concourse.mybir as mybir
from concourse.bass_utils import run_bass_kernel_spmd

F32 = mybir.dt.float32
I32 = mybir.dt.int32
AF = mybir.ActivationFunctionType
OP = mybir.AluOpType
AX = mybir.AxisListType

T = 2048
NT = T // 128
D = 1024
NI = D // 128
H = 16
DH = 64
EPS_LN = 1e-5
EPS_L2 = 1e-12
FF = 512
HID = 3072
HC = 3
NHG = (HID // 128) // HC
CS = 1024
NCH = T // CS
LOG_N = float(np.log(8192.0))
QSCALE = 0.125


QK_MODE = "full"


class Builder:
    def __init__(self, tap_names=(), stop_after=None):
        self.nc = bass.Bass("TRN2", target_bir_lowering=False,
                            detect_race_conditions=False)
        self.tap_names = set(tap_names)
        self.stop_after = stop_after
        self.stack = contextlib.ExitStack()
        self.substacks = []

    GP_NAMES = ("sto", "stoS", "sp", "ccl", "outd")

    def sems(self, st, names):
        out, i = {}, 0
        for n in names:
            if n in self.GP_NAMES:
                out[n] = self.stack.enter_context(self.nc.semaphore(self._uniq(n)))
            else:
                out[n] = self.pool[i]
                i += 1
        return out

    def clear_block(self):
        nc = self.nc
        with nc.Block() as blk:
            @blk.sync
            def _(sync):
                for sem in self.pool:
                    sync.sem_clear(sem)

    _n = 0

    def _uniq(self, n):
        Builder._n += 1
        return f"{n}_{Builder._n}"

    def sb(self, st, name, shape, dtype=F32):
        return st.enter_context(self.nc.sbuf_tensor(self._uniq(name), shape, dtype))

    def ps(self, st, name, shape):
        return st.enter_context(self.nc.psum_tensor(self._uniq(name), shape, F32))

    def build(self):
        nc = self.nc
        big = self.stack

        dt_ = nc.dram_tensor
        self.x_in = dt_("x_in", [D, T], F32, kind="ExternalInput")
        self.l0_wqkv = dt_("l0_wqkv", [D, 3 * D], F32, kind="ExternalInput")
        self.l0_bqkv = dt_("l0_bqkv", [1, 3 * D], F32, kind="ExternalInput")
        self.l0_wo = dt_("l0_wo", [D, D], F32, kind="ExternalInput")
        self.l0_bo = dt_("l0_bo", [128, 8], F32, kind="ExternalInput")
        self.l0_m = dt_("l0_m", [1, H], F32, kind="ExternalInput")
        self.l0_SR = dt_("l0_SR", [4 * FF, FF], F32, kind="ExternalInput")
        self.l0_TR = dt_("l0_TR", [4 * FF, FF], F32, kind="ExternalInput")
        self.l0_SI = dt_("l0_SI", [4 * FF, FF], F32, kind="ExternalInput")
        self.l0_TI = dt_("l0_TI", [4 * FF, FF], F32, kind="ExternalInput")
        self.l0_brbi = dt_("l0_brbi", [128, 8], F32, kind="ExternalInput")
        self.l1_wqkvT = dt_("l1_wqkvT", [D, 3 * D], F32, kind="ExternalInput")
        self.l1_woutT = dt_("l1_woutT", [D, D], F32, kind="ExternalInput")
        self.l1_bout = dt_("l1_bout", [128, 8], F32, kind="ExternalInput")
        self.l1_lam1 = dt_("l1_lam1", [2, 64], F32, kind="ExternalInput")
        self.l1_lam2T = dt_("l1_lam2T", [64, H], F32, kind="ExternalInput")
        self.l1_lam2b = dt_("l1_lam2b", [1, H], F32, kind="ExternalInput")
        self.l1_wgT = dt_("l1_wgT", [D, HID], I32, kind="ExternalInput")
        self.l1_wuT = dt_("l1_wuT", [D, HID], I32, kind="ExternalInput")
        self.l1_wdT = dt_("l1_wdT", [HID, D], I32, kind="ExternalInput")
        self.ident_in = dt_("ident_in", [128, 128], F32, kind="ExternalInput")
        self.posb_in = dt_("posb_in", [1, 1], F32, kind="ExternalInput")
        self.y_out = dt_("y_out", [T, D], F32, kind="ExternalOutput")

        self.xspill = dt_("xspill", [D, T], F32)
        self.q_tok = dt_("q_tok", [T, D], F32)
        self.k_tok = dt_("k_tok", [T, D], F32)
        self.v_tok = dt_("v_tok", [T, D], F32)
        self.qhat_f = dt_("qhat_f", [D, T], F32)
        self.s0_f = dt_("s0_f", [D, T], F32)
        self.s_f = dt_("s_f", [D, T], F32)
        self.x2spill = dt_("x2spill", [D, T], F32)
        self.xn1spill = dt_("xn1spill", [D, T], F32)
        self.qf_sp = dt_("qf_sp", [D, T], F32)
        self.kf_sp = dt_("kf_sp", [D, T], F32)
        self.vtok_sp = dt_("vtok_sp", [T, D], F32)
        self.attnf_sp = dt_("attnf_sp", [D, T], F32)
        self.s1_sp = dt_("s1_sp", [D, T], F32)
        self.cc_in = dt_("cc_in", [64, D], F32)
        self.cc_out = dt_("cc_out", [64, D], F32)
        self.xn0spill = dt_("xn0spill", [D, T], F32)

        self._tappable = {
            "xn0": (self.xn0spill, [D, T]), "q_tok": (self.q_tok, [T, D]),
            "k_tok": (self.k_tok, [T, D]), "v_tok": (self.v_tok, [T, D]),
            "qhat_f": (self.qhat_f, [D, T]), "kv": (self.cc_out, [64, D]),
            "s0_f": (self.s0_f, [D, T]), "s_f": (self.s_f, [D, T]),
            "x2": (self.x2spill, [D, T]), "xn1": (self.xn1spill, [D, T]),
            "qf": (self.qf_sp, [D, T]), "kf": (self.kf_sp, [D, T]),
            "vtok": (self.vtok_sp, [T, D]), "attnf": (self.attnf_sp, [D, T]),
            "s1": (self.s1_sp, [D, T]),
        }
        self.taps = {}
        for name in self.tap_names:
            src, shape = self._tappable[name]
            self.taps[name] = (src, dt_("tap_" + name, shape, F32,
                                        kind="ExternalOutput"))

        self.pool = [big.enter_context(nc.semaphore(f"sp{i}")) for i in range(26)]

        self.ident = self.sb(big, "ident", [128, 128])
        self.ones_col = self.sb(big, "ones_col", [128, 1])
        self.ones2 = self.sb(big, "ones2", [128, 2])
        self.ones_row = self.sb(big, "ones_row", [1, 512])
        self.ones_r1 = self.sb(big, "ones_r1", [1, 128])
        self.bo_cols = self.sb(big, "bo_cols", [128, 8])
        self.brbi_cols = self.sb(big, "brbi_cols", [128, 8])
        self.bout_cols = self.sb(big, "bout_cols", [128, 8])
        self.m_row = self.sb(big, "m_row", [1, H])
        self.posb = self.sb(big, "posb", [1, 1])
        self.xt = [self.sb(big, f"xt{i}", [128, T]) for i in range(NI)]
        self.xnt = [self.sb(big, f"xnt{i}", [128, T]) for i in range(NI)]

        def stop(name):
            return self.stop_after is not None and name == self.stop_after

        done = False
        self.p_load()
        if not stop("load"):
            self.clear_block()
            self.p_ln("ln0", self.xt, self.xnt,
                      tap_spill=self.xn0spill if "xn0" in self.tap_names else None)
        else:
            done = True
        if not done and not stop("ln0"):
            self.clear_block()
            self.p_l0_qkv()
        else:
            done = True
        if not done and not stop("qkv"):
            self.clear_block()
            self.p_l0_kv()
        else:
            done = True
        if not done and not stop("kv"):
            self.clear_block()
            self.p_l0_cc_qhat()
        else:
            done = True
        if not done and not stop("cc"):
            self.clear_block()
            self.p_l0_s0()
            self.kv_stack.close()
        else:
            done = True
        if not done and not stop("s0"):
            self.clear_block()
            self.p_wtbuild()
            self.clear_block()
            self.p_l0_oproj_ffn()
            self.wt_stack.close()
        else:
            done = True
        if not done and not stop("oproj"):
            self.clear_block()
            self.p_ln_reload("ln1", self.x2spill, self.xt, self.xnt,
                             tap_spill=self.xn1spill)
        else:
            done = True
        if not done and not stop("ln1"):
            self.clear_block()
            self.p_l1_qk()
        else:
            done = True
        if not done and not stop("qk"):
            self.clear_block()
            self.p_l1_lam_bound()
        else:
            done = True
        if not done and not stop("lam"):
            self.clear_block()
            self.p_l1_v()
        else:
            done = True
        if not done and not stop("v"):
            self.clear_block()
            self.p_l1_attn()
            self.l1a_stack.close()
        else:
            done = True
        if not done and not stop("attn"):
            self.clear_block()
            self.p_l1_outproj()
        else:
            done = True
        if not done and not stop("outproj"):
            self.clear_block()
            self.p_l1_ffn_final()
        else:
            done = True
        self.clear_block()
        self.p_taps()

        for stk in reversed(self.substacks):
            stk.close()
        self.stack.close()
        return self.nc

    def p_load(self):
        nc = self.nc
        with contextlib.ExitStack() as st:
            s = self.sems(st, ["ld", "sto"])
            n = [0]
            with nc.Block() as blk:
                @blk.sync
                def _(sync):
                    for i in range(NI):
                        sync.dma_start(out=self.xt[i][:],
                                       in_=self.x_in.ap()[128 * i:128 * (i + 1), :]).then_inc(s["ld"], 16)
                        n[0] += 16
                    for dst, src in [(self.ident, self.ident_in),
                                     (self.bo_cols, self.l0_bo),
                                     (self.brbi_cols, self.l0_brbi),
                                     (self.bout_cols, self.l1_bout),
                                     (self.m_row, self.l0_m),
                                     (self.posb, self.posb_in)]:
                        sync.dma_start(out=dst[:], in_=src.ap()[:, :]).then_inc(s["ld"], 16)
                        n[0] += 16

                @blk.gpsimd
                def _(gpsimd):
                    gpsimd.dma_start(out=self.xspill.ap()[:, :],
                                     in_=self.x_in.ap()[:, :]).then_inc(s["sto"], 16)

                @blk.vector
                def _(vector):
                    vector.memset(self.ones_col[:], 1.0)
                    vector.memset(self.ones2[:], 1.0)
                    vector.memset(self.ones_row[:], 1.0)
                    vector.memset(self.ones_r1[:], 1.0)
                    vector.wait_ge(s["ld"], n[0])

    def _emit_ln(self, tag, src_tiles, dst_tiles, tap_spill):
        nc = self.nc
        with contextlib.ExitStack() as st:
            s = self.sems(st, ["sq", "mm", "ev", "bc", "sqr", "sto"])
            with contextlib.ExitStack() as st_stats:
                ps_sum = self.ps(st_stats, "ps_sum", [1, T])
                ps_sumsq = self.ps(st_stats, "ps_sumsq", [1, T])
                xsq = [self.sb(st, "xsq", [128, T]) for _ in range(2)]
                mu = self.sb(st, "mu", [1, T])
                rstd = self.sb(st, "rstd", [1, T])
                tmp1 = self.sb(st, "tmp1", [1, T])
                tmp2 = self.sb(st, "tmp2", [1, T])

                with nc.Block() as blk:
                    @blk.scalar
                    def _(scalar):
                        for i in range(NI):
                            if i >= 2:
                                scalar.wait_ge(s["mm"], i - 1)
                            scalar.activation(xsq[i % 2][:], src_tiles[i][:],
                                              AF.Square).then_inc(s["sq"], 1)
                        scalar.wait_ge(s["ev"], 1)
                        scalar.activation(tmp2[:], tmp1[:], AF.Sqrt).then_inc(s["sqr"], 1)

                    @blk.tensor
                    def _(tensor):
                        for i in range(NI):
                            for q in range(4):
                                sl = slice(512 * q, 512 * (q + 1))
                                tensor.matmul(ps_sum.ap()[:, sl], self.ones_col[:],
                                              src_tiles[i][:, sl],
                                              start=(i == 0), stop=(i == NI - 1))
                            tensor.wait_ge(s["sq"], i + 1)
                            for q in range(4):
                                sl = slice(512 * q, 512 * (q + 1))
                                mmi = tensor.matmul(ps_sumsq.ap()[:, sl], self.ones_col[:],
                                                    xsq[i % 2][:, sl],
                                                    start=(i == 0), stop=(i == NI - 1))
                            mmi.then_inc(s["mm"], 1)

                    @blk.vector
                    def _(vector):
                        vector.wait_ge(s["mm"], NI)
                        vector.tensor_scalar(mu[:], ps_sum.ap()[:, :], 1.0 / D, None, OP.mult)
                        vector.tensor_scalar(tmp1[:], ps_sumsq.ap()[:, :], 1.0 / D, None, OP.mult)
                        vector.tensor_mul(tmp2[:], mu[:], mu[:])
                        vector.tensor_sub(tmp1[:], tmp1[:], tmp2[:])
                        vector.tensor_scalar(tmp1[:], tmp1[:], EPS_LN, None,
                                             OP.add).then_inc(s["ev"], 1)
                        vector.wait_ge(s["sqr"], 1)
                        vector.reciprocal(rstd[:], tmp2[:]).then_inc(s["ev"], 1)

            ps_mu = self.ps(st, "ps_mu", [128, T])
            ps_rs = self.ps(st, "ps_rs", [128, T])
            with nc.Block() as blk:
                @blk.tensor
                def _(tensor):
                    for q in range(4):
                        sl = slice(512 * q, 512 * (q + 1))
                        tensor.matmul(ps_mu.ap()[:, sl], self.ones_r1[:],
                                      mu[:, sl], start=True, stop=True)
                        mmi = tensor.matmul(ps_rs.ap()[:, sl], self.ones_r1[:],
                                            rstd[:, sl], start=True, stop=True)
                    mmi.then_inc(s["bc"], 1)

                @blk.vector
                def _(vector):
                    vector.wait_ge(s["bc"], 1)
                    for i in range(NI):
                        for q in range(4):
                            sl = slice(512 * q, 512 * (q + 1))
                            vector.tensor_sub(dst_tiles[i][:, sl], src_tiles[i][:, sl],
                                              ps_mu.ap()[:, sl])
                            vector.tensor_mul(dst_tiles[i][:, sl], dst_tiles[i][:, sl],
                                              ps_rs.ap()[:, sl]).then_inc(s["ev"], 1)

                @blk.gpsimd
                def _(gpsimd):
                    if tap_spill is not None:
                        for i in range(NI):
                            gpsimd.wait_ge(s["ev"], 2 + 4 * (i + 1))
                            gpsimd.dma_start(out=tap_spill.ap()[128 * i:128 * (i + 1), :],
                                             in_=dst_tiles[i][:]).then_inc(s["sto"], 16)

    def p_ln(self, tag, src, dst, tap_spill=None):
        self._emit_ln(tag, src, dst, tap_spill)

    def p_ln_reload(self, tag, spill, src_tiles, dst_tiles, tap_spill=None):
        nc = self.nc
        with contextlib.ExitStack() as st:
            s = self.sems(st, ["ld"])
            with nc.Block() as blk:
                @blk.sync
                def _(sync):
                    for i in range(NI):
                        sync.dma_start(out=src_tiles[i][:],
                                       in_=spill.ap()[128 * i:128 * (i + 1), :]).then_inc(s["ld"], 16)

                @blk.vector
                def _(vector):
                    vector.wait_ge(s["ld"], 16 * NI)
        self.clear_block()
        self._emit_ln(tag, src_tiles, dst_tiles, tap_spill)

    def p_wtbuild(self):
        nc = self.nc
        self.wt_stack = contextlib.ExitStack()
        self.substacks.append(self.wt_stack)
        big = self.wt_stack
        self.wrT = [self.sb(big, "wrT", [128, FF]) for _ in range(4)]
        self.wiT = [self.sb(big, "wiT", [128, FF]) for _ in range(4)]
        self.wiTn = [self.sb(big, "wiTn", [128, FF]) for _ in range(4)]
        with contextlib.ExitStack() as st:
            s = self.sems(st, ["ld", "used"])
            sbuf = [self.sb(st, "stbuf", [128, FF]) for _ in range(4)]
            tmp = self.sb(st, "wtmp", [128, FF])
            jobs = []
            jobs.append((self.l0_SR, self.l0_TR, self.wrT))
            jobs.append((self.l0_SI, self.l0_TI, self.wiT))
            seq = []
            for Sd, Td, dest in jobs:
                for r in range(4):
                    for c in range(4):
                        seq.append((Sd, Td, dest, r, c))
            with nc.Block() as blk:
                @blk.sync
                def _(sync):
                    for j, (Sd, Td, dest, r, c) in enumerate(seq):
                        if j >= 2:
                            sync.wait_ge(s["used"], j - 1)
                        row = 512 * c + 128 * r
                        sync.dma_start(out=sbuf[2 * (j % 2)][:],
                                       in_=Sd.ap()[row:row + 128, :]).then_inc(s["ld"], 16)
                        sync.dma_start(out=sbuf[2 * (j % 2) + 1][:],
                                       in_=Td.ap()[row:row + 128, :]).then_inc(s["ld"], 16)

                @blk.vector
                def _(vector):
                    for j, (Sd, Td, dest, r, c) in enumerate(seq):
                        vector.wait_ge(s["ld"], 32 * (j + 1))
                        a, b = sbuf[2 * (j % 2)], sbuf[2 * (j % 2) + 1]
                        if c == 0:
                            vector.tensor_mul(dest[r][:], a[:], b[:]).then_inc(s["used"], 1)
                        else:
                            vector.tensor_mul(tmp[:], a[:], b[:]).then_inc(s["used"], 1)
                            vector.tensor_add(dest[r][:], dest[r][:], tmp[:])
                    for r in range(4):
                        vector.tensor_scalar(self.wiTn[r][:], self.wiT[r][:],
                                             -1.0, None, OP.mult)

    def p_l0_qkv(self):
        nc = self.nc
        NOB = 6
        dests = [(self.q_tok, 0), (self.q_tok, 512), (self.k_tok, 0),
                 (self.k_tok, 512), (self.v_tok, 0), (self.v_tok, 512)]
        with contextlib.ExitStack() as st:
            s = self.sems(st, ["wld", "wfree", "mm", "ev", "sto", "bq", "bev"])
            wbuf = [self.sb(st, "wbuf", [128, 8 * 512]) for _ in range(2)]
            obuf = [self.sb(st, "obuf", [128, 512]) for _ in range(2)]
            qkvb = self.sb(st, "qkvb", [128, 3 * D])
            bqrow = self.sb(st, "bqrow", [1, 3 * D])
            psm = [self.ps(st, "psqkv", [128, 512]) for _ in range(2)]
            psb = self.ps(st, "psbias", [128, 512])

            with nc.Block() as blk:
                @blk.sync
                def _(sync):
                    sync.dma_start(out=bqrow[:], in_=self.l0_bqkv.ap()[:, :]).then_inc(s["bq"], 16)
                    for ob in range(NOB):
                        if ob >= 2:
                            sync.wait_ge(s["ev"], NT * (ob - 1))
                        for i in range(NI):
                            sync.dma_start(
                                out=wbuf[ob % 2][:, 512 * i:512 * (i + 1)],
                                in_=self.l0_wqkv.ap()[128 * i:128 * (i + 1),
                                                      512 * ob:512 * (ob + 1)]).then_inc(s["wld"], 16)

                @blk.tensor
                def _(tensor):
                    tensor.wait_ge(s["bq"], 16)
                    for b6 in range(6):
                        if b6 >= 1:
                            tensor.wait_ge(s["bev"], b6)
                        tensor.matmul(psb.ap()[:, :], self.ones_r1[:],
                                      bqrow[:, 512 * b6:512 * (b6 + 1)],
                                      start=True, stop=True).then_inc(s["mm"], 1)
                    for ob in range(NOB):
                        tensor.wait_ge(s["wld"], 16 * 8 * (ob + 1))
                        for t in range(NT):
                            idx = ob * NT + t
                            if idx >= 2:
                                tensor.wait_ge(s["ev"], idx - 1)
                            for i in range(NI):
                                mmi = tensor.matmul(
                                    psm[idx % 2].ap()[:, :],
                                    self.xnt[i][:, 128 * t:128 * (t + 1)],
                                    wbuf[ob % 2][:, 512 * i:512 * (i + 1)],
                                    start=(i == 0), stop=(i == NI - 1))
                            mmi.then_inc(s["mm"], 1)

                @blk.vector
                def _(vector):
                    for b6 in range(6):
                        vector.wait_ge(s["mm"], b6 + 1)
                        vector.tensor_copy(qkvb[:, 512 * b6:512 * (b6 + 1)],
                                           psb.ap()[:, :]).then_inc(s["bev"], 1)
                    for ob in range(NOB):
                        for t in range(NT):
                            idx = ob * NT + t
                            vector.wait_ge(s["mm"], 6 + idx + 1)
                            if idx >= 2:
                                vector.wait_ge(s["sto"], 16 * (idx - 1))
                            vector.tensor_tensor(
                                obuf[idx % 2][:], psm[idx % 2].ap()[:, :],
                                qkvb[:, 512 * ob:512 * (ob + 1)],
                                OP.add).then_inc(s["ev"], 1)

                @blk.gpsimd
                def _(gpsimd):
                    for ob in range(NOB):
                        dstd, coff = dests[ob]
                        for t in range(NT):
                            idx = ob * NT + t
                            gpsimd.wait_ge(s["ev"], idx + 1)
                            gpsimd.dma_start(
                                out=bass.AP(dstd, 128 * t * D + coff,
                                            [[D, 128], [1, 512]]),
                                in_=obuf[idx % 2][:]).then_inc(s["sto"], 16)

    def p_l0_kv(self):
        nc = self.nc
        self.kv_stack = contextlib.ExitStack()
        self.substacks.append(self.kv_stack)
        big = self.kv_stack
        self.srecip = self.sb(big, "srecip", [1, H])
        with contextlib.ExitStack() as st:
            s = self.sems(st, ["ld", "sq", "vr", "sr", "kh", "mm", "ev", "sg", "rc",
                               "sto"])
            kt = [self.sb(st, "ktile", [128, D]) for _ in range(2)]
            vt = [self.sb(st, "vtile", [128, D]) for _ in range(2)]
            ksq = [self.sb(st, "ksq", [128, D]) for _ in range(2)]
            khat = [self.sb(st, "khat", [128, D]) for _ in range(2)]
            ssq = [self.sb(st, "ssq", [128, H]) for _ in range(2)]
            nrm = [self.sb(st, "nrm", [128, H]) for _ in range(2)]
            kvpart = self.sb(st, "kvpart", [64, D])
            sgr = self.sb(st, "sgr", [1, H])
            sclr = self.sb(st, "sclr", [1, H])
            pskv = self.ps(st, "pskv", [64, D])

            with nc.Block() as blk:
                @blk.sync
                def _(sync):
                    for t in range(NT):
                        if t >= 2:
                            sync.wait_ge(s["mm"], t - 1)
                        sync.dma_start(out=kt[t % 2][:],
                                       in_=self.k_tok.ap()[128 * t:128 * (t + 1), :]).then_inc(s["ld"], 16)
                        sync.dma_start(out=vt[t % 2][:],
                                       in_=self.v_tok.ap()[128 * t:128 * (t + 1), :]).then_inc(s["ld"], 16)

                @blk.scalar
                def _(scalar):
                    scalar.activation(sgr[:], self.m_row[:], AF.Sigmoid)
                    scalar.activation(sclr[:], sgr[:], AF.Exp, scale=LOG_N).then_inc(s["sg"], 1)
                    for t in range(NT):
                        scalar.wait_ge(s["ld"], 32 * (t + 1))
                        if t >= 2:
                            scalar.wait_ge(s["vr"], t - 1)
                        scalar.activation(ksq[t % 2][:], kt[t % 2][:],
                                          AF.Square).then_inc(s["sq"], 1)
                        scalar.wait_ge(s["vr"], t + 1)
                        scalar.activation(nrm[t % 2][:], ssq[t % 2][:],
                                          AF.Sqrt).then_inc(s["sr"], 1)

                @blk.vector
                def _(vector):
                    vector.wait_ge(s["sg"], 1)
                    vector.reciprocal(self.srecip[:], sclr[:])
                    for t in range(NT):
                        vector.wait_ge(s["sq"], t + 1)
                        v3 = bass.AP(ksq[t % 2], 0, [[D, 128], [DH, H], [1, DH]])
                        vector.tensor_reduce(ssq[t % 2][:], v3, AX.X,
                                             OP.add).then_inc(s["vr"], 1)
                        vector.wait_ge(s["sr"], t + 1)
                        vector.tensor_scalar_max(nrm[t % 2][:], nrm[t % 2][:], EPS_L2)
                        vector.reciprocal(nrm[t % 2][:], nrm[t % 2][:]).then_inc(s["rc"], 1)
                        vector.wait_ge(s["rc"], t + 1)
                        if t >= 2:
                            vector.wait_ge(s["mm"], t - 1)
                        for h in range(H):
                            vi = vector.tensor_scalar(
                                khat[t % 2][:, DH * h:DH * (h + 1)],
                                kt[t % 2][:, DH * h:DH * (h + 1)],
                                nrm[t % 2][:, h:h + 1], None, OP.mult)
                        vi.then_inc(s["kh"], 1)
                    vector.wait_ge(s["mm"], NT)
                    vector.tensor_copy(kvpart[:], pskv.ap()[:, :]).then_inc(s["ev"], 1)

                @blk.tensor
                def _(tensor):
                    for t in range(NT):
                        tensor.wait_ge(s["kh"], t + 1)
                        for h in range(H):
                            mmi = tensor.matmul(
                                pskv.ap()[:, DH * h:DH * (h + 1)],
                                khat[t % 2][:, DH * h:DH * (h + 1)],
                                vt[t % 2][:, DH * h:DH * (h + 1)],
                                start=(t == 0 and h % 8 == 0),
                                stop=(t == NT - 1 and h % 8 == 7),
                                skip_group_check=True)
                        mmi.then_inc(s["mm"], 1)

                @blk.gpsimd
                def _(gpsimd):
                    gpsimd.wait_ge(s["ev"], 1)
                    gpsimd.dma_start(out=self.cc_in.ap()[:, :],
                                     in_=kvpart[:]).then_inc(s["sto"], 16)
                    gpsimd.wait_ge(s["sto"], 16)

    def p_l0_cc_qhat(self):
        nc = self.nc
        big = self.kv_stack
        self.kvbar = self.sb(big, "kvbar", [64, D])
        with contextlib.ExitStack() as st:
            s = self.sems(st, ["cc", "ccl", "bc", "bev", "ld", "sq", "vr", "sr",
                              "qh", "tp", "ev", "rc", "sto"])
            qt = [self.sb(st, "qtile", [128, D]) for _ in range(2)]
            qsq = [self.sb(st, "qsq", [128, D]) for _ in range(2)]
            qhat = [self.sb(st, "qhat", [128, D]) for _ in range(2)]
            ssq = [self.sb(st, "ssq2", [128, H]) for _ in range(2)]
            nrm = [self.sb(st, "nrm2", [128, H]) for _ in range(2)]
            qtr = [self.sb(st, "qtr", [64, T]) for _ in range(2)]
            kvld = self.sb(st, "kvld", [64, D])
            srp = self.sb(st, "srp", [64, H])
            pssr = self.ps(st, "pssr", [64, 512])
            pstr = [self.ps(st, "pstr", [64, 512]) for _ in range(4)]

            with nc.Block() as blk:
                @blk.gpsimd
                def _(gpsimd):
                    gpsimd.collective_compute(
                        "AllReduce", OP.add,
                        ins=[self.cc_in.ap()[:, :]],
                        outs=[self.cc_out.ap()[:, :]],
                        replica_groups=[[0, 1, 2, 3], [4, 5, 6, 7]],
                    ).then_inc(s["cc"], 1)
                    gpsimd.wait_ge(s["cc"], 1)
                    gpsimd.dma_start(out=kvld[:], in_=self.cc_out.ap()[:, :]).then_inc(s["ccl"], 16)
                    for t in range(NT):
                        gpsimd.wait_ge(s["ev"], (t + 1) * H)
                        gpsimd.dma_start(
                            out=bass.AP(self.qhat_f, 128 * t,
                                        [[T, 64], [64 * T, H], [1, 128]]),
                            in_=bass.AP(qtr[t % 2], 0, [[T, 64], [128, H], [1, 128]]),
                        ).then_inc(s["sto"], 16)

                @blk.tensor
                def _(tensor):
                    tensor.matmul(pssr.ap()[:, 0:H], self.ones_r1[:, 0:64],
                                  self.srecip[:], start=True, stop=True).then_inc(s["bc"], 1)
                    for t in range(NT):
                        tensor.wait_ge(s["qh"], t + 1)
                        for h in range(H):
                            j = (t * H + h) % 4
                            if t * H + h >= 4:
                                tensor.wait_ge(s["ev"], t * H + h - 3)
                            tensor.transpose(pstr[j].ap()[:, 0:128],
                                             qhat[t % 2][:, DH * h:DH * (h + 1)],
                                             self.ident[:]).then_inc(s["tp"], 1)

                @blk.sync
                def _(sync):
                    for t in range(NT):
                        if t >= 2:
                            sync.wait_ge(s["tp"], (t - 1) * H)
                        sync.dma_start(out=qt[t % 2][:],
                                       in_=self.q_tok.ap()[128 * t:128 * (t + 1), :]).then_inc(s["ld"], 16)

                @blk.scalar
                def _(scalar):
                    for t in range(NT):
                        scalar.wait_ge(s["ld"], 16 * (t + 1))
                        if t >= 2:
                            scalar.wait_ge(s["vr"], t - 1)
                        scalar.activation(qsq[t % 2][:], qt[t % 2][:],
                                          AF.Square).then_inc(s["sq"], 1)
                        scalar.wait_ge(s["vr"], t + 1)
                        scalar.activation(nrm[t % 2][:], ssq[t % 2][:],
                                          AF.Sqrt).then_inc(s["sr"], 1)

                @blk.vector
                def _(vector):
                    vector.wait_ge(s["bc"], 1)
                    vector.tensor_copy(srp[:], pssr.ap()[:, 0:H]).then_inc(s["bev"], 1)
                    vector.wait_ge(s["bev"], 1)
                    vector.wait_ge(s["ccl"], 16)
                    for h in range(H):
                        vector.tensor_scalar(self.kvbar[:, DH * h:DH * (h + 1)],
                                             kvld[:, DH * h:DH * (h + 1)],
                                             srp[:, h:h + 1], None, OP.mult)
                    for t in range(NT):
                        vector.wait_ge(s["sq"], t + 1)
                        v3 = bass.AP(qsq[t % 2], 0, [[D, 128], [DH, H], [1, DH]])
                        vector.tensor_reduce(ssq[t % 2][:], v3, AX.X,
                                             OP.add).then_inc(s["vr"], 1)
                        vector.wait_ge(s["sr"], t + 1)
                        vector.tensor_scalar_max(nrm[t % 2][:], nrm[t % 2][:], EPS_L2)
                        vector.reciprocal(nrm[t % 2][:], nrm[t % 2][:]).then_inc(s["rc"], 1)
                        vector.wait_ge(s["rc"], t + 1)
                        if t >= 2:
                            vector.wait_ge(s["tp"], (t - 1) * H)
                        for h in range(H):
                            vi = vector.tensor_scalar(
                                qhat[t % 2][:, DH * h:DH * (h + 1)],
                                qt[t % 2][:, DH * h:DH * (h + 1)],
                                nrm[t % 2][:, h:h + 1], None, OP.mult)
                        vi.then_inc(s["qh"], 1)
                        if t >= 2:
                            vector.wait_ge(s["sto"], 16 * (t - 1))
                        for h in range(H):
                            vector.wait_ge(s["tp"], t * H + h + 1)
                            vector.tensor_copy(qtr[t % 2][:, 128 * h:128 * (h + 1)],
                                               pstr[(t * H + h) % 4].ap()[:, 0:128]).then_inc(s["ev"], 1)

    def p_l0_s0(self):
        nc = self.nc
        with contextlib.ExitStack() as st:
            s = self.sems(st, ["ld", "mm", "ev", "sto"])
            qhf = [self.sb(st, "qhf", [64, T]) for _ in range(2)]
            s0b = [self.sb(st, "s0b", [64, T]) for _ in range(2)]
            psb = [self.ps(st, "pss0", [64, 512]) for _ in range(2)]
            with nc.Block() as blk:
                @blk.sync
                def _(sync):
                    for h in range(H):
                        if h >= 2:
                            sync.wait_ge(s["mm"], 4 * (h - 1))
                        sync.dma_start(out=qhf[h % 2][:],
                                       in_=self.qhat_f.ap()[64 * h:64 * (h + 1), :]).then_inc(s["ld"], 16)

                @blk.tensor
                def _(tensor):
                    for h in range(H):
                        tensor.wait_ge(s["ld"], 16 * (h + 1))
                        for q in range(4):
                            idx = h * 4 + q
                            if idx >= 2:
                                tensor.wait_ge(s["ev"], idx - 1)
                            tensor.matmul(psb[idx % 2].ap()[:, :],
                                          self.kvbar[:, DH * h:DH * (h + 1)],
                                          qhf[h % 2][:, 512 * q:512 * (q + 1)],
                                          start=True, stop=True).then_inc(s["mm"], 1)

                @blk.vector
                def _(vector):
                    for h in range(H):
                        if h >= 2:
                            vector.wait_ge(s["sto"], 16 * (h - 1))
                        for q in range(4):
                            idx = h * 4 + q
                            vector.wait_ge(s["mm"], idx + 1)
                            vector.tensor_copy(s0b[h % 2][:, 512 * q:512 * (q + 1)],
                                               psb[idx % 2].ap()[:, :]).then_inc(s["ev"], 1)

                @blk.gpsimd
                def _(gpsimd):
                    for h in range(H):
                        gpsimd.wait_ge(s["ev"], 4 * (h + 1))
                        gpsimd.dma_start(out=self.s0_f.ap()[64 * h:64 * (h + 1), :],
                                         in_=s0b[h % 2][:]).then_inc(s["sto"], 16)

    def p_l0_oproj_ffn(self):
        nc = self.nc
        s0t = self.xt
        with contextlib.ExitStack() as st:
            s = self.sems(st, ["l0", "wld", "wfree", "mmS", "evS", "mmG", "gv",
                              "ldx", "ev2", "sto", "stoS"])
            wo_buf = [self.sb(st, "wo_buf", [128, D]) for _ in range(2)]
            schunk = self.sb(st, "schunk", [128, T])
            gchunk = self.sb(st, "gchunk", [128, T])
            xold = [self.sb(st, "xold", [128, T]) for _ in range(2)]
            x2b = [self.sb(st, "x2b", [128, 512]) for _ in range(2)]
            psS = [self.ps(st, "psS", [128, 512]) for _ in range(2)]
            psG = [self.ps(st, "psG", [128, 512]) for _ in range(2)]

            with nc.Block() as blk:
                @blk.sync
                def _(sync):
                    for i in range(NI):
                        sync.dma_start(out=s0t[i][:],
                                       in_=self.s0_f.ap()[128 * i:128 * (i + 1), :]).then_inc(s["l0"], 16)
                    for oc in range(NI):
                        if oc >= 2:
                            sync.wait_ge(s["evS"], 4 * (oc - 1))
                        for i in range(NI):
                            sync.dma_start(
                                out=wo_buf[oc % 2][:, 128 * i:128 * (i + 1)],
                                in_=self.l0_wo.ap()[128 * i:128 * (i + 1),
                                                    128 * oc:128 * (oc + 1)]).then_inc(s["wld"], 16)
                        if oc >= 2:
                            sync.wait_ge(s["ev2"], 4 * (oc - 1))
                        sync.dma_start(out=xold[oc % 2][:],
                                       in_=self.xspill.ap()[128 * oc:128 * (oc + 1), :]).then_inc(s["ldx"], 16)

                @blk.tensor
                def _(tensor):
                    tensor.wait_ge(s["l0"], 16 * NI)
                    for oc in range(NI):
                        tensor.wait_ge(s["wld"], 16 * 8 * (oc + 1))
                        for q in range(4):
                            idx = oc * 4 + q
                            if idx >= 2:
                                tensor.wait_ge(s["evS"], idx - 1)
                            for i in range(NI):
                                mmi = tensor.matmul(
                                    psS[idx % 2].ap()[:, :],
                                    wo_buf[oc % 2][:, 128 * i:128 * (i + 1)],
                                    s0t[i][:, 512 * q:512 * (q + 1)],
                                    start=(i == 0), stop=(i == NI - 1))
                            mmi.then_inc(s["mmS"], 1)
                            if idx >= 2:
                                tensor.wait_ge(s["gv"], idx - 1)
                            oc2 = oc % 4
                            if oc < 4:
                                pairs = [(self.wrT[ic], self.xnt[ic]) for ic in range(4)] + \
                                        [(self.wiTn[ic], self.xnt[4 + ic]) for ic in range(4)]
                            else:
                                pairs = [(self.wiT[ic], self.xnt[ic]) for ic in range(4)] + \
                                        [(self.wrT[ic], self.xnt[4 + ic]) for ic in range(4)]
                            for jj, (wt, xn) in enumerate(pairs):
                                mmi = tensor.matmul(
                                    psG[idx % 2].ap()[:, :],
                                    wt[:, 128 * oc2:128 * (oc2 + 1)],
                                    xn[:, 512 * q:512 * (q + 1)],
                                    start=(jj == 0), stop=(jj == 7))
                            mmi.then_inc(s["mmG"], 1)

                @blk.scalar
                def _(scalar):
                    for oc in range(NI):
                        for q in range(4):
                            idx = oc * 4 + q
                            scalar.wait_ge(s["mmG"], idx + 1)
                            if idx >= 1:
                                scalar.wait_ge(s["ev2"], idx)
                            scalar.activation(gchunk[:, 512 * q:512 * (q + 1)],
                                              psG[idx % 2].ap()[:, :], AF.Silu,
                                              bias=self.brbi_cols[:, oc:oc + 1]).then_inc(s["gv"], 1)

                @blk.vector
                def _(vector):
                    for oc in range(NI):
                        vector.wait_ge(s["ldx"], 16 * (oc + 1))
                        if oc >= 1:
                            vector.wait_ge(s["stoS"], 16 * oc)
                        for q in range(4):
                            idx = oc * 4 + q
                            sl = slice(512 * q, 512 * (q + 1))
                            vector.wait_ge(s["mmS"], idx + 1)
                            vector.tensor_scalar(schunk[:, sl], psS[idx % 2].ap()[:, :],
                                                 self.bo_cols[:, oc:oc + 1], None,
                                                 OP.add).then_inc(s["evS"], 1)
                            vector.wait_ge(s["gv"], idx + 1)
                            if idx >= 2:
                                vector.wait_ge(s["sto"], 16 * (idx - 1))
                            vector.tensor_mul(x2b[idx % 2][:],
                                              schunk[:, sl], gchunk[:, sl])
                            vector.tensor_add(x2b[idx % 2][:],
                                              x2b[idx % 2][:],
                                              xold[oc % 2][:, sl]).then_inc(s["ev2"], 1)

                @blk.gpsimd
                def _(gpsimd):
                    for oc in range(NI):
                        for q in range(4):
                            idx = oc * 4 + q
                            gpsimd.wait_ge(s["ev2"], idx + 1)
                            gpsimd.dma_start(
                                out=self.x2spill.ap()[128 * oc:128 * (oc + 1),
                                                      512 * q:512 * (q + 1)],
                                in_=x2b[idx % 2][:]).then_inc(s["sto"], 16)
                        gpsimd.wait_ge(s["evS"], 4 * (oc + 1))
                        gpsimd.dma_start(out=self.s_f.ap()[128 * oc:128 * (oc + 1), :],
                                         in_=schunk[:]).then_inc(s["stoS"], 16)

    def p_l1_qk(self):
        nc = self.nc
        self.l1a_stack = contextlib.ExitStack()
        self.substacks.append(self.l1a_stack)
        big = self.l1a_stack
        self.normq = self.sb(big, "normq", [128, 256])
        self.normk = self.sb(big, "normk", [128, 256])
        with contextlib.ExitStack() as st:
            s = self.sems(st, ["wld", "wfree", "mm", "sq", "nm", "ev", "sto"])
            wbuf = [self.sb(st, "wbuf1", [128, D]) for _ in range(4)]
            qkb = [self.sb(st, "qkb", [128, 512]) for _ in range(2)]
            nbounce = self.sb(st, "nbounce", [128, 512])
            sqb = [self.sb(st, "sqb", [128, 512]) for _ in range(2)]
            psm = [self.ps(st, "psqk", [128, 512]) for _ in range(2)]
            psnq = self.ps(st, "psnq", [128, 512])
            psnk = self.ps(st, "psnk", [128, 512])

            with nc.Block() as blk:
                @blk.sync
                def _(sync):
                    for oc in range(16):
                        if oc >= 4:
                            sync.wait_ge(s["ev"], 4 * (oc - 3))
                        for i in range(NI):
                            sync.dma_start(
                                out=wbuf[oc % 4][:, 128 * i:128 * (i + 1)],
                                in_=self.l1_wqkvT.ap()[128 * i:128 * (i + 1),
                                                       128 * oc:128 * (oc + 1)]).then_inc(s["wld"], 16)

                @blk.tensor
                def _(tensor):
                    for oc in range(16):
                        tensor.wait_ge(s["wld"], 16 * 8 * (oc + 1))
                        for q in range(4):
                            idx = oc * 4 + q
                            if idx >= 2:
                                tensor.wait_ge(s["ev"], idx - 1)
                                tensor.wait_ge(s["nm"], idx - 1)
                            for i in range(NI):
                                mmi = tensor.matmul(
                                    psm[idx % 2].ap()[:, :],
                                    wbuf[oc % 4][:, 128 * i:128 * (i + 1)],
                                    self.xnt[i][:, 512 * q:512 * (q + 1)],
                                    start=(i == 0), stop=(i == NI - 1))
                            mmi.then_inc(s["mm"], 1)
                            if QK_MODE == "nonorm":
                                tensor.nop().then_inc(s["nm"], 1)
                                continue
                            tensor.wait_ge(s["sq"], idx + 1)
                            pn = psnq if oc < 8 else psnk
                            for half in range(2):
                                hcol = (2 * (oc % 8) + half) * 16
                                for m in range(4):
                                    col = hcol + q * 4 + m
                                    mmi = tensor.matmul(
                                        pn.ap()[:, 2 * col:2 * col + 2],
                                        sqb[idx % 2][64 * half:64 * half + 64,
                                                     128 * m:128 * (m + 1)],
                                        self.ones2[64 * half:64 * half + 64, :],
                                        start=True, stop=True)
                            mmi.then_inc(s["nm"], 1)

                @blk.scalar
                def _(scalar):
                    if QK_MODE == "nonorm":
                        pass
                    else:
                        for oc in range(16):
                            for q in range(4):
                                idx = oc * 4 + q
                                scalar.wait_ge(s["mm"], idx + 1)
                                if idx >= 2:
                                    scalar.wait_ge(s["nm"], idx - 1)
                                    scalar.wait_ge(s["ev"], idx - 1)
                                scalar.activation(sqb[idx % 2][:], psm[idx % 2].ap()[:, :],
                                                  AF.Square).then_inc(s["sq"], 1)

                @blk.vector
                def _(vector):
                    for oc in range(16):
                        for q in range(4):
                            idx = oc * 4 + q
                            vector.wait_ge(s["sq"], idx + 1)
                            if idx >= 2:
                                vector.wait_ge(s["sto"], 16 * (idx - 1))
                            vector.tensor_copy(qkb[idx % 2][:],
                                               psm[idx % 2].ap()[:, :]).then_inc(s["ev"], 1)
                    vector.wait_ge(s["nm"], 64)
                    vector.tensor_copy(nbounce[:], psnq.ap()[:, 0:512])
                    vector.tensor_copy(self.normq[:],
                                       bass.AP(nbounce, 0, [[512, 128], [2, 256]]))
                    vector.tensor_copy(nbounce[:], psnk.ap()[:, 0:512])
                    vector.tensor_copy(self.normk[:],
                                       bass.AP(nbounce, 0, [[512, 128], [2, 256]]))

                @blk.gpsimd
                def _(gpsimd):
                    for oc in range(16):
                        dstd = self.qf_sp if oc < 8 else self.kf_sp
                        for q in range(4):
                            idx = oc * 4 + q
                            gpsimd.wait_ge(s["ev"], idx + 1)
                            gpsimd.dma_start(
                                out=dstd.ap()[128 * (oc % 8):128 * (oc % 8 + 1),
                                              512 * q:512 * (q + 1)],
                                in_=qkb[idx % 2][:]).then_inc(s["sto"], 16)

    def p_l1_lam_bound(self):
        nc = self.nc
        big = self.l1a_stack
        self.lam8t = [self.sb(big, "lam8t", [128, H]) for _ in range(NT)]
        self.negB = self.sb(big, "negB", [128, 32])
        with contextlib.ExitStack() as st:
            s = self.sems(st, ["io", "ln", "mm1", "rl", "mm2", "sgm", "l8",
                              "tp", "ev", "nb", "ld"])
            posrow = self.sb(st, "posrow", [1, T])
            logrow = self.sb(st, "logrow", [1, T])
            lamrhs = self.sb(st, "lamrhs", [2, T])
            h1 = self.sb(st, "h1", [64, T])
            lamf = self.sb(st, "lamf", [H, T])
            lam8f = self.sb(st, "lam8f", [H, T])
            lam1t = self.sb(st, "lam1t", [2, 64])
            lam2Tt = self.sb(st, "lam2Tt", [64, H])
            lam2bt = self.sb(st, "lam2bt", [1, H])
            mqcol = self.sb(st, "mqcol", [128, 2])
            mkcol = self.sb(st, "mkcol", [128, 2])
            mq2 = self.sb(st, "mq2", [1, 32])
            mk2 = self.sb(st, "mk2", [1, 32])
            brow = self.sb(st, "brow", [1, 32])
            ps1 = [self.ps(st, "pslam", [64, 512]) for _ in range(2)]
            ps2 = [self.ps(st, "pslam2", [H, 512]) for _ in range(2)]
            pst = [self.ps(st, "pslt", [128, 512]) for _ in range(2)]
            psr = self.ps(st, "pslr", [1, 128])
            psnb = self.ps(st, "psnb", [128, 32])

            with nc.Block() as blk:
                @blk.sync
                def _(sync):
                    sync.dma_start(out=lam1t[:], in_=self.l1_lam1.ap()[:, :]).then_inc(s["ld"], 16)
                    sync.dma_start(out=lam2Tt[:], in_=self.l1_lam2T.ap()[:, :]).then_inc(s["ld"], 16)
                    sync.dma_start(out=lam2bt[:], in_=self.l1_lam2b.ap()[:, :]).then_inc(s["ld"], 16)

                @blk.gpsimd
                def _(gpsimd):
                    gpsimd.iota(posrow[:], [[1, T]], channel_multiplier=0,
                                allow_small_or_imprecise_dtypes=True).then_inc(s["io"], 1)

                @blk.vector
                def _(vector):
                    vector.wait_ge(s["io"], 1)
                    vector.tensor_scalar(posrow[:], posrow[:], self.posb[:], 1.0,
                                         OP.add, OP.add).then_inc(s["io"], 1)
                    vector.wait_ge(s["ln"], 1)
                    vector.memset(lamrhs[:, :], 1.0)
                    vector.tensor_copy(lamrhs[0:1, :], logrow[:]).then_inc(s["io"], 1)
                    vector.wait_ge(s["sgm"], 4)
                    vector.tensor_scalar(lam8f[:], lamf[:], QSCALE, None,
                                         OP.mult).then_inc(s["l8"], 1)
                    for t in range(NT):
                        vector.wait_ge(s["tp"], t + 1)
                        vector.tensor_copy(self.lam8t[t][:], pst[t % 2].ap()[:, 0:H]).then_inc(s["ev"], 1)
                    for j, (src, dst) in enumerate([(self.normq, mqcol), (self.normk, mkcol)]):
                        for half in range(2):
                            vector.wait_ge(s["tp"], NT + 2 * j + half + 1)
                            vector.reduce_max(dst[:, half:half + 1],
                                              pst[(2 * j + half) % 2].ap()[:, 0:128],
                                              AX.X).then_inc(s["ev"], 1)
                    for j, (colt, dst) in enumerate([(mqcol, mq2), (mkcol, mk2)]):
                        for half in range(2):
                            vector.wait_ge(s["tp"], NT + 4 + 2 * j + half + 1)
                            v4 = bass.AP(psr, 0, [[128, 1], [16, 8], [8, 2], [1, 8]])
                            vector.reduce_max(dst[:, 16 * half:16 * (half + 1)],
                                              v4, AX.X).then_inc(s["ev"], 1)
                    vector.tensor_mul(brow[:], mq2[:], mk2[:]).then_inc(s["nb"], 1)
                    vector.wait_ge(s["nb"], 2)
                    vector.tensor_scalar(brow[:], brow[:], -QSCALE, None,
                                         OP.mult).then_inc(s["nb"], 1)
                    vector.wait_ge(s["nb"], 4)
                    vector.tensor_copy(self.negB[:], psnb.ap()[:, :])

                @blk.scalar
                def _(scalar):
                    scalar.wait_ge(s["io"], 2)
                    scalar.activation(logrow[:], posrow[:], AF.Ln).then_inc(s["ln"], 1)
                    for q in range(4):
                        scalar.wait_ge(s["mm1"], q + 1)
                        scalar.activation(h1[:, 512 * q:512 * (q + 1)],
                                          ps1[q % 2].ap()[:, :], AF.Relu).then_inc(s["rl"], 1)
                    for q in range(4):
                        scalar.wait_ge(s["mm2"], q + 1)
                        scalar.activation(lamf[:, 512 * q:512 * (q + 1)],
                                          ps2[q % 2].ap()[:, :], AF.Sigmoid).then_inc(s["sgm"], 1)
                    scalar.wait_ge(s["nb"], 1)
                    scalar.activation(brow[:], brow[:], AF.Sqrt).then_inc(s["nb"], 1)

                @blk.tensor
                def _(tensor):
                    tensor.wait_ge(s["io"], 3)
                    tensor.wait_ge(s["ld"], 48)
                    for q in range(4):
                        if q >= 2:
                            tensor.wait_ge(s["rl"], q - 1)
                        tensor.matmul(ps1[q % 2].ap()[:, :], lam1t[:],
                                      lamrhs[:, 512 * q:512 * (q + 1)],
                                      start=True, stop=True).then_inc(s["mm1"], 1)
                    for q in range(4):
                        if q >= 2:
                            tensor.wait_ge(s["sgm"], q - 1)
                        tensor.wait_ge(s["rl"], q + 1)
                        tensor.matmul(ps2[q % 2].ap()[:, :], lam2Tt[:],
                                      h1[:, 512 * q:512 * (q + 1)],
                                      start=True, stop=False)
                        tensor.matmul(ps2[q % 2].ap()[:, :], lam2bt[:],
                                      self.ones_row[:, 0:512],
                                      start=False, stop=True).then_inc(s["mm2"], 1)
                    tensor.wait_ge(s["l8"], 1)
                    for t in range(NT):
                        if t >= 2:
                            tensor.wait_ge(s["ev"], t - 1)
                        tensor.transpose(pst[t % 2].ap()[:, 0:H],
                                         lam8f[:, 128 * t:128 * (t + 1)],
                                         self.ident[0:H, 0:H]).then_inc(s["tp"], 1)
                    for j, src in enumerate([self.normq, self.normk]):
                        for half in range(2):
                            k = 2 * j + half
                            tensor.wait_ge(s["ev"], NT - 1 + k)
                            tensor.transpose(pst[k % 2].ap()[:, 0:128],
                                             src[:, 128 * half:128 * (half + 1)],
                                             self.ident[:]).then_inc(s["tp"], 1)
                    for j, colt in enumerate([mqcol, mkcol]):
                        for half in range(2):
                            k = 2 * j + half
                            tensor.wait_ge(s["ev"], NT + 2 + k)
                            tensor.transpose(psr.ap()[:, :], colt[:, half:half + 1],
                                             self.ident[:]).then_inc(s["tp"], 1)
                            tensor.wait_ge(s["ev"], NT + 4 + k + 1)
                    tensor.wait_ge(s["nb"], 3)
                    tensor.matmul(psnb.ap()[:, :], self.ones_r1[:], brow[:],
                                  start=True, stop=True).then_inc(s["nb"], 1)

    def p_l1_v(self):
        nc = self.nc
        with contextlib.ExitStack() as st:
            s = self.sems(st, ["wld", "mm", "ev", "sto"])
            wv = self.sb(st, "wvslab", [128, 8 * D])
            vb = [self.sb(st, "vb", [128, D]) for _ in range(2)]
            psm = [self.ps(st, "psv", [128, D]) for _ in range(2)]
            with nc.Block() as blk:
                @blk.sync
                def _(sync):
                    for i in range(NI):
                        sync.dma_start(
                            out=wv[:, D * i:D * (i + 1)],
                            in_=self.l1_wqkvT.ap()[128 * i:128 * (i + 1), 2 * D:3 * D]).then_inc(s["wld"], 16)

                @blk.tensor
                def _(tensor):
                    tensor.wait_ge(s["wld"], 16 * NI)
                    for t in range(NT):
                        if t >= 2:
                            tensor.wait_ge(s["ev"], t - 1)
                        for i in range(NI):
                            for half in range(2):
                                mmi = tensor.matmul(
                                    psm[t % 2].ap()[:, 512 * half:512 * (half + 1)],
                                    self.xnt[i][:, 128 * t:128 * (t + 1)],
                                    wv[:, D * i + 512 * half:D * i + 512 * (half + 1)],
                                    start=(i == 0), stop=(i == NI - 1))
                        mmi.then_inc(s["mm"], 1)

                @blk.vector
                def _(vector):
                    for t in range(NT):
                        vector.wait_ge(s["mm"], t + 1)
                        if t >= 2:
                            vector.wait_ge(s["sto"], 16 * (t - 1))
                        vector.tensor_copy(vb[t % 2][:], psm[t % 2].ap()[:, :]).then_inc(s["ev"], 1)

                @blk.gpsimd
                def _(gpsimd):
                    for t in range(NT):
                        gpsimd.wait_ge(s["ev"], t + 1)
                        gpsimd.dma_start(out=self.vtok_sp.ap()[128 * t:128 * (t + 1), :],
                                         in_=vb[t % 2][:]).then_inc(s["sto"], 16)

    def p_l1_attn(self):
        nc = self.nc
        NKT = CS // 128
        with contextlib.ExitStack() as st:
            s = self.sems(st, ["sp", "ld", "vst", "mmS", "ex", "mmP", "evp",
                               "tp", "eva", "rc", "sto"])
            qfc = [self.xt[i][:, 0:CS] for i in range(NI)]
            kfc = [self.xt[i][:, CS:2 * CS] for i in range(NI)]
            vc = [self.xnt[kt] for kt in range(NI)]
            pt = [self.sb(st, "ptile", [128, CS]) for _ in range(NKT)]
            atk = [self.sb(st, "atk", [128, D]) for _ in range(NKT)]
            atf = [self.sb(st, "atf", [128, CS]) for _ in range(2)]
            rec = self.sb(st, "recip", [128, 1])
            psS = [self.ps(st, "psSc", [128, CS]) for _ in range(2)]
            psP = [self.ps(st, "psPV", [128, 512]) for _ in range(2)]
            psT = [self.ps(st, "psTr", [128, 512]) for _ in range(2)]
            NLD = 16 * (2 * NI + NKT)

            with nc.Block() as blk:
                @blk.gpsimd
                def _(gpsimd):
                    for i in range(NI):
                        gpsimd.dma_start(out=self.xn1spill.ap()[128 * i:128 * (i + 1), :],
                                         in_=self.xnt[i][:, :]).then_inc(s["sp"], 16)
                    for ch in range(NCH):
                        for fc in range(NI):
                            gpsimd.wait_ge(s["eva"], ch * NI * NKT + (fc + 1) * NKT)
                            gpsimd.dma_start(
                                out=self.attnf_sp.ap()[128 * fc:128 * (fc + 1),
                                                       CS * ch:CS * (ch + 1)],
                                in_=atf[fc % 2][:]).then_inc(s["sto"], 16)

                @blk.sync
                def _(sync):
                    sync.wait_ge(s["sp"], 16 * NI)
                    for ch in range(NCH):
                        if ch >= 1:
                            sync.wait_ge(s["mmP"], ch * H * NKT)
                        for i in range(NI):
                            sync.dma_start(out=qfc[i],
                                           in_=self.qf_sp.ap()[128 * i:128 * (i + 1),
                                                               CS * ch:CS * (ch + 1)]).then_inc(s["ld"], 16)
                            sync.dma_start(out=kfc[i],
                                           in_=self.kf_sp.ap()[128 * i:128 * (i + 1),
                                                               CS * ch:CS * (ch + 1)]).then_inc(s["ld"], 16)
                        for kt in range(NKT):
                            sync.dma_start(
                                out=bass.AP(vc[kt], 0, [[T, 128], [68, H], [1, DH]]),
                                in_=bass.AP(self.vtok_sp, (CS * ch + 128 * kt) * D,
                                            [[D, 128], [DH, H], [1, DH]])).then_inc(s["ld"], 16)

                @blk.vector
                def _(vector):
                    for ch in range(NCH):
                        vector.wait_ge(s["ld"], NLD * (ch + 1))
                        for kt in range(NKT):
                            vector.memset(bass.AP(vc[kt], DH, [[T, 128], [68, H], [1, 4]]), 0.0)
                            vi = vector.memset(bass.AP(vc[kt], DH, [[T, 128], [68, H]]), 1.0)
                        vi.then_inc(s["vst"], 1)
                        for h in range(H):
                            for qc in range(NKT):
                                idx = (ch * H + h) * NKT + qc
                                vector.wait_ge(s["mmP"], idx + 1)
                                vector.reciprocal(rec[:], psP[idx % 2].ap()[:, 64:65]).then_inc(s["rc"], 1)
                                vector.wait_ge(s["rc"], idx + 1)
                                vector.tensor_scalar(
                                    atk[qc][:, DH * h:DH * (h + 1)],
                                    psP[idx % 2].ap()[:, 0:DH],
                                    rec[:], None, OP.mult).then_inc(s["evp"], 1)
                        for fc in range(NI):
                            if ch * NI + fc >= 2:
                                vector.wait_ge(s["sto"], 16 * (ch * NI + fc - 1))
                            for qc in range(NKT):
                                tidx = (ch * NI + fc) * NKT + qc
                                vector.wait_ge(s["tp"], tidx + 1)
                                vector.tensor_copy(atf[fc % 2][:, 128 * qc:128 * (qc + 1)],
                                                   psT[tidx % 2].ap()[:, 0:128]).then_inc(s["eva"], 1)

                @blk.scalar
                def _(scalar):
                    for ch in range(NCH):
                        for h in range(H):
                            for kt in range(NKT):
                                sidx = (ch * H + h) * NKT + kt
                                scalar.wait_ge(s["mmS"], sidx + 1)
                                gt = ch * NKT + kt
                                scalar.activation(
                                    pt[kt][:], psS[sidx % 2].ap()[:, :], AF.Exp,
                                    scale=self.lam8t[gt][:, h:h + 1],
                                    bias=self.negB[:, 2 * h + ch:2 * h + ch + 1]).then_inc(s["ex"], 1)

                @blk.tensor
                def _(tensor):
                    for ch in range(NCH):
                        tensor.wait_ge(s["ld"], NLD * (ch + 1))
                        tensor.wait_ge(s["vst"], ch + 1)
                        for h in range(H):
                            oc, half = h // 2, h % 2
                            qsl = slice(64 * half, 64 * half + 64)
                            for kt in range(NKT):
                                sidx = (ch * H + h) * NKT + kt
                                if sidx >= 2:
                                    tensor.wait_ge(s["ex"], sidx - 1)
                                for h2 in range(2):
                                    mmi = tensor.matmul(
                                        psS[sidx % 2].ap()[:, 512 * h2:512 * (h2 + 1)],
                                        kfc[oc][qsl, 128 * kt:128 * (kt + 1)],
                                        qfc[oc][qsl, 512 * h2:512 * (h2 + 1)],
                                        start=True, stop=True)
                                mmi.then_inc(s["mmS"], 1)
                            tensor.wait_ge(s["ex"], (ch * H + h + 1) * NKT)
                            for qc in range(NKT):
                                idx = (ch * H + h) * NKT + qc
                                if idx >= 2:
                                    tensor.wait_ge(s["evp"], idx - 1)
                                for kt in range(NKT):
                                    mmi = tensor.matmul(
                                        psP[idx % 2].ap()[:, 0:66],
                                        pt[kt][:, 128 * qc:128 * (qc + 1)],
                                        bass.AP(vc[kt], 68 * h, [[T, 128], [1, 66]]),
                                        start=(kt == 0), stop=(kt == NKT - 1))
                                mmi.then_inc(s["mmP"], 1)
                        tensor.wait_ge(s["evp"], (ch + 1) * H * NKT)
                        for fc in range(NI):
                            for qc in range(NKT):
                                tidx = (ch * NI + fc) * NKT + qc
                                if tidx >= 2:
                                    tensor.wait_ge(s["eva"], tidx - 1)
                                tensor.transpose(psT[tidx % 2].ap()[:, 0:128],
                                                 atk[qc][:, 128 * fc:128 * (fc + 1)],
                                                 self.ident[:]).then_inc(s["tp"], 1)

    def p_l1_outproj(self):
        nc = self.nc
        atf = self.xt
        with contextlib.ExitStack() as st:
            s = self.sems(st, ["l0", "wld", "wfree", "mm", "ev", "sto"])
            wbuf = [self.sb(st, "wbuf2", [128, D]) for _ in range(2)]
            s1b = [self.sb(st, "s1b", [128, 512]) for _ in range(2)]
            psm = [self.ps(st, "pss1", [128, 512]) for _ in range(2)]
            with nc.Block() as blk:
                @blk.sync
                def _(sync):
                    for i in range(NI):
                        sync.dma_start(out=atf[i][:],
                                       in_=self.attnf_sp.ap()[128 * i:128 * (i + 1), :]).then_inc(s["l0"], 16)
                    for oc in range(NI):
                        if oc >= 2:
                            sync.wait_ge(s["ev"], 4 * (oc - 1))
                        for i in range(NI):
                            sync.dma_start(
                                out=wbuf[oc % 2][:, 128 * i:128 * (i + 1)],
                                in_=self.l1_woutT.ap()[128 * i:128 * (i + 1),
                                                       128 * oc:128 * (oc + 1)]).then_inc(s["wld"], 16)

                @blk.tensor
                def _(tensor):
                    tensor.wait_ge(s["l0"], 16 * NI)
                    for oc in range(NI):
                        tensor.wait_ge(s["wld"], 16 * 8 * (oc + 1))
                        for q in range(4):
                            idx = oc * 4 + q
                            if idx >= 2:
                                tensor.wait_ge(s["ev"], idx - 1)
                            for i in range(NI):
                                mmi = tensor.matmul(
                                    psm[idx % 2].ap()[:, :],
                                    wbuf[oc % 2][:, 128 * i:128 * (i + 1)],
                                    atf[i][:, 512 * q:512 * (q + 1)],
                                    start=(i == 0), stop=(i == NI - 1))
                            mmi.then_inc(s["mm"], 1)

                @blk.vector
                def _(vector):
                    for idx in range(NI * 4):
                        vector.wait_ge(s["mm"], idx + 1)
                        if idx >= 2:
                            vector.wait_ge(s["sto"], 16 * (idx - 1))
                        oc = idx // 4
                        vector.tensor_scalar(s1b[idx % 2][:], psm[idx % 2].ap()[:, :],
                                             self.bout_cols[:, oc:oc + 1], None,
                                             OP.add).then_inc(s["ev"], 1)

                @blk.gpsimd
                def _(gpsimd):
                    for idx in range(NI * 4):
                        oc, q = idx // 4, idx % 4
                        gpsimd.wait_ge(s["ev"], idx + 1)
                        gpsimd.dma_start(
                            out=self.s1_sp.ap()[128 * oc:128 * (oc + 1),
                                                512 * q:512 * (q + 1)],
                            in_=s1b[idx % 2][:]).then_inc(s["sto"], 16)

    def p_l1_ffn_final(self):
        nc = self.nc
        HC2 = 2
        NHG2 = (HID // 128) // HC2
        SW = 256
        TH = T // 2
        NS = TH // SW
        BF = mybir.dt.bfloat16
        dacc = self.xnt
        with contextlib.ExitStack() as st:
            s = self.sems(st, ["xl", "spl", "wld", "cv", "mmGU", "sl", "hh",
                               "mmD", "evD", "g1", "lds", "fx", "tp", "evy",
                               "outd"])
            xh = [self.sb(st, "xh", [128, TH], BF) for _ in range(NI)]
            xl = [self.sb(st, "xl", [128, TH], BF) for _ in range(NI)]
            xhi32 = self.sb(st, "xhi32", [128, TH])
            wi32 = self.sb(st, "wi32", [128, 2048], I32)
            wgB = self.sb(st, "wgB", [128, 2048], BF)
            wuB = self.sb(st, "wuB", [128, 2048], BF)
            wdB = self.sb(st, "wdB", [128, 2048], BF)
            gsil = self.sb(st, "gsil", [128, SW])
            hh = [self.sb(st, "hh", [128, SW], BF) for _ in range(HC2)]
            hl = [self.sb(st, "hl", [128, SW], BF) for _ in range(HC2)]
            hsc = self.sb(st, "hsc", [128, SW])
            psG = self.ps(st, "psGate", [128, 512])
            psU = self.ps(st, "psUp", [128, 512])
            psD = [self.ps(st, "psD", [128, 512]) for _ in range(4)]
            psY = [self.ps(st, "psY", [128, 512]) for _ in range(2)]
            WLDG = 16 * (8 + 8 + 2)

            with nc.Block() as blk:
                @blk.sync
                def _(sync):
                    for th in range(2):
                        csl = slice(TH * th, TH * (th + 1))
                        bsl = slice(CS * (1 - th), CS * (1 - th) + CS)
                        s1c = self.xt[0][:, bsl]
                        x2c = self.xt[1][:, bsl]
                        for i in range(NI):
                            if th == 1:
                                sync.wait_ge(s["mmGU"], th * NHG2 * NS * HC2)
                            sync.dma_start(out=xsc[:] if False else xhi32[:],
                                           in_=self.xn1spill.ap()[128 * i:128 * (i + 1), csl]).then_inc(s["xl"], 16)
                            sync.wait_ge(s["spl"], th * NI + i + 1)
                        for hg in range(NHG2):
                            g = th * NHG2 + hg
                            if g >= 1:
                                sync.wait_ge(s["cv"], 3 * g)
                            for i in range(NI):
                                sync.dma_start(
                                    out=bass.AP(wi32, 256 * i, [[2048, 128], [1, 256]]),
                                    in_=self.l1_wgT.ap()[128 * i:128 * (i + 1),
                                                         256 * hg:256 * (hg + 1)]).then_inc(s["wld"], 16)
                            sync.wait_ge(s["cv"], 3 * g + 1)
                            for i in range(NI):
                                sync.dma_start(
                                    out=bass.AP(wi32, 256 * i, [[2048, 128], [1, 256]]),
                                    in_=self.l1_wuT.ap()[128 * i:128 * (i + 1),
                                                         256 * hg:256 * (hg + 1)]).then_inc(s["wld"], 16)
                            sync.wait_ge(s["cv"], 3 * g + 2)
                            for j in range(HC2):
                                sync.dma_start(
                                    out=bass.AP(wi32, 1024 * j, [[2048, 128], [1, 1024]]),
                                    in_=self.l1_wdT.ap()[256 * hg + 128 * j:256 * hg + 128 * (j + 1),
                                                         :]).then_inc(s["wld"], 16)
                        for oc in range(NI):
                            j = th * NI + oc
                            if j >= 1:
                                sync.wait_ge(s["fx"], j)
                            sync.dma_start(out=s1c[:],
                                           in_=self.s1_sp.ap()[128 * oc:128 * (oc + 1), csl]).then_inc(s["lds"], 16)
                            sync.dma_start(out=x2c[:],
                                           in_=self.x2spill.ap()[128 * oc:128 * (oc + 1), csl]).then_inc(s["lds"], 16)

                @blk.vector
                def _(vector):
                    for th in range(2):
                        bsl = slice(CS * (1 - th), CS * (1 - th) + CS)
                        s1c = self.xt[0][:, bsl]
                        x2c = self.xt[1][:, bsl]
                        g1b = self.xt[2][:, bsl]
                        xfb = self.xt[3][:, bsl]
                        if th == 1:
                            vector.wait_ge(s["outd"], 16 * NI)
                        for i in range(NI):
                            vector.wait_ge(s["xl"], 16 * (th * NI + i + 1))
                            vector.tensor_copy(xh[i][:], xhi32[:])
                            vector.tensor_sub(xhi32[:], xhi32[:], xh[i][:])
                            vector.tensor_copy(xl[i][:], xhi32[:]).then_inc(s["spl"], 1)
                        for hg in range(NHG2):
                            g = th * NHG2 + hg
                            if g >= 1:
                                vector.wait_ge(s["mmD"], 2 * NS * g)
                            vector.wait_ge(s["wld"], WLDG * g + 16 * 8)
                            vector.tensor_copy(wgB[:], wi32[:]).then_inc(s["cv"], 1)
                            vector.wait_ge(s["wld"], WLDG * g + 16 * 16)
                            vector.tensor_copy(wuB[:], wi32[:]).then_inc(s["cv"], 1)
                            vector.wait_ge(s["wld"], WLDG * g + 16 * 18)
                            vector.tensor_copy(wdB[:], wi32[:]).then_inc(s["cv"], 1)
                            for sp in range(NS):
                                gsp = g * NS + sp
                                if gsp >= 1:
                                    vector.wait_ge(s["mmD"], 2 * gsp)
                                for hc in range(HC2):
                                    j = gsp * HC2 + hc
                                    vector.wait_ge(s["sl"], j + 1)
                                    vector.tensor_tensor(hsc[:], gsil[:],
                                                         psU.ap()[:, 0:SW], OP.mult)
                                    vector.tensor_copy(hh[hc][:], hsc[:])
                                    vector.tensor_sub(hsc[:], hsc[:], hh[hc][:])
                                    vector.tensor_copy(hl[hc][:], hsc[:]).then_inc(s["hh"], 1)
                                for g4 in range(2):
                                    didx = gsp * 2 + g4
                                    vector.wait_ge(s["mmD"], didx + 1)
                                    ssl = slice(TH * th + SW * sp, TH * th + SW * (sp + 1))
                                    for oc in range(4 * g4, 4 * g4 + 4):
                                        if hg == 0:
                                            vi = vector.tensor_copy(dacc[oc][:, ssl],
                                                                    psD[oc % 4].ap()[:, 0:SW])
                                        else:
                                            vi = vector.tensor_add(dacc[oc][:, ssl],
                                                                   dacc[oc][:, ssl],
                                                                   psD[oc % 4].ap()[:, 0:SW])
                                    vi.then_inc(s["evD"], 1)
                        for oc in range(NI):
                            j = th * NI + oc
                            vector.wait_ge(s["lds"], 32 * (j + 1))
                            vector.wait_ge(s["g1"], j + 1)
                            if j >= 1:
                                vector.wait_ge(s["tp"], j * (TH // 128))
                            vector.tensor_mul(xfb[:], s1c[:], g1b[:])
                            vector.tensor_add(xfb[:], xfb[:], x2c[:]).then_inc(s["fx"], 1)
                            for t in range(TH // 128):
                                tidx = j * (TH // 128) + t
                                vector.wait_ge(s["tp"], tidx + 1)
                                vector.tensor_copy(
                                    self.xt[t][:, CS * th + 128 * oc:CS * th + 128 * (oc + 1)],
                                    psY[tidx % 2].ap()[:, 0:128]).then_inc(s["evy"], 1)

                @blk.scalar
                def _(scalar):
                    for th in range(2):
                        bsl = slice(CS * (1 - th), CS * (1 - th) + CS)
                        g1b = self.xt[2][:, bsl]
                        for hg in range(NHG2):
                            g = th * NHG2 + hg
                            for sp in range(NS):
                                for hc in range(HC2):
                                    j = (g * NS + sp) * HC2 + hc
                                    scalar.wait_ge(s["mmGU"], j + 1)
                                    if j >= 1:
                                        scalar.wait_ge(s["hh"], j)
                                    scalar.activation(gsil[:], psG.ap()[:, 0:SW], AF.Silu,
                                                      scale=1.0 / 127.0).then_inc(s["sl"], 1)
                        for oc in range(NI):
                            j = th * NI + oc
                            scalar.wait_ge(s["evD"], 2 * NS * NHG2 * (th + 1))
                            if j >= 1:
                                scalar.wait_ge(s["fx"], j)
                            scalar.activation(g1b[:], dacc[oc][:, TH * th:TH * (th + 1)],
                                              AF.Silu,
                                              scale=1.0 / (127.0 * 127.0)).then_inc(s["g1"], 1)

                @blk.tensor
                def _(tensor):
                    for th in range(2):
                        bsl = slice(CS * (1 - th), CS * (1 - th) + CS)
                        xfb = self.xt[3][:, bsl]
                        tensor.wait_ge(s["spl"], th * NI + NI)
                        for hg in range(NHG2):
                            g = th * NHG2 + hg
                            for sp in range(NS):
                                gsp = g * NS + sp
                                ssl = slice(SW * sp, SW * (sp + 1))
                                for hc in range(HC2):
                                    j = gsp * HC2 + hc
                                    tensor.wait_ge(s["cv"], 3 * g + 2)
                                    if j >= 1:
                                        tensor.wait_ge(s["sl"], j)
                                        tensor.wait_ge(s["hh"], j)
                                    for i in range(NI):
                                        wsl = slice(256 * i + 128 * hc, 256 * i + 128 * (hc + 1))
                                        tensor.matmul(psG.ap()[:, 0:SW], wgB[:, wsl],
                                                      xh[i][:, ssl],
                                                      start=(i == 0), stop=False)
                                        tensor.matmul(psG.ap()[:, 0:SW], wgB[:, wsl],
                                                      xl[i][:, ssl],
                                                      start=False, stop=(i == NI - 1))
                                    for i in range(NI):
                                        wsl = slice(256 * i + 128 * hc, 256 * i + 128 * (hc + 1))
                                        tensor.matmul(psU.ap()[:, 0:SW], wuB[:, wsl],
                                                      xh[i][:, ssl],
                                                      start=(i == 0), stop=False)
                                        mmi = tensor.matmul(psU.ap()[:, 0:SW], wuB[:, wsl],
                                                            xl[i][:, ssl],
                                                            start=False, stop=(i == NI - 1))
                                    mmi.then_inc(s["mmGU"], 1)
                                tensor.wait_ge(s["cv"], 3 * g + 3)
                                tensor.wait_ge(s["hh"], gsp * HC2 + HC2)
                                for g4 in range(2):
                                    didx = gsp * 2 + g4
                                    if didx >= 1:
                                        tensor.wait_ge(s["evD"], didx)
                                    for oc in range(4 * g4, 4 * g4 + 4):
                                        for hc in range(HC2):
                                            wsl = slice(1024 * hc + 128 * oc, 1024 * hc + 128 * (oc + 1))
                                            tensor.matmul(psD[oc % 4].ap()[:, 0:SW],
                                                          wdB[:, wsl], hh[hc][:],
                                                          start=(hc == 0), stop=False)
                                            mmi = tensor.matmul(psD[oc % 4].ap()[:, 0:SW],
                                                                wdB[:, wsl], hl[hc][:],
                                                                start=False, stop=(hc == HC2 - 1))
                                    mmi.then_inc(s["mmD"], 1)
                        for oc in range(NI):
                            j = th * NI + oc
                            tensor.wait_ge(s["fx"], j + 1)
                            for t in range(TH // 128):
                                tidx = j * (TH // 128) + t
                                if tidx >= 2:
                                    tensor.wait_ge(s["evy"], tidx - 1)
                                tensor.transpose(psY[tidx % 2].ap()[:, 0:128],
                                                 xfb[:, 128 * t:128 * (t + 1)],
                                                 self.ident[:]).then_inc(s["tp"], 1)

                @blk.gpsimd
                def _(gpsimd):
                    for th in range(2):
                        for t in range(TH // 128):
                            gt = th * (TH // 128) + t
                            gpsimd.wait_ge(s["evy"], (th + 1) * NI * (TH // 128))
                            gpsimd.dma_start(
                                out=self.y_out.ap()[128 * gt:128 * (gt + 1), :],
                                in_=self.xt[t][:, CS * th:CS * th + D]).then_inc(s["outd"], 16)
                    gpsimd.wait_ge(s["outd"], 16 * NT)

    def p_taps(self):
        if not self.taps:
            return
        nc = self.nc
        with contextlib.ExitStack() as st:
            s = self.sems(st, ["sto"])
            n = [0]
            with nc.Block() as blk:
                @blk.gpsimd
                def _(gpsimd):
                    for name, (srcd, dst) in self.taps.items():
                        gpsimd.dma_start(out=dst.ap()[:, :], in_=srcd.ap()[:, :]).then_inc(s["sto"], 16)
                        n[0] += 16
                    gpsimd.wait_ge(s["sto"], n[0])


def build(tap_names=(), stop_after=None):
    return Builder(tap_names, stop_after=stop_after).build()


def _prep_shared(inputs):
    f = lambda k: np.asarray(inputs[k], dtype=np.float32)
    sh = {}
    sh["l0_wqkv"] = np.ascontiguousarray(
        np.concatenate([f("L0_attn_Wq"), f("L0_attn_Wk"), f("L0_attn_Wv")], axis=1))
    sh["l0_bqkv"] = np.concatenate(
        [f("L0_attn_bq"), f("L0_attn_bk"), f("L0_attn_bv")]).reshape(1, 3 * D)
    sh["l0_wo"] = np.ascontiguousarray(f("L0_attn_Wo"))
    sh["l0_bo"] = np.ascontiguousarray(f("L0_attn_bo").reshape(8, 128).T)
    sh["l0_m"] = f("L0_attn_m").reshape(1, H)
    G1R, G2R = f("L0_ffn_G1_R"), f("L0_ffn_G2_R")
    G1I, G2I = f("L0_ffn_G1_I"), f("L0_ffn_G2_I")

    def s_arr(G1):
        out = np.empty((4, FF, FF), dtype=np.float32)
        for c in range(4):
            g1c = G1[0, :, c, :]
            out[c] = np.broadcast_to(g1c.T[:, None, :, None],
                                     (32, 16, 32, 16)).reshape(FF, FF)
        return np.ascontiguousarray(out.reshape(4 * FF, FF))

    def t_arr(G2):
        out = np.empty((4, FF, FF), dtype=np.float32)
        for c in range(4):
            g2c = G2[c, :, 0, :]
            out[c] = np.broadcast_to(g2c.T[None, :, None, :],
                                     (32, 16, 32, 16)).reshape(FF, FF)
        return np.ascontiguousarray(out.reshape(4 * FF, FF))

    sh["l0_SR"], sh["l0_TR"] = s_arr(G1R), t_arr(G2R)
    sh["l0_SI"], sh["l0_TI"] = s_arr(G1I), t_arr(G2I)
    sh["l0_brbi"] = np.ascontiguousarray(
        np.concatenate([f("L0_ffn_bias_real"), f("L0_ffn_bias_imag")]).reshape(8, 128).T)
    sh["l1_wqkvT"] = np.ascontiguousarray(f("L1_attn_Wqkv").T)
    sh["l1_woutT"] = np.ascontiguousarray(f("L1_attn_Wout").T)
    sh["l1_bout"] = np.ascontiguousarray(f("L1_attn_bout").reshape(8, 128).T)
    sh["l1_lam1"] = np.ascontiguousarray(
        np.stack([f("L1_attn_Wl1")[:, 0], f("L1_attn_bl1")]))
    sh["l1_lam2T"] = np.ascontiguousarray(f("L1_attn_Wl2").T)
    sh["l1_lam2b"] = f("L1_attn_bl2").reshape(1, H)
    sh["l1_wgT"] = np.ascontiguousarray(np.asarray(inputs["L1_ffn_w_gate"]).T)
    sh["l1_wuT"] = np.ascontiguousarray(np.asarray(inputs["L1_ffn_w_up"]).T)
    sh["l1_wdT"] = np.ascontiguousarray(np.asarray(inputs["L1_ffn_w_down"]).T)
    sh["ident_in"] = np.eye(128, dtype=np.float32)
    return sh


def make_in_maps(inputs):
    shared = _prep_shared(inputs)
    maps = []
    for c in range(8):
        b, t0 = c // 4, (c % 4) * T
        d = dict(shared)
        x = np.asarray(inputs["L0_attn_x"][b, t0:t0 + T, :], dtype=np.float32)
        d["x_in"] = np.ascontiguousarray(x.T)
        d["posb_in"] = np.array([[float(t0)]], dtype=np.float32)
        maps.append(d)
    return maps


def kernel(**inputs):
    in_maps = make_in_maps(inputs)
    nc = build()
    res = run_bass_kernel_spmd(nc, in_maps, core_ids=list(range(8)))
    y = np.stack([np.asarray(res.results[c]["y_out"]) for c in range(8)])
    return np.ascontiguousarray(y.reshape(2, 4 * T, D)).astype(np.float32)


# version 07
# speedup vs baseline: 1.1121x; 1.0251x over previous
import contextlib
import numpy as np

import concourse.bass as bass
import concourse.mybir as mybir
from concourse.bass_utils import run_bass_kernel_spmd

F32 = mybir.dt.float32
I32 = mybir.dt.int32
AF = mybir.ActivationFunctionType
OP = mybir.AluOpType
AX = mybir.AxisListType

T = 2048
NT = T // 128
D = 1024
NI = D // 128
H = 16
DH = 64
EPS_LN = 1e-5
EPS_L2 = 1e-12
FF = 512
HID = 3072
HC = 3
NHG = (HID // 128) // HC
CS = 1024
NCH = T // CS
LOG_N = float(np.log(8192.0))
QSCALE = 0.125


QK_MODE = "full"


class Builder:
    def __init__(self, tap_names=(), stop_after=None):
        self.nc = bass.Bass("TRN2", target_bir_lowering=False,
                            detect_race_conditions=False)
        self.tap_names = set(tap_names)
        self.stop_after = stop_after
        self.stack = contextlib.ExitStack()
        self.substacks = []

    GP_NAMES = ("sto", "stoS", "sp", "ccl", "outd")

    def sems(self, st, names):
        out, i = {}, 0
        for n in names:
            if n in self.GP_NAMES:
                out[n] = self.stack.enter_context(self.nc.semaphore(self._uniq(n)))
            else:
                out[n] = self.pool[i]
                i += 1
        return out

    def clear_block(self):
        nc = self.nc
        with nc.Block() as blk:
            @blk.sync
            def _(sync):
                for sem in self.pool:
                    sync.sem_clear(sem)

    _n = 0

    def _uniq(self, n):
        Builder._n += 1
        return f"{n}_{Builder._n}"

    def sb(self, st, name, shape, dtype=F32):
        return st.enter_context(self.nc.sbuf_tensor(self._uniq(name), shape, dtype))

    def ps(self, st, name, shape):
        return st.enter_context(self.nc.psum_tensor(self._uniq(name), shape, F32))

    def build(self):
        nc = self.nc
        big = self.stack

        dt_ = nc.dram_tensor
        self.x_in = dt_("x_in", [D, T], F32, kind="ExternalInput")
        self.l0_wqkv = dt_("l0_wqkv", [D, 3 * D], F32, kind="ExternalInput")
        self.l0_bqkv = dt_("l0_bqkv", [1, 3 * D], F32, kind="ExternalInput")
        self.l0_wo = dt_("l0_wo", [D, D], F32, kind="ExternalInput")
        self.l0_bo = dt_("l0_bo", [128, 8], F32, kind="ExternalInput")
        self.l0_m = dt_("l0_m", [1, H], F32, kind="ExternalInput")
        self.l0_SR = dt_("l0_SR", [4 * FF, FF], F32, kind="ExternalInput")
        self.l0_TR = dt_("l0_TR", [4 * FF, FF], F32, kind="ExternalInput")
        self.l0_SI = dt_("l0_SI", [4 * FF, FF], F32, kind="ExternalInput")
        self.l0_TI = dt_("l0_TI", [4 * FF, FF], F32, kind="ExternalInput")
        self.l0_brbi = dt_("l0_brbi", [128, 8], F32, kind="ExternalInput")
        self.l1_wqkvT = dt_("l1_wqkvT", [D, 3 * D], F32, kind="ExternalInput")
        self.l1_woutT = dt_("l1_woutT", [D, D], F32, kind="ExternalInput")
        self.l1_bout = dt_("l1_bout", [128, 8], F32, kind="ExternalInput")
        self.l1_lam1 = dt_("l1_lam1", [2, 64], F32, kind="ExternalInput")
        self.l1_lam2T = dt_("l1_lam2T", [64, H], F32, kind="ExternalInput")
        self.l1_lam2b = dt_("l1_lam2b", [1, H], F32, kind="ExternalInput")
        self.l1_wgT = dt_("l1_wgT", [D, HID], I32, kind="ExternalInput")
        self.l1_wuT = dt_("l1_wuT", [D, HID], I32, kind="ExternalInput")
        self.l1_wdT = dt_("l1_wdT", [HID, D], I32, kind="ExternalInput")
        self.ident_in = dt_("ident_in", [128, 128], F32, kind="ExternalInput")
        self.posb_in = dt_("posb_in", [1, 1], F32, kind="ExternalInput")
        self.y_out = dt_("y_out", [T, D], F32, kind="ExternalOutput")

        self.xspill = dt_("xspill", [D, T], F32)
        self.q_tok = dt_("q_tok", [T, D], F32)
        self.k_tok = dt_("k_tok", [T, D], F32)
        self.v_tok = dt_("v_tok", [T, D], F32)
        self.qhat_f = dt_("qhat_f", [D, T], F32)
        self.s0_f = dt_("s0_f", [D, T], F32)
        self.s_f = dt_("s_f", [D, T], F32)
        self.x2spill = dt_("x2spill", [D, T], F32)
        self.xn1spill = dt_("xn1spill", [D, T], F32)
        self.qf_sp = dt_("qf_sp", [D, T], F32)
        self.kf_sp = dt_("kf_sp", [D, T], F32)
        self.vtok_sp = dt_("vtok_sp", [T, D], F32)
        self.attnf_sp = dt_("attnf_sp", [D, T], F32)
        self.s1_sp = dt_("s1_sp", [D, T], F32)
        self.cc_in = dt_("cc_in", [64, D], F32)
        self.cc_out = dt_("cc_out", [64, D], F32)
        self.xn0spill = dt_("xn0spill", [D, T], F32)

        self._tappable = {
            "xn0": (self.xn0spill, [D, T]), "q_tok": (self.q_tok, [T, D]),
            "k_tok": (self.k_tok, [T, D]), "v_tok": (self.v_tok, [T, D]),
            "qhat_f": (self.qhat_f, [D, T]), "kv": (self.cc_out, [64, D]),
            "s0_f": (self.s0_f, [D, T]), "s_f": (self.s_f, [D, T]),
            "x2": (self.x2spill, [D, T]), "xn1": (self.xn1spill, [D, T]),
            "qf": (self.qf_sp, [D, T]), "kf": (self.kf_sp, [D, T]),
            "vtok": (self.vtok_sp, [T, D]), "attnf": (self.attnf_sp, [D, T]),
            "s1": (self.s1_sp, [D, T]),
        }
        self.taps = {}
        for name in self.tap_names:
            src, shape = self._tappable[name]
            self.taps[name] = (src, dt_("tap_" + name, shape, F32,
                                        kind="ExternalOutput"))

        self.pool = [big.enter_context(nc.semaphore(f"sp{i}")) for i in range(26)]

        self.ident = self.sb(big, "ident", [128, 128])
        self.ones_col = self.sb(big, "ones_col", [128, 1])
        self.ones2 = self.sb(big, "ones2", [128, 2])
        self.ones_row = self.sb(big, "ones_row", [1, 512])
        self.ones_r1 = self.sb(big, "ones_r1", [1, 128])
        self.bo_cols = self.sb(big, "bo_cols", [128, 8])
        self.brbi_cols = self.sb(big, "brbi_cols", [128, 8])
        self.bout_cols = self.sb(big, "bout_cols", [128, 8])
        self.m_row = self.sb(big, "m_row", [1, H])
        self.posb = self.sb(big, "posb", [1, 1])
        self.xt = [self.sb(big, f"xt{i}", [128, T]) for i in range(NI)]
        self.normq = self.sb(big, "normq", [128, 256])
        self.normk = self.sb(big, "normk", [128, 256])
        self.lam8t = [self.sb(big, "lam8t", [128, H]) for _ in range(NT)]
        self.negB = self.sb(big, "negB", [128, 32])

        def stop(name):
            return self.stop_after is not None and name == self.stop_after

        done = False
        BF = mybir.dt.bfloat16
        self.p_load()
        if not stop("load"):
            self.xhl_stack = contextlib.ExitStack()
            self.substacks.append(self.xhl_stack)
            self.xh = [self.sb(self.xhl_stack, f"xh{i}", [128, T], BF) for i in range(NI)]
            self.xl = [self.sb(self.xhl_stack, f"xl{i}", [128, T], BF) for i in range(NI)]
            self.clear_block()
            self.p_ln("ln0", self.xt, self.xh, self.xl,
                      spill=self.xn0spill if "xn0" in self.tap_names else None)
        else:
            done = True
        if not done and not stop("ln0"):
            self.clear_block()
            self.p_l0_qkv()
        else:
            done = True
        if not done and not stop("qkv"):
            self.clear_block()
            self.p_l0_kv()
        else:
            done = True
        if not done and not stop("kv"):
            self.clear_block()
            self.p_l0_cc_qhat()
        else:
            done = True
        if not done and not stop("cc"):
            self.clear_block()
            self.p_l0_s0()
            self.kv_stack.close()
        else:
            done = True
        if not done and not stop("s0"):
            self.clear_block()
            self.p_wtbuild()
            self.clear_block()
            self.p_l0_oproj_ffn()
            self.wt_stack.close()
            self.xhl_stack.close()
        else:
            done = True
        if not done and not stop("oproj"):
            self.xhl_stack = contextlib.ExitStack()
            self.substacks.append(self.xhl_stack)
            self.xh = [self.sb(self.xhl_stack, f"x1h{i}", [128, T], BF) for i in range(NI)]
            self.xl = [self.sb(self.xhl_stack, f"x1l{i}", [128, T], BF) for i in range(NI)]
            self.clear_block()
            self.p_ln_reload("ln1", self.x2spill, self.xt, self.xh, self.xl,
                             spill=self.xn1spill)
        else:
            done = True
        if not done and not stop("ln1"):
            self.clear_block()
            self.p_l1_qk()
        else:
            done = True
        if not done and not stop("qk"):
            self.clear_block()
            self.p_l1_lam_bound()
        else:
            done = True
        if not done and not stop("lam"):
            self.clear_block()
            self.p_l1_v()
            self.xhl_stack.close()
        else:
            done = True
        if not done and not stop("v"):
            self.clear_block()
            self.p_l1_attn()
        else:
            done = True
        if not done and not stop("attn"):
            self.clear_block()
            self.p_l1_outproj()
        else:
            done = True
        if not done and not stop("outproj"):
            self.clear_block()
            self.p_l1_ffn_final()
        else:
            done = True
        self.clear_block()
        self.p_taps()

        for stk in reversed(self.substacks):
            stk.close()
        self.stack.close()
        return self.nc

    def p_load(self):
        nc = self.nc
        with contextlib.ExitStack() as st:
            s = self.sems(st, ["ld", "sto"])
            n = [0]
            with nc.Block() as blk:
                @blk.sync
                def _(sync):
                    for i in range(NI):
                        sync.dma_start(out=self.xt[i][:],
                                       in_=self.x_in.ap()[128 * i:128 * (i + 1), :]).then_inc(s["ld"], 16)
                        n[0] += 16
                    for dst, src in [(self.ident, self.ident_in),
                                     (self.bo_cols, self.l0_bo),
                                     (self.brbi_cols, self.l0_brbi),
                                     (self.bout_cols, self.l1_bout),
                                     (self.m_row, self.l0_m),
                                     (self.posb, self.posb_in)]:
                        sync.dma_start(out=dst[:], in_=src.ap()[:, :]).then_inc(s["ld"], 16)
                        n[0] += 16

                @blk.gpsimd
                def _(gpsimd):
                    gpsimd.dma_start(out=self.xspill.ap()[:, :],
                                     in_=self.x_in.ap()[:, :]).then_inc(s["sto"], 16)

                @blk.vector
                def _(vector):
                    vector.memset(self.ones_col[:], 1.0)
                    vector.memset(self.ones2[:], 1.0)
                    vector.memset(self.ones_row[:], 1.0)
                    vector.memset(self.ones_r1[:], 1.0)
                    vector.wait_ge(s["ld"], n[0])

    def _emit_ln(self, tag, src_tiles, xh_tiles, xl_tiles, spill):
        nc = self.nc
        with contextlib.ExitStack() as st:
            s = self.sems(st, ["sq", "mm", "ev", "bc", "sqr", "xns", "sto"])
            with contextlib.ExitStack() as st_stats:
                ps_sum = self.ps(st_stats, "ps_sum", [1, T])
                ps_sumsq = self.ps(st_stats, "ps_sumsq", [1, T])
                xsq = [self.sb(st, "xsq", [128, T]) for _ in range(2)]
                mu = self.sb(st, "mu", [1, T])
                rstd = self.sb(st, "rstd", [1, T])
                tmp1 = self.sb(st, "tmp1", [1, T])
                tmp2 = self.sb(st, "tmp2", [1, T])
                xnf = [self.sb(st, "xnf", [128, T]) for _ in range(2)]

                with nc.Block() as blk:
                    @blk.scalar
                    def _(scalar):
                        for i in range(NI):
                            if i >= 2:
                                scalar.wait_ge(s["mm"], i - 1)
                            scalar.activation(xsq[i % 2][:], src_tiles[i][:],
                                              AF.Square).then_inc(s["sq"], 1)
                        scalar.wait_ge(s["ev"], 1)
                        scalar.activation(tmp2[:], tmp1[:], AF.Sqrt).then_inc(s["sqr"], 1)

                    @blk.tensor
                    def _(tensor):
                        for i in range(NI):
                            for q in range(4):
                                sl = slice(512 * q, 512 * (q + 1))
                                tensor.matmul(ps_sum.ap()[:, sl], self.ones_col[:],
                                              src_tiles[i][:, sl],
                                              start=(i == 0), stop=(i == NI - 1))
                            tensor.wait_ge(s["sq"], i + 1)
                            for q in range(4):
                                sl = slice(512 * q, 512 * (q + 1))
                                mmi = tensor.matmul(ps_sumsq.ap()[:, sl], self.ones_col[:],
                                                    xsq[i % 2][:, sl],
                                                    start=(i == 0), stop=(i == NI - 1))
                            mmi.then_inc(s["mm"], 1)

                    @blk.vector
                    def _(vector):
                        vector.wait_ge(s["mm"], NI)
                        vector.tensor_scalar(mu[:], ps_sum.ap()[:, :], 1.0 / D, None, OP.mult)
                        vector.tensor_scalar(tmp1[:], ps_sumsq.ap()[:, :], 1.0 / D, None, OP.mult)
                        vector.tensor_mul(tmp2[:], mu[:], mu[:])
                        vector.tensor_sub(tmp1[:], tmp1[:], tmp2[:])
                        vector.tensor_scalar(tmp1[:], tmp1[:], EPS_LN, None,
                                             OP.add).then_inc(s["ev"], 1)
                        vector.wait_ge(s["sqr"], 1)
                        vector.reciprocal(rstd[:], tmp2[:]).then_inc(s["ev"], 1)

            ps_mu = self.ps(st, "ps_mu", [128, T])
            ps_rs = self.ps(st, "ps_rs", [128, T])
            with nc.Block() as blk:
                @blk.tensor
                def _(tensor):
                    for q in range(4):
                        sl = slice(512 * q, 512 * (q + 1))
                        tensor.matmul(ps_mu.ap()[:, sl], self.ones_r1[:],
                                      mu[:, sl], start=True, stop=True)
                        mmi = tensor.matmul(ps_rs.ap()[:, sl], self.ones_r1[:],
                                            rstd[:, sl], start=True, stop=True)
                    mmi.then_inc(s["bc"], 1)

                @blk.vector
                def _(vector):
                    vector.wait_ge(s["bc"], 1)
                    for i in range(NI):
                        if spill is not None and i >= 2:
                            vector.wait_ge(s["sto"], 16 * (i - 1))
                        for q in range(4):
                            sl = slice(512 * q, 512 * (q + 1))
                            vector.tensor_sub(xnf[i % 2][:, sl], src_tiles[i][:, sl],
                                              ps_mu.ap()[:, sl])
                            vector.tensor_mul(xnf[i % 2][:, sl], xnf[i % 2][:, sl],
                                              ps_rs.ap()[:, sl])
                        vector.tensor_copy(xh_tiles[i][:], xnf[i % 2][:])
                        vector.tensor_sub(xsq[i % 2][:], xnf[i % 2][:], xh_tiles[i][:])
                        vector.tensor_copy(xl_tiles[i][:], xsq[i % 2][:]).then_inc(s["xns"], 1)

                @blk.gpsimd
                def _(gpsimd):
                    if spill is not None:
                        for i in range(NI):
                            gpsimd.wait_ge(s["xns"], i + 1)
                            gpsimd.dma_start(out=spill.ap()[128 * i:128 * (i + 1), :],
                                             in_=xnf[i % 2][:]).then_inc(s["sto"], 16)

    def p_ln(self, tag, src, xh, xl, spill=None):
        self._emit_ln(tag, src, xh, xl, spill)

    def p_ln_reload(self, tag, spill_in, src_tiles, xh, xl, spill=None):
        nc = self.nc
        with contextlib.ExitStack() as st:
            s = self.sems(st, ["ld"])
            with nc.Block() as blk:
                @blk.sync
                def _(sync):
                    for i in range(NI):
                        sync.dma_start(out=src_tiles[i][:],
                                       in_=spill_in.ap()[128 * i:128 * (i + 1), :]).then_inc(s["ld"], 16)

                @blk.vector
                def _(vector):
                    vector.wait_ge(s["ld"], 16 * NI)
        self.clear_block()
        self._emit_ln(tag, src_tiles, xh, xl, spill)

    def p_wtbuild(self):
        nc = self.nc
        BF = mybir.dt.bfloat16
        self.wt_stack = contextlib.ExitStack()
        self.substacks.append(self.wt_stack)
        big = self.wt_stack
        self.wrTh = [self.sb(big, "wrTh", [128, FF], BF) for _ in range(4)]
        self.wrTl = [self.sb(big, "wrTl", [128, FF], BF) for _ in range(4)]
        self.wiTh = [self.sb(big, "wiTh", [128, FF], BF) for _ in range(4)]
        self.wiTl = [self.sb(big, "wiTl", [128, FF], BF) for _ in range(4)]
        self.wiTnh = [self.sb(big, "wiTnh", [128, FF], BF) for _ in range(4)]
        self.wiTnl = [self.sb(big, "wiTnl", [128, FF], BF) for _ in range(4)]
        with contextlib.ExitStack() as st:
            s = self.sems(st, ["ld", "used"])
            sbuf = [self.sb(st, "stbuf", [128, FF]) for _ in range(4)]
            tmp = self.sb(st, "wtmp", [128, FF])
            acc = self.sb(st, "wacc", [128, FF])
            jobs = [(self.l0_SR, self.l0_TR, self.wrTh, self.wrTl, None, None),
                    (self.l0_SI, self.l0_TI, self.wiTh, self.wiTl,
                     self.wiTnh, self.wiTnl)]
            seq = []
            for job in jobs:
                for r in range(4):
                    for c in range(4):
                        seq.append(job + (r, c))
            with nc.Block() as blk:
                @blk.sync
                def _(sync):
                    for j, (Sd, Td, dh, dl, dnh, dnl, r, c) in enumerate(seq):
                        if j >= 2:
                            sync.wait_ge(s["used"], j - 1)
                        row = 512 * c + 128 * r
                        sync.dma_start(out=sbuf[2 * (j % 2)][:],
                                       in_=Sd.ap()[row:row + 128, :]).then_inc(s["ld"], 16)
                        sync.dma_start(out=sbuf[2 * (j % 2) + 1][:],
                                       in_=Td.ap()[row:row + 128, :]).then_inc(s["ld"], 16)

                @blk.vector
                def _(vector):
                    for j, (Sd, Td, dh, dl, dnh, dnl, r, c) in enumerate(seq):
                        vector.wait_ge(s["ld"], 32 * (j + 1))
                        a, b = sbuf[2 * (j % 2)], sbuf[2 * (j % 2) + 1]
                        if c == 0:
                            vector.tensor_mul(acc[:], a[:], b[:]).then_inc(s["used"], 1)
                        else:
                            vector.tensor_mul(tmp[:], a[:], b[:]).then_inc(s["used"], 1)
                            vector.tensor_add(acc[:], acc[:], tmp[:])
                        if c == 3:
                            vector.tensor_copy(dh[r][:], acc[:])
                            vector.tensor_sub(acc[:], acc[:], dh[r][:])
                            vector.tensor_copy(dl[r][:], acc[:])
                            if dnh is not None:
                                vector.tensor_scalar(dnh[r][:], dh[r][:], -1.0, None, OP.mult)
                                vector.tensor_scalar(dnl[r][:], dl[r][:], -1.0, None, OP.mult)

    def p_l0_qkv(self):
        nc = self.nc
        BF = mybir.dt.bfloat16
        NOB = 6
        dests = [(self.q_tok, 0), (self.q_tok, 512), (self.k_tok, 0),
                 (self.k_tok, 512), (self.v_tok, 0), (self.v_tok, 512)]
        with contextlib.ExitStack() as st:
            s = self.sems(st, ["wld", "wsp", "mm", "ev", "bq", "bev", "sto"])
            wf32 = self.sb(st, "wf32", [128, 2048])
            wh = [self.sb(st, "wh", [128, 8 * 512], BF) for _ in range(2)]
            wl = [self.sb(st, "wl", [128, 8 * 512], BF) for _ in range(2)]
            obuf = [self.sb(st, "obuf", [128, 512]) for _ in range(2)]
            qkvb = self.sb(st, "qkvb", [128, 3 * D])
            bqrow = self.sb(st, "bqrow", [1, 3 * D])
            psm = [self.ps(st, "psqkv", [128, 512]) for _ in range(2)]
            psb = self.ps(st, "psbias", [128, 512])

            with nc.Block() as blk:
                @blk.sync
                def _(sync):
                    sync.dma_start(out=bqrow[:], in_=self.l0_bqkv.ap()[:, :]).then_inc(s["bq"], 16)
                    for ob in range(NOB):
                        for r in range(2):
                            j = 2 * ob + r
                            if j >= 1:
                                sync.wait_ge(s["wsp"], j)
                            for i2 in range(4):
                                i = 4 * r + i2
                                sync.dma_start(
                                    out=wf32[:, 512 * i2:512 * (i2 + 1)],
                                    in_=self.l0_wqkv.ap()[128 * i:128 * (i + 1),
                                                          512 * ob:512 * (ob + 1)]).then_inc(s["wld"], 16)

                @blk.vector
                def _(vector):
                    for b6 in range(6):
                        vector.wait_ge(s["mm"], b6 + 1)
                        vector.tensor_copy(qkvb[:, 512 * b6:512 * (b6 + 1)],
                                           psb.ap()[:, :]).then_inc(s["bev"], 1)
                    for ob in range(NOB):
                        for r in range(2):
                            j = 2 * ob + r
                            vector.wait_ge(s["wld"], 64 * (j + 1))
                            if ob >= 2:
                                vector.wait_ge(s["ev"], NT * (ob - 1))
                            csl = slice(2048 * r, 2048 * (r + 1))
                            vector.tensor_copy(wh[ob % 2][:, csl], wf32[:])
                            vector.tensor_sub(wf32[:], wf32[:], wh[ob % 2][:, csl])
                            vector.tensor_copy(wl[ob % 2][:, csl], wf32[:]).then_inc(s["wsp"], 1)
                        for t in range(NT):
                            idx = ob * NT + t
                            vector.wait_ge(s["mm"], 6 + idx + 1)
                            if idx >= 2:
                                vector.wait_ge(s["sto"], 16 * (idx - 1))
                            vector.tensor_tensor(
                                obuf[idx % 2][:], psm[idx % 2].ap()[:, :],
                                qkvb[:, 512 * ob:512 * (ob + 1)],
                                OP.add).then_inc(s["ev"], 1)

                @blk.tensor
                def _(tensor):
                    tensor.wait_ge(s["bq"], 16)
                    for b6 in range(6):
                        if b6 >= 1:
                            tensor.wait_ge(s["bev"], b6)
                        tensor.matmul(psb.ap()[:, :], self.ones_r1[:],
                                      bqrow[:, 512 * b6:512 * (b6 + 1)],
                                      start=True, stop=True).then_inc(s["mm"], 1)
                    for ob in range(NOB):
                        tensor.wait_ge(s["wsp"], 2 * (ob + 1))
                        for t in range(NT):
                            idx = ob * NT + t
                            if idx >= 2:
                                tensor.wait_ge(s["ev"], idx - 1)
                            tsl = slice(128 * t, 128 * (t + 1))
                            for i in range(NI):
                                wsl = slice(512 * i, 512 * (i + 1))
                                tensor.matmul(psm[idx % 2].ap()[:, :],
                                              self.xh[i][:, tsl], wh[ob % 2][:, wsl],
                                              start=(i == 0), stop=False)
                                tensor.matmul(psm[idx % 2].ap()[:, :],
                                              self.xh[i][:, tsl], wl[ob % 2][:, wsl],
                                              start=False, stop=False)
                                mmi = tensor.matmul(psm[idx % 2].ap()[:, :],
                                                    self.xl[i][:, tsl], wh[ob % 2][:, wsl],
                                                    start=False, stop=(i == NI - 1))
                            mmi.then_inc(s["mm"], 1)

                @blk.gpsimd
                def _(gpsimd):
                    for ob in range(NOB):
                        dstd, coff = dests[ob]
                        for t in range(NT):
                            idx = ob * NT + t
                            gpsimd.wait_ge(s["ev"], idx + 1)
                            gpsimd.dma_start(
                                out=bass.AP(dstd, 128 * t * D + coff,
                                            [[D, 128], [1, 512]]),
                                in_=obuf[idx % 2][:]).then_inc(s["sto"], 16)

    def p_l0_kv(self):
        nc = self.nc
        self.kv_stack = contextlib.ExitStack()
        self.substacks.append(self.kv_stack)
        big = self.kv_stack
        self.srecip = self.sb(big, "srecip", [1, H])
        with contextlib.ExitStack() as st:
            s = self.sems(st, ["ld", "sq", "vr", "sr", "kh", "mm", "ev", "sg", "rc",
                               "sto"])
            kt = [self.sb(st, "ktile", [128, D]) for _ in range(2)]
            vt = [self.sb(st, "vtile", [128, D]) for _ in range(2)]
            ksq = [self.sb(st, "ksq", [128, D]) for _ in range(2)]
            khat = [self.sb(st, "khat", [128, D]) for _ in range(2)]
            ssq = [self.sb(st, "ssq", [128, H]) for _ in range(2)]
            nrm = [self.sb(st, "nrm", [128, H]) for _ in range(2)]
            kvpart = self.sb(st, "kvpart", [64, D])
            sgr = self.sb(st, "sgr", [1, H])
            sclr = self.sb(st, "sclr", [1, H])
            pskv = self.ps(st, "pskv", [64, D])

            with nc.Block() as blk:
                @blk.sync
                def _(sync):
                    for t in range(NT):
                        if t >= 2:
                            sync.wait_ge(s["mm"], t - 1)
                        sync.dma_start(out=kt[t % 2][:],
                                       in_=self.k_tok.ap()[128 * t:128 * (t + 1), :]).then_inc(s["ld"], 16)
                        sync.dma_start(out=vt[t % 2][:],
                                       in_=self.v_tok.ap()[128 * t:128 * (t + 1), :]).then_inc(s["ld"], 16)

                @blk.scalar
                def _(scalar):
                    scalar.activation(sgr[:], self.m_row[:], AF.Sigmoid)
                    scalar.activation(sclr[:], sgr[:], AF.Exp, scale=LOG_N).then_inc(s["sg"], 1)
                    for t in range(NT):
                        scalar.wait_ge(s["ld"], 32 * (t + 1))
                        if t >= 2:
                            scalar.wait_ge(s["vr"], t - 1)
                        scalar.activation(ksq[t % 2][:], kt[t % 2][:],
                                          AF.Square).then_inc(s["sq"], 1)
                        scalar.wait_ge(s["vr"], t + 1)
                        scalar.activation(nrm[t % 2][:], ssq[t % 2][:],
                                          AF.Sqrt).then_inc(s["sr"], 1)

                @blk.vector
                def _(vector):
                    vector.wait_ge(s["sg"], 1)
                    vector.reciprocal(self.srecip[:], sclr[:])
                    for t in range(NT):
                        vector.wait_ge(s["sq"], t + 1)
                        v3 = bass.AP(ksq[t % 2], 0, [[D, 128], [DH, H], [1, DH]])
                        vector.tensor_reduce(ssq[t % 2][:], v3, AX.X,
                                             OP.add).then_inc(s["vr"], 1)
                        vector.wait_ge(s["sr"], t + 1)
                        vector.tensor_scalar_max(nrm[t % 2][:], nrm[t % 2][:], EPS_L2)
                        vector.reciprocal(nrm[t % 2][:], nrm[t % 2][:]).then_inc(s["rc"], 1)
                        vector.wait_ge(s["rc"], t + 1)
                        if t >= 2:
                            vector.wait_ge(s["mm"], t - 1)
                        for h in range(H):
                            vi = vector.tensor_scalar(
                                khat[t % 2][:, DH * h:DH * (h + 1)],
                                kt[t % 2][:, DH * h:DH * (h + 1)],
                                nrm[t % 2][:, h:h + 1], None, OP.mult)
                        vi.then_inc(s["kh"], 1)
                    vector.wait_ge(s["mm"], NT)
                    vector.tensor_copy(kvpart[:], pskv.ap()[:, :]).then_inc(s["ev"], 1)

                @blk.tensor
                def _(tensor):
                    for t in range(NT):
                        tensor.wait_ge(s["kh"], t + 1)
                        for h in range(H):
                            mmi = tensor.matmul(
                                pskv.ap()[:, DH * h:DH * (h + 1)],
                                khat[t % 2][:, DH * h:DH * (h + 1)],
                                vt[t % 2][:, DH * h:DH * (h + 1)],
                                start=(t == 0 and h % 8 == 0),
                                stop=(t == NT - 1 and h % 8 == 7),
                                skip_group_check=True)
                        mmi.then_inc(s["mm"], 1)

                @blk.gpsimd
                def _(gpsimd):
                    gpsimd.wait_ge(s["ev"], 1)
                    gpsimd.dma_start(out=self.cc_in.ap()[:, :],
                                     in_=kvpart[:]).then_inc(s["sto"], 16)
                    gpsimd.wait_ge(s["sto"], 16)

    def p_l0_cc_qhat(self):
        nc = self.nc
        big = self.kv_stack
        self.kvbar = self.sb(big, "kvbar", [64, D])
        with contextlib.ExitStack() as st:
            s = self.sems(st, ["cc", "ccl", "bc", "bev", "ld", "sq", "vr", "sr",
                              "qh", "tp", "ev", "rc", "sto"])
            qt = [self.sb(st, "qtile", [128, D]) for _ in range(2)]
            qsq = [self.sb(st, "qsq", [128, D]) for _ in range(2)]
            qhat = [self.sb(st, "qhat", [128, D]) for _ in range(2)]
            ssq = [self.sb(st, "ssq2", [128, H]) for _ in range(2)]
            nrm = [self.sb(st, "nrm2", [128, H]) for _ in range(2)]
            qtr = [self.sb(st, "qtr", [64, T]) for _ in range(2)]
            kvld = self.sb(st, "kvld", [64, D])
            srp = self.sb(st, "srp", [64, H])
            pssr = self.ps(st, "pssr", [64, 512])
            pstr = [self.ps(st, "pstr", [64, 512]) for _ in range(4)]

            with nc.Block() as blk:
                @blk.gpsimd
                def _(gpsimd):
                    gpsimd.collective_compute(
                        "AllReduce", OP.add,
                        ins=[self.cc_in.ap()[:, :]],
                        outs=[self.cc_out.ap()[:, :]],
                        replica_groups=[[0, 1, 2, 3], [4, 5, 6, 7]],
                    ).then_inc(s["cc"], 1)
                    gpsimd.wait_ge(s["cc"], 1)
                    gpsimd.dma_start(out=kvld[:], in_=self.cc_out.ap()[:, :]).then_inc(s["ccl"], 16)
                    for t in range(NT):
                        gpsimd.wait_ge(s["ev"], (t + 1) * H)
                        gpsimd.dma_start(
                            out=bass.AP(self.qhat_f, 128 * t,
                                        [[T, 64], [64 * T, H], [1, 128]]),
                            in_=bass.AP(qtr[t % 2], 0, [[T, 64], [128, H], [1, 128]]),
                        ).then_inc(s["sto"], 16)

                @blk.tensor
                def _(tensor):
                    tensor.matmul(pssr.ap()[:, 0:H], self.ones_r1[:, 0:64],
                                  self.srecip[:], start=True, stop=True).then_inc(s["bc"], 1)
                    for t in range(NT):
                        tensor.wait_ge(s["qh"], t + 1)
                        for h in range(H):
                            j = (t * H + h) % 4
                            if t * H + h >= 4:
                                tensor.wait_ge(s["ev"], t * H + h - 3)
                            tensor.transpose(pstr[j].ap()[:, 0:128],
                                             qhat[t % 2][:, DH * h:DH * (h + 1)],
                                             self.ident[:]).then_inc(s["tp"], 1)

                @blk.sync
                def _(sync):
                    for t in range(NT):
                        if t >= 2:
                            sync.wait_ge(s["tp"], (t - 1) * H)
                        sync.dma_start(out=qt[t % 2][:],
                                       in_=self.q_tok.ap()[128 * t:128 * (t + 1), :]).then_inc(s["ld"], 16)

                @blk.scalar
                def _(scalar):
                    for t in range(NT):
                        scalar.wait_ge(s["ld"], 16 * (t + 1))
                        if t >= 2:
                            scalar.wait_ge(s["vr"], t - 1)
                        scalar.activation(qsq[t % 2][:], qt[t % 2][:],
                                          AF.Square).then_inc(s["sq"], 1)
                        scalar.wait_ge(s["vr"], t + 1)
                        scalar.activation(nrm[t % 2][:], ssq[t % 2][:],
                                          AF.Sqrt).then_inc(s["sr"], 1)

                @blk.vector
                def _(vector):
                    vector.wait_ge(s["bc"], 1)
                    vector.tensor_copy(srp[:], pssr.ap()[:, 0:H]).then_inc(s["bev"], 1)
                    vector.wait_ge(s["bev"], 1)
                    vector.wait_ge(s["ccl"], 16)
                    for h in range(H):
                        vector.tensor_scalar(self.kvbar[:, DH * h:DH * (h + 1)],
                                             kvld[:, DH * h:DH * (h + 1)],
                                             srp[:, h:h + 1], None, OP.mult)
                    for t in range(NT):
                        vector.wait_ge(s["sq"], t + 1)
                        v3 = bass.AP(qsq[t % 2], 0, [[D, 128], [DH, H], [1, DH]])
                        vector.tensor_reduce(ssq[t % 2][:], v3, AX.X,
                                             OP.add).then_inc(s["vr"], 1)
                        vector.wait_ge(s["sr"], t + 1)
                        vector.tensor_scalar_max(nrm[t % 2][:], nrm[t % 2][:], EPS_L2)
                        vector.reciprocal(nrm[t % 2][:], nrm[t % 2][:]).then_inc(s["rc"], 1)
                        vector.wait_ge(s["rc"], t + 1)
                        if t >= 2:
                            vector.wait_ge(s["tp"], (t - 1) * H)
                        for h in range(H):
                            vi = vector.tensor_scalar(
                                qhat[t % 2][:, DH * h:DH * (h + 1)],
                                qt[t % 2][:, DH * h:DH * (h + 1)],
                                nrm[t % 2][:, h:h + 1], None, OP.mult)
                        vi.then_inc(s["qh"], 1)
                        if t >= 2:
                            vector.wait_ge(s["sto"], 16 * (t - 1))
                        for h in range(H):
                            vector.wait_ge(s["tp"], t * H + h + 1)
                            vector.tensor_copy(qtr[t % 2][:, 128 * h:128 * (h + 1)],
                                               pstr[(t * H + h) % 4].ap()[:, 0:128]).then_inc(s["ev"], 1)

    def p_l0_s0(self):
        nc = self.nc
        with contextlib.ExitStack() as st:
            s = self.sems(st, ["ld", "mm", "ev", "sto"])
            qhf = [self.sb(st, "qhf", [64, T]) for _ in range(2)]
            s0b = [self.sb(st, "s0b", [64, T]) for _ in range(2)]
            psb = [self.ps(st, "pss0", [64, 512]) for _ in range(2)]
            with nc.Block() as blk:
                @blk.sync
                def _(sync):
                    for h in range(H):
                        if h >= 2:
                            sync.wait_ge(s["mm"], 4 * (h - 1))
                        sync.dma_start(out=qhf[h % 2][:],
                                       in_=self.qhat_f.ap()[64 * h:64 * (h + 1), :]).then_inc(s["ld"], 16)

                @blk.tensor
                def _(tensor):
                    for h in range(H):
                        tensor.wait_ge(s["ld"], 16 * (h + 1))
                        for q in range(4):
                            idx = h * 4 + q
                            if idx >= 2:
                                tensor.wait_ge(s["ev"], idx - 1)
                            tensor.matmul(psb[idx % 2].ap()[:, :],
                                          self.kvbar[:, DH * h:DH * (h + 1)],
                                          qhf[h % 2][:, 512 * q:512 * (q + 1)],
                                          start=True, stop=True).then_inc(s["mm"], 1)

                @blk.vector
                def _(vector):
                    for h in range(H):
                        if h >= 2:
                            vector.wait_ge(s["sto"], 16 * (h - 1))
                        for q in range(4):
                            idx = h * 4 + q
                            vector.wait_ge(s["mm"], idx + 1)
                            vector.tensor_copy(s0b[h % 2][:, 512 * q:512 * (q + 1)],
                                               psb[idx % 2].ap()[:, :]).then_inc(s["ev"], 1)

                @blk.gpsimd
                def _(gpsimd):
                    for h in range(H):
                        gpsimd.wait_ge(s["ev"], 4 * (h + 1))
                        gpsimd.dma_start(out=self.s0_f.ap()[64 * h:64 * (h + 1), :],
                                         in_=s0b[h % 2][:]).then_inc(s["sto"], 16)

    def p_l0_oproj_ffn(self):
        nc = self.nc
        s0t = self.xt
        with contextlib.ExitStack() as st:
            s = self.sems(st, ["l0", "wld", "wfree", "mmS", "evS", "mmG", "gv",
                              "ldx", "ev2", "sto", "stoS"])
            wo_buf = [self.sb(st, "wo_buf", [128, D]) for _ in range(2)]
            schunk = self.sb(st, "schunk", [128, T])
            gchunk = self.sb(st, "gchunk", [128, T])
            xold = [self.sb(st, "xold", [128, T]) for _ in range(2)]
            x2b = [self.sb(st, "x2b", [128, 512]) for _ in range(2)]
            psS = [self.ps(st, "psS", [128, 512]) for _ in range(2)]
            psG = [self.ps(st, "psG", [128, 512]) for _ in range(2)]

            with nc.Block() as blk:
                @blk.sync
                def _(sync):
                    for i in range(NI):
                        sync.dma_start(out=s0t[i][:],
                                       in_=self.s0_f.ap()[128 * i:128 * (i + 1), :]).then_inc(s["l0"], 16)
                    for oc in range(NI):
                        if oc >= 2:
                            sync.wait_ge(s["evS"], 4 * (oc - 1))
                        for i in range(NI):
                            sync.dma_start(
                                out=wo_buf[oc % 2][:, 128 * i:128 * (i + 1)],
                                in_=self.l0_wo.ap()[128 * i:128 * (i + 1),
                                                    128 * oc:128 * (oc + 1)]).then_inc(s["wld"], 16)
                        if oc >= 2:
                            sync.wait_ge(s["ev2"], 4 * (oc - 1))
                        sync.dma_start(out=xold[oc % 2][:],
                                       in_=self.xspill.ap()[128 * oc:128 * (oc + 1), :]).then_inc(s["ldx"], 16)

                @blk.tensor
                def _(tensor):
                    tensor.wait_ge(s["l0"], 16 * NI)
                    for oc in range(NI):
                        tensor.wait_ge(s["wld"], 16 * 8 * (oc + 1))
                        for q in range(4):
                            idx = oc * 4 + q
                            if idx >= 2:
                                tensor.wait_ge(s["evS"], idx - 1)
                            for i in range(NI):
                                mmi = tensor.matmul(
                                    psS[idx % 2].ap()[:, :],
                                    wo_buf[oc % 2][:, 128 * i:128 * (i + 1)],
                                    s0t[i][:, 512 * q:512 * (q + 1)],
                                    start=(i == 0), stop=(i == NI - 1))
                            mmi.then_inc(s["mmS"], 1)
                            if idx >= 2:
                                tensor.wait_ge(s["gv"], idx - 1)
                            oc2 = oc % 4
                            if oc < 4:
                                pairs = [(self.wrTh[ic], self.wrTl[ic], ic) for ic in range(4)] + \
                                        [(self.wiTnh[ic], self.wiTnl[ic], 4 + ic) for ic in range(4)]
                            else:
                                pairs = [(self.wiTh[ic], self.wiTl[ic], ic) for ic in range(4)] + \
                                        [(self.wrTh[ic], self.wrTl[ic], 4 + ic) for ic in range(4)]
                            wsl = slice(128 * oc2, 128 * (oc2 + 1))
                            ssl = slice(512 * q, 512 * (q + 1))
                            for jj, (wth, wtl, xi) in enumerate(pairs):
                                tensor.matmul(psG[idx % 2].ap()[:, :], wth[:, wsl],
                                              self.xh[xi][:, ssl],
                                              start=(jj == 0), stop=False)
                                tensor.matmul(psG[idx % 2].ap()[:, :], wtl[:, wsl],
                                              self.xh[xi][:, ssl],
                                              start=False, stop=False)
                                mmi = tensor.matmul(psG[idx % 2].ap()[:, :], wth[:, wsl],
                                                    self.xl[xi][:, ssl],
                                                    start=False, stop=(jj == 7))
                            mmi.then_inc(s["mmG"], 1)

                @blk.scalar
                def _(scalar):
                    for oc in range(NI):
                        for q in range(4):
                            idx = oc * 4 + q
                            scalar.wait_ge(s["mmG"], idx + 1)
                            if idx >= 1:
                                scalar.wait_ge(s["ev2"], idx)
                            scalar.activation(gchunk[:, 512 * q:512 * (q + 1)],
                                              psG[idx % 2].ap()[:, :], AF.Silu,
                                              bias=self.brbi_cols[:, oc:oc + 1]).then_inc(s["gv"], 1)

                @blk.vector
                def _(vector):
                    for oc in range(NI):
                        vector.wait_ge(s["ldx"], 16 * (oc + 1))
                        if oc >= 1:
                            vector.wait_ge(s["stoS"], 16 * oc)
                        for q in range(4):
                            idx = oc * 4 + q
                            sl = slice(512 * q, 512 * (q + 1))
                            vector.wait_ge(s["mmS"], idx + 1)
                            vector.tensor_scalar(schunk[:, sl], psS[idx % 2].ap()[:, :],
                                                 self.bo_cols[:, oc:oc + 1], None,
                                                 OP.add).then_inc(s["evS"], 1)
                            vector.wait_ge(s["gv"], idx + 1)
                            if idx >= 2:
                                vector.wait_ge(s["sto"], 16 * (idx - 1))
                            vector.tensor_mul(x2b[idx % 2][:],
                                              schunk[:, sl], gchunk[:, sl])
                            vector.tensor_add(x2b[idx % 2][:],
                                              x2b[idx % 2][:],
                                              xold[oc % 2][:, sl]).then_inc(s["ev2"], 1)

                @blk.gpsimd
                def _(gpsimd):
                    for oc in range(NI):
                        for q in range(4):
                            idx = oc * 4 + q
                            gpsimd.wait_ge(s["ev2"], idx + 1)
                            gpsimd.dma_start(
                                out=self.x2spill.ap()[128 * oc:128 * (oc + 1),
                                                      512 * q:512 * (q + 1)],
                                in_=x2b[idx % 2][:]).then_inc(s["sto"], 16)
                        gpsimd.wait_ge(s["evS"], 4 * (oc + 1))
                        gpsimd.dma_start(out=self.s_f.ap()[128 * oc:128 * (oc + 1), :],
                                         in_=schunk[:]).then_inc(s["stoS"], 16)

    def p_l1_qk(self):
        nc = self.nc
        with contextlib.ExitStack() as st:
            s = self.sems(st, ["wld", "wsp", "mm", "sq", "nm", "ev", "sto"])
            BF = mybir.dt.bfloat16
            wf32 = self.sb(st, "wqkf32", [128, D])
            whB = [self.sb(st, "whB", [128, D], BF) for _ in range(2)]
            wlB = [self.sb(st, "wlB", [128, D], BF) for _ in range(2)]
            qkb = [self.sb(st, "qkb", [128, 512]) for _ in range(2)]
            nbounce = self.sb(st, "nbounce", [128, 512])
            sqb = [self.sb(st, "sqb", [128, 512]) for _ in range(2)]
            psm = [self.ps(st, "psqk", [128, 512]) for _ in range(2)]
            psnq = self.ps(st, "psnq", [128, 512])
            psnk = self.ps(st, "psnk", [128, 512])

            with nc.Block() as blk:
                @blk.sync
                def _(sync):
                    for oc in range(16):
                        if oc >= 1:
                            sync.wait_ge(s["wsp"], oc)
                        for i in range(NI):
                            sync.dma_start(
                                out=wf32[:, 128 * i:128 * (i + 1)],
                                in_=self.l1_wqkvT.ap()[128 * i:128 * (i + 1),
                                                       128 * oc:128 * (oc + 1)]).then_inc(s["wld"], 16)

                @blk.tensor
                def _(tensor):
                    for oc in range(16):
                        tensor.wait_ge(s["wsp"], oc + 1)
                        for q in range(4):
                            idx = oc * 4 + q
                            if idx >= 2:
                                tensor.wait_ge(s["ev"], idx - 1)
                                tensor.wait_ge(s["nm"], idx - 1)
                            ssl = slice(512 * q, 512 * (q + 1))
                            for i in range(NI):
                                wsl = slice(128 * i, 128 * (i + 1))
                                tensor.matmul(psm[idx % 2].ap()[:, :],
                                              whB[oc % 2][:, wsl],
                                              self.xh[i][:, ssl],
                                              start=(i == 0), stop=False)
                                tensor.matmul(psm[idx % 2].ap()[:, :],
                                              whB[oc % 2][:, wsl],
                                              self.xl[i][:, ssl],
                                              start=False, stop=False)
                                mmi = tensor.matmul(psm[idx % 2].ap()[:, :],
                                                    wlB[oc % 2][:, wsl],
                                                    self.xh[i][:, ssl],
                                                    start=False, stop=(i == NI - 1))
                            mmi.then_inc(s["mm"], 1)
                            if QK_MODE == "nonorm":
                                tensor.nop().then_inc(s["nm"], 1)
                                continue
                            tensor.wait_ge(s["sq"], idx + 1)
                            pn = psnq if oc < 8 else psnk
                            for half in range(2):
                                hcol = (2 * (oc % 8) + half) * 16
                                for m in range(4):
                                    col = hcol + q * 4 + m
                                    mmi = tensor.matmul(
                                        pn.ap()[:, 2 * col:2 * col + 2],
                                        sqb[idx % 2][64 * half:64 * half + 64,
                                                     128 * m:128 * (m + 1)],
                                        self.ones2[64 * half:64 * half + 64, :],
                                        start=True, stop=True)
                            mmi.then_inc(s["nm"], 1)

                @blk.scalar
                def _(scalar):
                    if QK_MODE == "nonorm":
                        pass
                    else:
                        for oc in range(16):
                            for q in range(4):
                                idx = oc * 4 + q
                                scalar.wait_ge(s["mm"], idx + 1)
                                if idx >= 2:
                                    scalar.wait_ge(s["nm"], idx - 1)
                                    scalar.wait_ge(s["ev"], idx - 1)
                                scalar.activation(sqb[idx % 2][:], psm[idx % 2].ap()[:, :],
                                                  AF.Square).then_inc(s["sq"], 1)

                @blk.vector
                def _(vector):
                    for oc in range(16):
                        vector.wait_ge(s["wld"], 16 * 8 * (oc + 1))
                        if oc >= 2:
                            vector.wait_ge(s["nm"], 4 * (oc - 1))
                        vector.tensor_copy(whB[oc % 2][:], wf32[:])
                        vector.tensor_sub(wf32[:], wf32[:], whB[oc % 2][:])
                        vector.tensor_copy(wlB[oc % 2][:], wf32[:]).then_inc(s["wsp"], 1)
                        for q in range(4):
                            idx = oc * 4 + q
                            vector.wait_ge(s["sq"], idx + 1)
                            if idx >= 2:
                                vector.wait_ge(s["sto"], 16 * (idx - 1))
                            vector.tensor_copy(qkb[idx % 2][:],
                                               psm[idx % 2].ap()[:, :]).then_inc(s["ev"], 1)
                    vector.wait_ge(s["nm"], 64)
                    vector.tensor_copy(nbounce[:], psnq.ap()[:, 0:512])
                    vector.tensor_copy(self.normq[:],
                                       bass.AP(nbounce, 0, [[512, 128], [2, 256]]))
                    vector.tensor_copy(nbounce[:], psnk.ap()[:, 0:512])
                    vector.tensor_copy(self.normk[:],
                                       bass.AP(nbounce, 0, [[512, 128], [2, 256]]))

                @blk.gpsimd
                def _(gpsimd):
                    for oc in range(16):
                        dstd = self.qf_sp if oc < 8 else self.kf_sp
                        for q in range(4):
                            idx = oc * 4 + q
                            gpsimd.wait_ge(s["ev"], idx + 1)
                            gpsimd.dma_start(
                                out=dstd.ap()[128 * (oc % 8):128 * (oc % 8 + 1),
                                              512 * q:512 * (q + 1)],
                                in_=qkb[idx % 2][:]).then_inc(s["sto"], 16)

    def p_l1_lam_bound(self):
        nc = self.nc
        with contextlib.ExitStack() as st:
            s = self.sems(st, ["io", "ln", "mm1", "rl", "mm2", "sgm", "l8",
                              "tp", "ev", "nb", "ld"])
            posrow = self.sb(st, "posrow", [1, T])
            logrow = self.sb(st, "logrow", [1, T])
            lamrhs = self.sb(st, "lamrhs", [2, T])
            h1 = self.sb(st, "h1", [64, T])
            lamf = self.sb(st, "lamf", [H, T])
            lam8f = self.sb(st, "lam8f", [H, T])
            lam1t = self.sb(st, "lam1t", [2, 64])
            lam2Tt = self.sb(st, "lam2Tt", [64, H])
            lam2bt = self.sb(st, "lam2bt", [1, H])
            mqcol = self.sb(st, "mqcol", [128, 2])
            mkcol = self.sb(st, "mkcol", [128, 2])
            mq2 = self.sb(st, "mq2", [1, 32])
            mk2 = self.sb(st, "mk2", [1, 32])
            brow = self.sb(st, "brow", [1, 32])
            ps1 = [self.ps(st, "pslam", [64, 512]) for _ in range(2)]
            ps2 = [self.ps(st, "pslam2", [H, 512]) for _ in range(2)]
            pst = [self.ps(st, "pslt", [128, 512]) for _ in range(2)]
            psr = self.ps(st, "pslr", [1, 128])
            psnb = self.ps(st, "psnb", [128, 32])

            with nc.Block() as blk:
                @blk.sync
                def _(sync):
                    sync.dma_start(out=lam1t[:], in_=self.l1_lam1.ap()[:, :]).then_inc(s["ld"], 16)
                    sync.dma_start(out=lam2Tt[:], in_=self.l1_lam2T.ap()[:, :]).then_inc(s["ld"], 16)
                    sync.dma_start(out=lam2bt[:], in_=self.l1_lam2b.ap()[:, :]).then_inc(s["ld"], 16)

                @blk.gpsimd
                def _(gpsimd):
                    gpsimd.iota(posrow[:], [[1, T]], channel_multiplier=0,
                                allow_small_or_imprecise_dtypes=True).then_inc(s["io"], 1)

                @blk.vector
                def _(vector):
                    vector.wait_ge(s["io"], 1)
                    vector.tensor_scalar(posrow[:], posrow[:], self.posb[:], 1.0,
                                         OP.add, OP.add).then_inc(s["io"], 1)
                    vector.wait_ge(s["ln"], 1)
                    vector.memset(lamrhs[:, :], 1.0)
                    vector.tensor_copy(lamrhs[0:1, :], logrow[:]).then_inc(s["io"], 1)
                    vector.wait_ge(s["sgm"], 4)
                    vector.tensor_scalar(lam8f[:], lamf[:], QSCALE, None,
                                         OP.mult).then_inc(s["l8"], 1)
                    for t in range(NT):
                        vector.wait_ge(s["tp"], t + 1)
                        vector.tensor_copy(self.lam8t[t][:], pst[t % 2].ap()[:, 0:H]).then_inc(s["ev"], 1)
                    for j, (src, dst) in enumerate([(self.normq, mqcol), (self.normk, mkcol)]):
                        for half in range(2):
                            vector.wait_ge(s["tp"], NT + 2 * j + half + 1)
                            vector.reduce_max(dst[:, half:half + 1],
                                              pst[(2 * j + half) % 2].ap()[:, 0:128],
                                              AX.X).then_inc(s["ev"], 1)
                    for j, (colt, dst) in enumerate([(mqcol, mq2), (mkcol, mk2)]):
                        for half in range(2):
                            vector.wait_ge(s["tp"], NT + 4 + 2 * j + half + 1)
                            v4 = bass.AP(psr, 0, [[128, 1], [16, 8], [8, 2], [1, 8]])
                            vector.reduce_max(dst[:, 16 * half:16 * (half + 1)],
                                              v4, AX.X).then_inc(s["ev"], 1)
                    vector.tensor_mul(brow[:], mq2[:], mk2[:]).then_inc(s["nb"], 1)
                    vector.wait_ge(s["nb"], 2)
                    vector.tensor_scalar(brow[:], brow[:], -QSCALE, None,
                                         OP.mult).then_inc(s["nb"], 1)
                    vector.wait_ge(s["nb"], 4)
                    vector.tensor_copy(self.negB[:], psnb.ap()[:, :])

                @blk.scalar
                def _(scalar):
                    scalar.wait_ge(s["io"], 2)
                    scalar.activation(logrow[:], posrow[:], AF.Ln).then_inc(s["ln"], 1)
                    for q in range(4):
                        scalar.wait_ge(s["mm1"], q + 1)
                        scalar.activation(h1[:, 512 * q:512 * (q + 1)],
                                          ps1[q % 2].ap()[:, :], AF.Relu).then_inc(s["rl"], 1)
                    for q in range(4):
                        scalar.wait_ge(s["mm2"], q + 1)
                        scalar.activation(lamf[:, 512 * q:512 * (q + 1)],
                                          ps2[q % 2].ap()[:, :], AF.Sigmoid).then_inc(s["sgm"], 1)
                    scalar.wait_ge(s["nb"], 1)
                    scalar.activation(brow[:], brow[:], AF.Sqrt).then_inc(s["nb"], 1)

                @blk.tensor
                def _(tensor):
                    tensor.wait_ge(s["io"], 3)
                    tensor.wait_ge(s["ld"], 48)
                    for q in range(4):
                        if q >= 2:
                            tensor.wait_ge(s["rl"], q - 1)
                        tensor.matmul(ps1[q % 2].ap()[:, :], lam1t[:],
                                      lamrhs[:, 512 * q:512 * (q + 1)],
                                      start=True, stop=True).then_inc(s["mm1"], 1)
                    for q in range(4):
                        if q >= 2:
                            tensor.wait_ge(s["sgm"], q - 1)
                        tensor.wait_ge(s["rl"], q + 1)
                        tensor.matmul(ps2[q % 2].ap()[:, :], lam2Tt[:],
                                      h1[:, 512 * q:512 * (q + 1)],
                                      start=True, stop=False)
                        tensor.matmul(ps2[q % 2].ap()[:, :], lam2bt[:],
                                      self.ones_row[:, 0:512],
                                      start=False, stop=True).then_inc(s["mm2"], 1)
                    tensor.wait_ge(s["l8"], 1)
                    for t in range(NT):
                        if t >= 2:
                            tensor.wait_ge(s["ev"], t - 1)
                        tensor.transpose(pst[t % 2].ap()[:, 0:H],
                                         lam8f[:, 128 * t:128 * (t + 1)],
                                         self.ident[0:H, 0:H]).then_inc(s["tp"], 1)
                    for j, src in enumerate([self.normq, self.normk]):
                        for half in range(2):
                            k = 2 * j + half
                            tensor.wait_ge(s["ev"], NT - 1 + k)
                            tensor.transpose(pst[k % 2].ap()[:, 0:128],
                                             src[:, 128 * half:128 * (half + 1)],
                                             self.ident[:]).then_inc(s["tp"], 1)
                    for j, colt in enumerate([mqcol, mkcol]):
                        for half in range(2):
                            k = 2 * j + half
                            tensor.wait_ge(s["ev"], NT + 2 + k)
                            tensor.transpose(psr.ap()[:, :], colt[:, half:half + 1],
                                             self.ident[:]).then_inc(s["tp"], 1)
                            tensor.wait_ge(s["ev"], NT + 4 + k + 1)
                    tensor.wait_ge(s["nb"], 3)
                    tensor.matmul(psnb.ap()[:, :], self.ones_r1[:], brow[:],
                                  start=True, stop=True).then_inc(s["nb"], 1)

    def p_l1_v(self):
        nc = self.nc
        with contextlib.ExitStack() as st:
            BF = mybir.dt.bfloat16
            s = self.sems(st, ["wld", "wsp", "mm", "ev", "sto"])
            wvf = self.sb(st, "wvf", [128, D])
            wvh = self.sb(st, "wvh", [128, 8 * D], BF)
            wvl = self.sb(st, "wvl", [128, 8 * D], BF)
            vb = [self.sb(st, "vb", [128, D]) for _ in range(2)]
            psm = [self.ps(st, "psv", [128, D]) for _ in range(2)]
            with nc.Block() as blk:
                @blk.sync
                def _(sync):
                    for i in range(NI):
                        if i >= 1:
                            sync.wait_ge(s["wsp"], i)
                        sync.dma_start(
                            out=wvf[:, :],
                            in_=self.l1_wqkvT.ap()[128 * i:128 * (i + 1), 2 * D:3 * D]).then_inc(s["wld"], 16)

                @blk.tensor
                def _(tensor):
                    tensor.wait_ge(s["wsp"], NI)
                    for t in range(NT):
                        if t >= 2:
                            tensor.wait_ge(s["ev"], t - 1)
                        tsl = slice(128 * t, 128 * (t + 1))
                        for i in range(NI):
                            for half in range(2):
                                psl = slice(512 * half, 512 * (half + 1))
                                wsl = slice(D * i + 512 * half, D * i + 512 * (half + 1))
                                tensor.matmul(psm[t % 2].ap()[:, psl],
                                              self.xh[i][:, tsl], wvh[:, wsl],
                                              start=(i == 0), stop=False)
                                tensor.matmul(psm[t % 2].ap()[:, psl],
                                              self.xh[i][:, tsl], wvl[:, wsl],
                                              start=False, stop=False)
                                mmi = tensor.matmul(psm[t % 2].ap()[:, psl],
                                                    self.xl[i][:, tsl], wvh[:, wsl],
                                                    start=False, stop=(i == NI - 1))
                        mmi.then_inc(s["mm"], 1)

                @blk.vector
                def _(vector):
                    for i in range(NI):
                        vector.wait_ge(s["wld"], 16 * (i + 1))
                        csl = slice(D * i, D * (i + 1))
                        vector.tensor_copy(wvh[:, csl], wvf[:])
                        vector.tensor_sub(wvf[:], wvf[:], wvh[:, csl])
                        vector.tensor_copy(wvl[:, csl], wvf[:]).then_inc(s["wsp"], 1)
                    for t in range(NT):
                        vector.wait_ge(s["mm"], t + 1)
                        if t >= 2:
                            vector.wait_ge(s["sto"], 16 * (t - 1))
                        vector.tensor_copy(vb[t % 2][:], psm[t % 2].ap()[:, :]).then_inc(s["ev"], 1)

                @blk.gpsimd
                def _(gpsimd):
                    for t in range(NT):
                        gpsimd.wait_ge(s["ev"], t + 1)
                        gpsimd.dma_start(out=self.vtok_sp.ap()[128 * t:128 * (t + 1), :],
                                         in_=vb[t % 2][:]).then_inc(s["sto"], 16)

    def p_l1_attn(self):
        nc = self.nc
        NKT = CS // 128
        with contextlib.ExitStack() as st:
            s = self.sems(st, ["sp", "ld", "vst", "mmS", "ex", "mmP", "evp",
                               "tp", "eva", "rc", "sto"])
            qfc = [self.xt[i][:, 0:CS] for i in range(NI)]
            kfc = [self.xt[i][:, CS:2 * CS] for i in range(NI)]
            vc = [self.sb(st, "vc", [128, 17 * DH]) for kt in range(NKT)]
            pt = [self.sb(st, "ptile", [128, CS]) for _ in range(NKT)]
            atk = [self.sb(st, "atk", [128, D]) for _ in range(NKT)]
            atf = [self.sb(st, "atf", [128, CS]) for _ in range(2)]
            rec = self.sb(st, "recip", [128, 1])
            psS = [self.ps(st, "psSc", [128, CS]) for _ in range(2)]
            psP = [self.ps(st, "psPV", [128, 512]) for _ in range(2)]
            psT = [self.ps(st, "psTr", [128, 512]) for _ in range(2)]
            NLD = 16 * (2 * NI + NKT)

            with nc.Block() as blk:
                @blk.gpsimd
                def _(gpsimd):
                    for ch in range(NCH):
                        for fc in range(NI):
                            gpsimd.wait_ge(s["eva"], ch * NI * NKT + (fc + 1) * NKT)
                            gpsimd.dma_start(
                                out=self.attnf_sp.ap()[128 * fc:128 * (fc + 1),
                                                       CS * ch:CS * (ch + 1)],
                                in_=atf[fc % 2][:]).then_inc(s["sto"], 16)

                @blk.sync
                def _(sync):
                    for ch in range(NCH):
                        if ch >= 1:
                            sync.wait_ge(s["mmP"], ch * H * NKT)
                        for i in range(NI):
                            sync.dma_start(out=qfc[i],
                                           in_=self.qf_sp.ap()[128 * i:128 * (i + 1),
                                                               CS * ch:CS * (ch + 1)]).then_inc(s["ld"], 16)
                            sync.dma_start(out=kfc[i],
                                           in_=self.kf_sp.ap()[128 * i:128 * (i + 1),
                                                               CS * ch:CS * (ch + 1)]).then_inc(s["ld"], 16)
                        for kt in range(NKT):
                            sync.dma_start(
                                out=bass.AP(vc[kt], 0, [[17 * DH, 128], [68, H], [1, DH]]),
                                in_=bass.AP(self.vtok_sp, (CS * ch + 128 * kt) * D,
                                            [[D, 128], [DH, H], [1, DH]])).then_inc(s["ld"], 16)

                @blk.vector
                def _(vector):
                    for ch in range(NCH):
                        vector.wait_ge(s["ld"], NLD * (ch + 1))
                        for kt in range(NKT):
                            vector.memset(bass.AP(vc[kt], DH, [[17 * DH, 128], [68, H], [1, 4]]), 0.0)
                            vi = vector.memset(bass.AP(vc[kt], DH, [[17 * DH, 128], [68, H]]), 1.0)
                        vi.then_inc(s["vst"], 1)
                        for h in range(H):
                            for qc in range(NKT):
                                idx = (ch * H + h) * NKT + qc
                                vector.wait_ge(s["mmP"], idx + 1)
                                vector.reciprocal(rec[:], psP[idx % 2].ap()[:, 64:65]).then_inc(s["rc"], 1)
                                vector.wait_ge(s["rc"], idx + 1)
                                vector.tensor_scalar(
                                    atk[qc][:, DH * h:DH * (h + 1)],
                                    psP[idx % 2].ap()[:, 0:DH],
                                    rec[:], None, OP.mult).then_inc(s["evp"], 1)
                        for fc in range(NI):
                            if ch * NI + fc >= 2:
                                vector.wait_ge(s["sto"], 16 * (ch * NI + fc - 1))
                            for qc in range(NKT):
                                tidx = (ch * NI + fc) * NKT + qc
                                vector.wait_ge(s["tp"], tidx + 1)
                                vector.tensor_copy(atf[fc % 2][:, 128 * qc:128 * (qc + 1)],
                                                   psT[tidx % 2].ap()[:, 0:128]).then_inc(s["eva"], 1)

                @blk.scalar
                def _(scalar):
                    for ch in range(NCH):
                        for h in range(H):
                            for kt in range(NKT):
                                sidx = (ch * H + h) * NKT + kt
                                scalar.wait_ge(s["mmS"], sidx + 1)
                                gt = ch * NKT + kt
                                scalar.activation(
                                    pt[kt][:], psS[sidx % 2].ap()[:, :], AF.Exp,
                                    scale=self.lam8t[gt][:, h:h + 1],
                                    bias=self.negB[:, 2 * h + ch:2 * h + ch + 1]).then_inc(s["ex"], 1)

                @blk.tensor
                def _(tensor):
                    for ch in range(NCH):
                        tensor.wait_ge(s["ld"], NLD * (ch + 1))
                        tensor.wait_ge(s["vst"], ch + 1)
                        for h in range(H):
                            oc, half = h // 2, h % 2
                            qsl = slice(64 * half, 64 * half + 64)
                            for kt in range(NKT):
                                sidx = (ch * H + h) * NKT + kt
                                if sidx >= 2:
                                    tensor.wait_ge(s["ex"], sidx - 1)
                                for h2 in range(2):
                                    mmi = tensor.matmul(
                                        psS[sidx % 2].ap()[:, 512 * h2:512 * (h2 + 1)],
                                        kfc[oc][qsl, 128 * kt:128 * (kt + 1)],
                                        qfc[oc][qsl, 512 * h2:512 * (h2 + 1)],
                                        start=True, stop=True)
                                mmi.then_inc(s["mmS"], 1)
                            tensor.wait_ge(s["ex"], (ch * H + h + 1) * NKT)
                            for qc in range(NKT):
                                idx = (ch * H + h) * NKT + qc
                                if idx >= 2:
                                    tensor.wait_ge(s["evp"], idx - 1)
                                for kt in range(NKT):
                                    mmi = tensor.matmul(
                                        psP[idx % 2].ap()[:, 0:66],
                                        pt[kt][:, 128 * qc:128 * (qc + 1)],
                                        bass.AP(vc[kt], 68 * h, [[17 * DH, 128], [1, 66]]),
                                        start=(kt == 0), stop=(kt == NKT - 1))
                                mmi.then_inc(s["mmP"], 1)
                        tensor.wait_ge(s["evp"], (ch + 1) * H * NKT)
                        for fc in range(NI):
                            for qc in range(NKT):
                                tidx = (ch * NI + fc) * NKT + qc
                                if tidx >= 2:
                                    tensor.wait_ge(s["eva"], tidx - 1)
                                tensor.transpose(psT[tidx % 2].ap()[:, 0:128],
                                                 atk[qc][:, 128 * fc:128 * (fc + 1)],
                                                 self.ident[:]).then_inc(s["tp"], 1)

    def p_l1_outproj(self):
        nc = self.nc
        atf = self.xt
        with contextlib.ExitStack() as st:
            s = self.sems(st, ["l0", "wld", "wfree", "mm", "ev", "sto"])
            wbuf = [self.sb(st, "wbuf2", [128, D]) for _ in range(2)]
            s1b = [self.sb(st, "s1b", [128, 512]) for _ in range(2)]
            psm = [self.ps(st, "pss1", [128, 512]) for _ in range(2)]
            with nc.Block() as blk:
                @blk.sync
                def _(sync):
                    for i in range(NI):
                        sync.dma_start(out=atf[i][:],
                                       in_=self.attnf_sp.ap()[128 * i:128 * (i + 1), :]).then_inc(s["l0"], 16)
                    for oc in range(NI):
                        if oc >= 2:
                            sync.wait_ge(s["ev"], 4 * (oc - 1))
                        for i in range(NI):
                            sync.dma_start(
                                out=wbuf[oc % 2][:, 128 * i:128 * (i + 1)],
                                in_=self.l1_woutT.ap()[128 * i:128 * (i + 1),
                                                       128 * oc:128 * (oc + 1)]).then_inc(s["wld"], 16)

                @blk.tensor
                def _(tensor):
                    tensor.wait_ge(s["l0"], 16 * NI)
                    for oc in range(NI):
                        tensor.wait_ge(s["wld"], 16 * 8 * (oc + 1))
                        for q in range(4):
                            idx = oc * 4 + q
                            if idx >= 2:
                                tensor.wait_ge(s["ev"], idx - 1)
                            for i in range(NI):
                                mmi = tensor.matmul(
                                    psm[idx % 2].ap()[:, :],
                                    wbuf[oc % 2][:, 128 * i:128 * (i + 1)],
                                    atf[i][:, 512 * q:512 * (q + 1)],
                                    start=(i == 0), stop=(i == NI - 1))
                            mmi.then_inc(s["mm"], 1)

                @blk.vector
                def _(vector):
                    for idx in range(NI * 4):
                        vector.wait_ge(s["mm"], idx + 1)
                        if idx >= 2:
                            vector.wait_ge(s["sto"], 16 * (idx - 1))
                        oc = idx // 4
                        vector.tensor_scalar(s1b[idx % 2][:], psm[idx % 2].ap()[:, :],
                                             self.bout_cols[:, oc:oc + 1], None,
                                             OP.add).then_inc(s["ev"], 1)

                @blk.gpsimd
                def _(gpsimd):
                    for idx in range(NI * 4):
                        oc, q = idx // 4, idx % 4
                        gpsimd.wait_ge(s["ev"], idx + 1)
                        gpsimd.dma_start(
                            out=self.s1_sp.ap()[128 * oc:128 * (oc + 1),
                                                512 * q:512 * (q + 1)],
                            in_=s1b[idx % 2][:]).then_inc(s["sto"], 16)

    def p_l1_ffn_final(self):
        nc = self.nc
        HC2 = 2
        NHG2 = (HID // 128) // HC2
        SW = 256
        TH = T // 2
        NS = TH // SW
        BF = mybir.dt.bfloat16

        with contextlib.ExitStack() as st:
            s = self.sems(st, ["xl", "spl", "wld", "mmGU", "sl", "hh",
                               "mmD", "evD", "g1", "lds", "fx", "tp", "evy",
                               "outd", "sto"])
            dacc = [self.sb(st, "dacc", [128, T]) for _ in range(NI)]
            xh = [self.sb(st, "fxh", [128, TH], BF) for _ in range(NI)]
            xl = [self.sb(st, "fxl", [128, TH], BF) for _ in range(NI)]
            xhi32 = self.sb(st, "xhi32", [128, TH])
            wgB = self.sb(st, "wgB", [128, 2048], BF)
            wuB = self.sb(st, "wuB", [128, 2048], BF)
            wdB = self.sb(st, "wdB", [128, 2048], BF)
            gsil = self.sb(st, "gsil", [128, SW])
            hh = [self.sb(st, "hh", [128, SW], BF) for _ in range(HC2)]
            hl = [self.sb(st, "hl", [128, SW], BF) for _ in range(HC2)]
            hsc = self.sb(st, "hsc", [128, SW])
            psG = self.ps(st, "psGate", [128, 512])
            psU = self.ps(st, "psUp", [128, 512])
            psD = [self.ps(st, "psD", [128, 512]) for _ in range(4)]
            psY = [self.ps(st, "psY", [128, 512]) for _ in range(2)]
            WLDG = 16 * (8 + 8 + 2)

            with nc.Block() as blk:
                @blk.sync
                def _(sync):
                    for th in range(2):
                        csl = slice(TH * th, TH * (th + 1))
                        bsl = slice(CS * (1 - th), CS * (1 - th) + CS)
                        s1c = self.xt[0][:, bsl]
                        x2c = self.xt[1][:, bsl]
                        for i in range(NI):
                            if th == 1:
                                sync.wait_ge(s["mmGU"], th * NHG2 * NS * HC2)
                            sync.dma_start(out=xsc[:] if False else xhi32[:],
                                           in_=self.xn1spill.ap()[128 * i:128 * (i + 1), csl]).then_inc(s["xl"], 16)
                            sync.wait_ge(s["spl"], th * NI + i + 1)
                        for oc in range(NI):
                            j = th * NI + oc
                            if j >= 1:
                                sync.wait_ge(s["fx"], j)
                            sync.dma_start(out=s1c[:],
                                           in_=self.s1_sp.ap()[128 * oc:128 * (oc + 1), csl]).then_inc(s["lds"], 16)
                            sync.dma_start(out=x2c[:],
                                           in_=self.x2spill.ap()[128 * oc:128 * (oc + 1), csl]).then_inc(s["lds"], 16)

                @blk.vector
                def _(vector):
                    for th in range(2):
                        bsl = slice(CS * (1 - th), CS * (1 - th) + CS)
                        s1c = self.xt[0][:, bsl]
                        x2c = self.xt[1][:, bsl]
                        g1b = self.xt[2][:, bsl]
                        xfb = self.xt[3][:, bsl]
                        if th == 1:
                            vector.wait_ge(s["outd"], 16 * NI)
                        for i in range(NI):
                            vector.wait_ge(s["xl"], 16 * (th * NI + i + 1))
                            vector.tensor_copy(xh[i][:], xhi32[:])
                            vector.tensor_sub(xhi32[:], xhi32[:], xh[i][:])
                            vector.tensor_copy(xl[i][:], xhi32[:]).then_inc(s["spl"], 1)
                        for hg in range(NHG2):
                            g = th * NHG2 + hg
                            for sp in range(NS):
                                gsp = g * NS + sp
                                if gsp >= 1:
                                    vector.wait_ge(s["mmD"], 2 * gsp)
                                for hc in range(HC2):
                                    j = gsp * HC2 + hc
                                    vector.wait_ge(s["sl"], j + 1)
                                    vector.tensor_tensor(hsc[:], gsil[:],
                                                         psU.ap()[:, 0:SW], OP.mult)
                                    vector.tensor_copy(hh[hc][:], hsc[:])
                                    vector.tensor_sub(hsc[:], hsc[:], hh[hc][:])
                                    vector.tensor_copy(hl[hc][:], hsc[:]).then_inc(s["hh"], 1)
                                for g4 in range(2):
                                    didx = gsp * 2 + g4
                                    vector.wait_ge(s["mmD"], didx + 1)
                                    ssl = slice(TH * th + SW * sp, TH * th + SW * (sp + 1))
                                    for oc in range(4 * g4, 4 * g4 + 4):
                                        if hg == 0:
                                            vi = vector.tensor_copy(dacc[oc][:, ssl],
                                                                    psD[oc % 4].ap()[:, 0:SW])
                                        else:
                                            vi = vector.tensor_add(dacc[oc][:, ssl],
                                                                   dacc[oc][:, ssl],
                                                                   psD[oc % 4].ap()[:, 0:SW])
                                    vi.then_inc(s["evD"], 1)
                        for oc in range(NI):
                            j = th * NI + oc
                            vector.wait_ge(s["lds"], 32 * (j + 1))
                            vector.wait_ge(s["g1"], j + 1)
                            if j >= 1:
                                vector.wait_ge(s["tp"], j * (TH // 128))
                            vector.tensor_mul(xfb[:], s1c[:], g1b[:])
                            vector.tensor_add(xfb[:], xfb[:], x2c[:]).then_inc(s["fx"], 1)
                            for t in range(TH // 128):
                                tidx = j * (TH // 128) + t
                                vector.wait_ge(s["tp"], tidx + 1)
                                vector.tensor_copy(
                                    self.xt[t][:, CS * th + 128 * oc:CS * th + 128 * (oc + 1)],
                                    psY[tidx % 2].ap()[:, 0:128]).then_inc(s["evy"], 1)

                @blk.scalar
                def _(scalar):
                    for th in range(2):
                        bsl = slice(CS * (1 - th), CS * (1 - th) + CS)
                        g1b = self.xt[2][:, bsl]
                        for hg in range(NHG2):
                            g = th * NHG2 + hg
                            for sp in range(NS):
                                for hc in range(HC2):
                                    j = (g * NS + sp) * HC2 + hc
                                    scalar.wait_ge(s["mmGU"], j + 1)
                                    if j >= 1:
                                        scalar.wait_ge(s["hh"], j)
                                    scalar.activation(gsil[:], psG.ap()[:, 0:SW], AF.Silu,
                                                      scale=1.0 / 127.0).then_inc(s["sl"], 1)
                        for oc in range(NI):
                            j = th * NI + oc
                            scalar.wait_ge(s["evD"], 2 * NS * NHG2 * (th + 1))
                            if j >= 1:
                                scalar.wait_ge(s["fx"], j)
                            scalar.activation(g1b[:], dacc[oc][:, TH * th:TH * (th + 1)],
                                              AF.Silu,
                                              scale=1.0 / (127.0 * 127.0)).then_inc(s["g1"], 1)

                @blk.tensor
                def _(tensor):
                    for th in range(2):
                        bsl = slice(CS * (1 - th), CS * (1 - th) + CS)
                        xfb = self.xt[3][:, bsl]
                        tensor.wait_ge(s["spl"], th * NI + NI)
                        for hg in range(NHG2):
                            g = th * NHG2 + hg
                            for sp in range(NS):
                                gsp = g * NS + sp
                                ssl = slice(SW * sp, SW * (sp + 1))
                                for hc in range(HC2):
                                    j = gsp * HC2 + hc
                                    tensor.wait_ge(s["sto"], 16 * (18 * g + 16))
                                    if j >= 1:
                                        tensor.wait_ge(s["sl"], j)
                                        tensor.wait_ge(s["hh"], j)
                                    for i in range(NI):
                                        wsl = slice(256 * i + 128 * hc, 256 * i + 128 * (hc + 1))
                                        tensor.matmul(psG.ap()[:, 0:SW], wgB[:, wsl],
                                                      xh[i][:, ssl],
                                                      start=(i == 0), stop=False)
                                        tensor.matmul(psG.ap()[:, 0:SW], wgB[:, wsl],
                                                      xl[i][:, ssl],
                                                      start=False, stop=(i == NI - 1))
                                    for i in range(NI):
                                        wsl = slice(256 * i + 128 * hc, 256 * i + 128 * (hc + 1))
                                        tensor.matmul(psU.ap()[:, 0:SW], wuB[:, wsl],
                                                      xh[i][:, ssl],
                                                      start=(i == 0), stop=False)
                                        mmi = tensor.matmul(psU.ap()[:, 0:SW], wuB[:, wsl],
                                                            xl[i][:, ssl],
                                                            start=False, stop=(i == NI - 1))
                                    mmi.then_inc(s["mmGU"], 1)
                                tensor.wait_ge(s["sto"], 16 * 18 * (g + 1))
                                tensor.wait_ge(s["hh"], gsp * HC2 + HC2)
                                for g4 in range(2):
                                    didx = gsp * 2 + g4
                                    if didx >= 1:
                                        tensor.wait_ge(s["evD"], didx)
                                    for oc in range(4 * g4, 4 * g4 + 4):
                                        for hc in range(HC2):
                                            wsl = slice(1024 * hc + 128 * oc, 1024 * hc + 128 * (oc + 1))
                                            tensor.matmul(psD[oc % 4].ap()[:, 0:SW],
                                                          wdB[:, wsl], hh[hc][:],
                                                          start=(hc == 0), stop=False)
                                            mmi = tensor.matmul(psD[oc % 4].ap()[:, 0:SW],
                                                                wdB[:, wsl], hl[hc][:],
                                                                start=False, stop=(hc == HC2 - 1))
                                    mmi.then_inc(s["mmD"], 1)
                        for oc in range(NI):
                            j = th * NI + oc
                            tensor.wait_ge(s["fx"], j + 1)
                            for t in range(TH // 128):
                                tidx = j * (TH // 128) + t
                                if tidx >= 2:
                                    tensor.wait_ge(s["evy"], tidx - 1)
                                tensor.transpose(psY[tidx % 2].ap()[:, 0:128],
                                                 xfb[:, 128 * t:128 * (t + 1)],
                                                 self.ident[:]).then_inc(s["tp"], 1)

                @blk.gpsimd
                def _(gpsimd):
                    for th in range(2):
                        for hg in range(NHG2):
                            g = th * NHG2 + hg
                            if g >= 1:
                                gpsimd.wait_ge(s["mmD"], 2 * NS * g)
                            for i in range(NI):
                                gpsimd.dma_start(
                                    out=wgB[:, 256 * i:256 * (i + 1)],
                                    in_=self.l1_wgT.ap()[128 * i:128 * (i + 1),
                                                         256 * hg:256 * (hg + 1)]).then_inc(s["sto"], 16)
                            for i in range(NI):
                                gpsimd.dma_start(
                                    out=wuB[:, 256 * i:256 * (i + 1)],
                                    in_=self.l1_wuT.ap()[128 * i:128 * (i + 1),
                                                         256 * hg:256 * (hg + 1)]).then_inc(s["sto"], 16)
                            for j in range(HC2):
                                gpsimd.dma_start(
                                    out=wdB[:, 1024 * j:1024 * (j + 1)],
                                    in_=self.l1_wdT.ap()[256 * hg + 128 * j:256 * hg + 128 * (j + 1),
                                                         :]).then_inc(s["sto"], 16)
                        for t in range(TH // 128):
                            gt = th * (TH // 128) + t
                            gpsimd.wait_ge(s["evy"], (th + 1) * NI * (TH // 128))
                            gpsimd.dma_start(
                                out=self.y_out.ap()[128 * gt:128 * (gt + 1), :],
                                in_=self.xt[t][:, CS * th:CS * th + D]).then_inc(s["outd"], 16)
                    gpsimd.wait_ge(s["outd"], 16 * NT)

    def p_taps(self):
        if not self.taps:
            return
        nc = self.nc
        with contextlib.ExitStack() as st:
            s = self.sems(st, ["sto"])
            n = [0]
            with nc.Block() as blk:
                @blk.gpsimd
                def _(gpsimd):
                    for name, (srcd, dst) in self.taps.items():
                        gpsimd.dma_start(out=dst.ap()[:, :], in_=srcd.ap()[:, :]).then_inc(s["sto"], 16)
                        n[0] += 16
                    gpsimd.wait_ge(s["sto"], n[0])


def build(tap_names=(), stop_after=None):
    return Builder(tap_names, stop_after=stop_after).build()


def _prep_shared(inputs):
    f = lambda k: np.asarray(inputs[k], dtype=np.float32)
    sh = {}
    sh["l0_wqkv"] = np.ascontiguousarray(
        np.concatenate([f("L0_attn_Wq"), f("L0_attn_Wk"), f("L0_attn_Wv")], axis=1))
    sh["l0_bqkv"] = np.concatenate(
        [f("L0_attn_bq"), f("L0_attn_bk"), f("L0_attn_bv")]).reshape(1, 3 * D)
    sh["l0_wo"] = np.ascontiguousarray(f("L0_attn_Wo"))
    sh["l0_bo"] = np.ascontiguousarray(f("L0_attn_bo").reshape(8, 128).T)
    sh["l0_m"] = f("L0_attn_m").reshape(1, H)
    G1R, G2R = f("L0_ffn_G1_R"), f("L0_ffn_G2_R")
    G1I, G2I = f("L0_ffn_G1_I"), f("L0_ffn_G2_I")

    def s_arr(G1):
        out = np.empty((4, FF, FF), dtype=np.float32)
        for c in range(4):
            g1c = G1[0, :, c, :]
            out[c] = np.broadcast_to(g1c.T[:, None, :, None],
                                     (32, 16, 32, 16)).reshape(FF, FF)
        return np.ascontiguousarray(out.reshape(4 * FF, FF))

    def t_arr(G2):
        out = np.empty((4, FF, FF), dtype=np.float32)
        for c in range(4):
            g2c = G2[c, :, 0, :]
            out[c] = np.broadcast_to(g2c.T[None, :, None, :],
                                     (32, 16, 32, 16)).reshape(FF, FF)
        return np.ascontiguousarray(out.reshape(4 * FF, FF))

    sh["l0_SR"], sh["l0_TR"] = s_arr(G1R), t_arr(G2R)
    sh["l0_SI"], sh["l0_TI"] = s_arr(G1I), t_arr(G2I)
    sh["l0_brbi"] = np.ascontiguousarray(
        np.concatenate([f("L0_ffn_bias_real"), f("L0_ffn_bias_imag")]).reshape(8, 128).T)
    sh["l1_wqkvT"] = np.ascontiguousarray(f("L1_attn_Wqkv").T)
    sh["l1_woutT"] = np.ascontiguousarray(f("L1_attn_Wout").T)
    sh["l1_bout"] = np.ascontiguousarray(f("L1_attn_bout").reshape(8, 128).T)
    sh["l1_lam1"] = np.ascontiguousarray(
        np.stack([f("L1_attn_Wl1")[:, 0], f("L1_attn_bl1")]))
    sh["l1_lam2T"] = np.ascontiguousarray(f("L1_attn_Wl2").T)
    sh["l1_lam2b"] = f("L1_attn_bl2").reshape(1, H)
    sh["l1_wgT"] = np.ascontiguousarray(np.asarray(inputs["L1_ffn_w_gate"]).T)
    sh["l1_wuT"] = np.ascontiguousarray(np.asarray(inputs["L1_ffn_w_up"]).T)
    sh["l1_wdT"] = np.ascontiguousarray(np.asarray(inputs["L1_ffn_w_down"]).T)
    sh["ident_in"] = np.eye(128, dtype=np.float32)
    return sh


def make_in_maps(inputs):
    shared = _prep_shared(inputs)
    maps = []
    for c in range(8):
        b, t0 = c // 4, (c % 4) * T
        d = dict(shared)
        x = np.asarray(inputs["L0_attn_x"][b, t0:t0 + T, :], dtype=np.float32)
        d["x_in"] = np.ascontiguousarray(x.T)
        d["posb_in"] = np.array([[float(t0)]], dtype=np.float32)
        maps.append(d)
    return maps


def kernel(**inputs):
    in_maps = make_in_maps(inputs)
    nc = build()
    res = run_bass_kernel_spmd(nc, in_maps, core_ids=list(range(8)))
    y = np.stack([np.asarray(res.results[c]["y_out"]) for c in range(8)])
    return np.ascontiguousarray(y.reshape(2, 4 * T, D)).astype(np.float32)


# version 08
# speedup vs baseline: 1.1740x; 1.0557x over previous
import contextlib
import numpy as np

import concourse.bass as bass
import concourse.mybir as mybir
from concourse.bass_utils import run_bass_kernel_spmd

F32 = mybir.dt.float32
I32 = mybir.dt.int32
AF = mybir.ActivationFunctionType
OP = mybir.AluOpType
AX = mybir.AxisListType

T = 2048
NT = T // 128
D = 1024
NI = D // 128
H = 16
DH = 64
EPS_LN = 1e-5
EPS_L2 = 1e-12
FF = 512
HID = 3072
HC = 3
NHG = (HID // 128) // HC
CS = 1024
NCH = T // CS
LOG_N = float(np.log(8192.0))
QSCALE = 0.125


QK_MODE = "full"


class Builder:
    def __init__(self, tap_names=(), stop_after=None):
        self.nc = bass.Bass("TRN2", target_bir_lowering=False,
                            detect_race_conditions=False)
        self.tap_names = set(tap_names)
        self.stop_after = stop_after
        self.stack = contextlib.ExitStack()
        self.substacks = []

    GP_NAMES = ("sto", "stoS", "sp", "ccl", "outd")

    def sems(self, st, names):
        out, i = {}, 0
        for n in names:
            if n in self.GP_NAMES:
                out[n] = self.stack.enter_context(self.nc.semaphore(self._uniq(n)))
            else:
                out[n] = self.pool[i]
                i += 1
        return out

    def clear_block(self):
        nc = self.nc
        with nc.Block() as blk:
            @blk.sync
            def _(sync):
                for sem in self.pool:
                    sync.sem_clear(sem)

    _n = 0

    def _uniq(self, n):
        Builder._n += 1
        return f"{n}_{Builder._n}"

    def sb(self, st, name, shape, dtype=F32):
        return st.enter_context(self.nc.sbuf_tensor(self._uniq(name), shape, dtype))

    def ps(self, st, name, shape):
        return st.enter_context(self.nc.psum_tensor(self._uniq(name), shape, F32))

    def build(self):
        nc = self.nc
        big = self.stack

        dt_ = nc.dram_tensor
        self.x_in = dt_("x_in", [D, T], F32, kind="ExternalInput")
        self.l0_wqkv = dt_("l0_wqkv", [D, 3 * D], F32, kind="ExternalInput")
        self.l0_bqkv = dt_("l0_bqkv", [1, 3 * D], F32, kind="ExternalInput")
        self.l0_wo = dt_("l0_wo", [D, D], F32, kind="ExternalInput")
        self.l0_bo = dt_("l0_bo", [128, 8], F32, kind="ExternalInput")
        self.l0_m = dt_("l0_m", [1, H], F32, kind="ExternalInput")
        self.l0_SR = dt_("l0_SR", [4 * FF, FF], F32, kind="ExternalInput")
        self.l0_TR = dt_("l0_TR", [4 * FF, FF], F32, kind="ExternalInput")
        self.l0_SI = dt_("l0_SI", [4 * FF, FF], F32, kind="ExternalInput")
        self.l0_TI = dt_("l0_TI", [4 * FF, FF], F32, kind="ExternalInput")
        self.l0_brbi = dt_("l0_brbi", [128, 8], F32, kind="ExternalInput")
        self.l1_wqkvT = dt_("l1_wqkvT", [D, 3 * D], F32, kind="ExternalInput")
        self.l1_woutT = dt_("l1_woutT", [D, D], F32, kind="ExternalInput")
        self.l1_bout = dt_("l1_bout", [128, 8], F32, kind="ExternalInput")
        self.l1_lam1 = dt_("l1_lam1", [2, 64], F32, kind="ExternalInput")
        self.l1_lam2T = dt_("l1_lam2T", [64, H], F32, kind="ExternalInput")
        self.l1_lam2b = dt_("l1_lam2b", [1, H], F32, kind="ExternalInput")
        self.l1_wgT = dt_("l1_wgT", [D, HID], I32, kind="ExternalInput")
        self.l1_wuT = dt_("l1_wuT", [D, HID], I32, kind="ExternalInput")
        self.l1_wdT = dt_("l1_wdT", [HID, D], I32, kind="ExternalInput")
        self.ident_in = dt_("ident_in", [128, 128], F32, kind="ExternalInput")
        self.posb_in = dt_("posb_in", [1, 1], F32, kind="ExternalInput")
        self.y_out = dt_("y_out", [T, D], F32, kind="ExternalOutput")

        self.xspill = dt_("xspill", [D, T], F32)
        self.q_tok = dt_("q_tok", [T, D], F32)
        self.k_tok = dt_("k_tok", [T, D], F32)
        self.v_tok = dt_("v_tok", [T, D], F32)
        self.qhat_f = dt_("qhat_f", [D, T], F32)
        self.s0_f = dt_("s0_f", [D, T], F32)
        self.s_f = dt_("s_f", [D, T], F32)
        self.x2spill = dt_("x2spill", [D, T], F32)
        self.xn1spill = dt_("xn1spill", [D, T], F32)
        self.qf_sp = dt_("qf_sp", [D, T], F32)
        self.kf_sp = dt_("kf_sp", [D, T], F32)
        self.vtok_sp = dt_("vtok_sp", [T, D], F32)
        self.attnf_sp = dt_("attnf_sp", [D, T], F32)
        self.s1_sp = dt_("s1_sp", [D, T], F32)
        self.cc_in = dt_("cc_in", [64, D], F32)
        self.cc_out = dt_("cc_out", [64, D], F32)
        self.xn0spill = dt_("xn0spill", [D, T], F32)

        self._tappable = {
            "xn0": (self.xn0spill, [D, T]), "q_tok": (self.q_tok, [T, D]),
            "k_tok": (self.k_tok, [T, D]), "v_tok": (self.v_tok, [T, D]),
            "qhat_f": (self.qhat_f, [D, T]), "kv": (self.cc_out, [64, D]),
            "s0_f": (self.s0_f, [D, T]), "s_f": (self.s_f, [D, T]),
            "x2": (self.x2spill, [D, T]), "xn1": (self.xn1spill, [D, T]),
            "qf": (self.qf_sp, [D, T]), "kf": (self.kf_sp, [D, T]),
            "vtok": (self.vtok_sp, [T, D]), "attnf": (self.attnf_sp, [D, T]),
            "s1": (self.s1_sp, [D, T]),
        }
        self.taps = {}
        for name in self.tap_names:
            src, shape = self._tappable[name]
            self.taps[name] = (src, dt_("tap_" + name, shape, F32,
                                        kind="ExternalOutput"))

        self.pool = [big.enter_context(nc.semaphore(f"sp{i}")) for i in range(26)]

        self.ident = self.sb(big, "ident", [128, 128])
        self.ones_col = self.sb(big, "ones_col", [128, 1])
        self.ones2 = self.sb(big, "ones2", [128, 2])
        self.ones_row = self.sb(big, "ones_row", [1, 512])
        self.ones_r1 = self.sb(big, "ones_r1", [1, 128])
        self.bo_cols = self.sb(big, "bo_cols", [128, 8])
        self.brbi_cols = self.sb(big, "brbi_cols", [128, 8])
        self.bout_cols = self.sb(big, "bout_cols", [128, 8])
        self.m_row = self.sb(big, "m_row", [1, H])
        self.posb = self.sb(big, "posb", [1, 1])
        self.xt = [self.sb(big, f"xt{i}", [128, T]) for i in range(NI)]
        self.normq = self.sb(big, "normq", [128, 256])
        self.normk = self.sb(big, "normk", [128, 256])
        self.lam8t = [self.sb(big, "lam8t", [128, H]) for _ in range(NT)]
        self.negB = self.sb(big, "negB", [128, 32])

        def stop(name):
            return self.stop_after is not None and name == self.stop_after

        done = False
        BF = mybir.dt.bfloat16
        self.p_load()
        if not stop("load"):
            self.xhl_stack = contextlib.ExitStack()
            self.substacks.append(self.xhl_stack)
            self.xh = [self.sb(self.xhl_stack, f"xh{i}", [128, T], BF) for i in range(NI)]
            self.xl = [self.sb(self.xhl_stack, f"xl{i}", [128, T], BF) for i in range(NI)]
            self.clear_block()
            self.p_ln("ln0", self.xt, self.xh, self.xl,
                      spill=self.xn0spill if "xn0" in self.tap_names else None)
        else:
            done = True
        if not done and not stop("ln0"):
            self.clear_block()
            self.p_l0_qkv()
        else:
            done = True
        if not done and not stop("qkv"):
            self.clear_block()
            self.p_l0_kv()
        else:
            done = True
        if not done and not stop("kv"):
            self.clear_block()
            self.p_l0_cc_qhat()
        else:
            done = True
        if not done and not stop("cc"):
            self.clear_block()
            self.p_l0_s0()
            self.kv_stack.close()
        else:
            done = True
        if not done and not stop("s0"):
            self.clear_block()
            self.p_wtbuild()
            self.clear_block()
            self.p_l0_oproj_ffn()
            self.wt_stack.close()
            self.xhl_stack.close()
        else:
            done = True
        if not done and not stop("oproj"):
            self.xhl_stack = contextlib.ExitStack()
            self.substacks.append(self.xhl_stack)
            self.xh = [self.sb(self.xhl_stack, f"x1h{i}", [128, T], BF) for i in range(NI)]
            self.xl = [self.sb(self.xhl_stack, f"x1l{i}", [128, T], BF) for i in range(NI)]
            self.clear_block()
            self.p_ln_reload("ln1", self.x2spill, self.xt, self.xh, self.xl,
                             spill=self.xn1spill)
        else:
            done = True
        if not done and not stop("ln1"):
            self.clear_block()
            self.p_l1_qk()
        else:
            done = True
        if not done and not stop("qk"):
            self.clear_block()
            self.p_l1_lam_bound()
        else:
            done = True
        if not done and not stop("lam"):
            self.clear_block()
            self.p_l1_v()
            self.xhl_stack.close()
        else:
            done = True
        if not done and not stop("v"):
            self.clear_block()
            self.p_l1_attn()
        else:
            done = True
        if not done and not stop("attn"):
            self.clear_block()
            self.p_l1_outproj()
        else:
            done = True
        if not done and not stop("outproj"):
            self.clear_block()
            self.p_l1_ffn_final()
        else:
            done = True
        self.clear_block()
        self.p_taps()

        for stk in reversed(self.substacks):
            stk.close()
        self.stack.close()
        return self.nc

    def p_load(self):
        nc = self.nc
        with contextlib.ExitStack() as st:
            s = self.sems(st, ["ld", "sto"])
            n = [0]
            with nc.Block() as blk:
                @blk.sync
                def _(sync):
                    for i in range(NI):
                        sync.dma_start(out=self.xt[i][:],
                                       in_=self.x_in.ap()[128 * i:128 * (i + 1), :]).then_inc(s["ld"], 16)
                        n[0] += 16
                    for dst, src in [(self.ident, self.ident_in),
                                     (self.bo_cols, self.l0_bo),
                                     (self.brbi_cols, self.l0_brbi),
                                     (self.bout_cols, self.l1_bout),
                                     (self.m_row, self.l0_m),
                                     (self.posb, self.posb_in)]:
                        sync.dma_start(out=dst[:], in_=src.ap()[:, :]).then_inc(s["ld"], 16)
                        n[0] += 16

                @blk.gpsimd
                def _(gpsimd):
                    gpsimd.dma_start(out=self.xspill.ap()[:, :],
                                     in_=self.x_in.ap()[:, :]).then_inc(s["sto"], 16)

                @blk.vector
                def _(vector):
                    vector.memset(self.ones_col[:], 1.0)
                    vector.memset(self.ones2[:], 1.0)
                    vector.memset(self.ones_row[:], 1.0)
                    vector.memset(self.ones_r1[:], 1.0)
                    vector.wait_ge(s["ld"], n[0])

    def _emit_ln(self, tag, src_tiles, xh_tiles, xl_tiles, spill):
        nc = self.nc
        with contextlib.ExitStack() as st:
            s = self.sems(st, ["sq", "mm", "ev", "bc", "sqr", "xns", "sto"])
            with contextlib.ExitStack() as st_stats:
                ps_sum = self.ps(st_stats, "ps_sum", [1, T])
                ps_sumsq = self.ps(st_stats, "ps_sumsq", [1, T])
                xsq = [self.sb(st, "xsq", [128, T]) for _ in range(2)]
                mu = self.sb(st, "mu", [1, T])
                rstd = self.sb(st, "rstd", [1, T])
                tmp1 = self.sb(st, "tmp1", [1, T])
                tmp2 = self.sb(st, "tmp2", [1, T])
                xnf = [self.sb(st, "xnf", [128, T]) for _ in range(2)]

                with nc.Block() as blk:
                    @blk.scalar
                    def _(scalar):
                        for i in range(NI):
                            if i >= 2:
                                scalar.wait_ge(s["mm"], i - 1)
                            scalar.activation(xsq[i % 2][:], src_tiles[i][:],
                                              AF.Square).then_inc(s["sq"], 1)
                        scalar.wait_ge(s["ev"], 1)
                        scalar.activation(tmp2[:], tmp1[:], AF.Sqrt).then_inc(s["sqr"], 1)

                    @blk.tensor
                    def _(tensor):
                        for i in range(NI):
                            for q in range(4):
                                sl = slice(512 * q, 512 * (q + 1))
                                tensor.matmul(ps_sum.ap()[:, sl], self.ones_col[:],
                                              src_tiles[i][:, sl],
                                              start=(i == 0), stop=(i == NI - 1))
                            tensor.wait_ge(s["sq"], i + 1)
                            for q in range(4):
                                sl = slice(512 * q, 512 * (q + 1))
                                mmi = tensor.matmul(ps_sumsq.ap()[:, sl], self.ones_col[:],
                                                    xsq[i % 2][:, sl],
                                                    start=(i == 0), stop=(i == NI - 1))
                            mmi.then_inc(s["mm"], 1)

                    @blk.vector
                    def _(vector):
                        vector.wait_ge(s["mm"], NI)
                        vector.tensor_scalar(mu[:], ps_sum.ap()[:, :], 1.0 / D, None, OP.mult)
                        vector.tensor_scalar(tmp1[:], ps_sumsq.ap()[:, :], 1.0 / D, None, OP.mult)
                        vector.tensor_mul(tmp2[:], mu[:], mu[:])
                        vector.tensor_sub(tmp1[:], tmp1[:], tmp2[:])
                        vector.tensor_scalar(tmp1[:], tmp1[:], EPS_LN, None,
                                             OP.add).then_inc(s["ev"], 1)
                        vector.wait_ge(s["sqr"], 1)
                        vector.reciprocal(rstd[:], tmp2[:]).then_inc(s["ev"], 1)

            ps_mu = self.ps(st, "ps_mu", [128, T])
            ps_rs = self.ps(st, "ps_rs", [128, T])
            with nc.Block() as blk:
                @blk.tensor
                def _(tensor):
                    for q in range(4):
                        sl = slice(512 * q, 512 * (q + 1))
                        tensor.matmul(ps_mu.ap()[:, sl], self.ones_r1[:],
                                      mu[:, sl], start=True, stop=True)
                        mmi = tensor.matmul(ps_rs.ap()[:, sl], self.ones_r1[:],
                                            rstd[:, sl], start=True, stop=True)
                    mmi.then_inc(s["bc"], 1)

                @blk.vector
                def _(vector):
                    vector.wait_ge(s["bc"], 1)
                    for i in range(NI):
                        if spill is not None and i >= 2:
                            vector.wait_ge(s["sto"], 16 * (i - 1))
                        for q in range(4):
                            sl = slice(512 * q, 512 * (q + 1))
                            vector.tensor_sub(xnf[i % 2][:, sl], src_tiles[i][:, sl],
                                              ps_mu.ap()[:, sl])
                            vector.tensor_mul(xnf[i % 2][:, sl], xnf[i % 2][:, sl],
                                              ps_rs.ap()[:, sl])
                        vector.tensor_copy(xh_tiles[i][:], xnf[i % 2][:])
                        vector.tensor_sub(xsq[i % 2][:], xnf[i % 2][:], xh_tiles[i][:])
                        vector.tensor_copy(xl_tiles[i][:], xsq[i % 2][:]).then_inc(s["xns"], 1)

                @blk.gpsimd
                def _(gpsimd):
                    if spill is not None:
                        for i in range(NI):
                            gpsimd.wait_ge(s["xns"], i + 1)
                            gpsimd.dma_start(out=spill.ap()[128 * i:128 * (i + 1), :],
                                             in_=xnf[i % 2][:]).then_inc(s["sto"], 16)

    def p_ln(self, tag, src, xh, xl, spill=None):
        self._emit_ln(tag, src, xh, xl, spill)

    def p_ln_reload(self, tag, spill_in, src_tiles, xh, xl, spill=None):
        nc = self.nc
        with contextlib.ExitStack() as st:
            s = self.sems(st, ["ld"])
            with nc.Block() as blk:
                @blk.sync
                def _(sync):
                    for i in range(NI):
                        sync.dma_start(out=src_tiles[i][:],
                                       in_=spill_in.ap()[128 * i:128 * (i + 1), :]).then_inc(s["ld"], 16)

                @blk.vector
                def _(vector):
                    vector.wait_ge(s["ld"], 16 * NI)
        self.clear_block()
        self._emit_ln(tag, src_tiles, xh, xl, spill)

    def p_wtbuild(self):
        nc = self.nc
        BF = mybir.dt.bfloat16
        self.wt_stack = contextlib.ExitStack()
        self.substacks.append(self.wt_stack)
        big = self.wt_stack
        self.wrTh = [self.sb(big, "wrTh", [128, FF], BF) for _ in range(4)]
        self.wrTl = [self.sb(big, "wrTl", [128, FF], BF) for _ in range(4)]
        self.wiTh = [self.sb(big, "wiTh", [128, FF], BF) for _ in range(4)]
        self.wiTl = [self.sb(big, "wiTl", [128, FF], BF) for _ in range(4)]
        self.wiTnh = [self.sb(big, "wiTnh", [128, FF], BF) for _ in range(4)]
        self.wiTnl = [self.sb(big, "wiTnl", [128, FF], BF) for _ in range(4)]
        with contextlib.ExitStack() as st:
            s = self.sems(st, ["ld", "used"])
            sbuf = [self.sb(st, "stbuf", [128, FF]) for _ in range(4)]
            tmp = self.sb(st, "wtmp", [128, FF])
            acc = self.sb(st, "wacc", [128, FF])
            jobs = [(self.l0_SR, self.l0_TR, self.wrTh, self.wrTl, None, None),
                    (self.l0_SI, self.l0_TI, self.wiTh, self.wiTl,
                     self.wiTnh, self.wiTnl)]
            seq = []
            for job in jobs:
                for r in range(4):
                    for c in range(4):
                        seq.append(job + (r, c))
            with nc.Block() as blk:
                @blk.sync
                def _(sync):
                    for j, (Sd, Td, dh, dl, dnh, dnl, r, c) in enumerate(seq):
                        if j >= 2:
                            sync.wait_ge(s["used"], j - 1)
                        row = 512 * c + 128 * r
                        sync.dma_start(out=sbuf[2 * (j % 2)][:],
                                       in_=Sd.ap()[row:row + 128, :]).then_inc(s["ld"], 16)
                        sync.dma_start(out=sbuf[2 * (j % 2) + 1][:],
                                       in_=Td.ap()[row:row + 128, :]).then_inc(s["ld"], 16)

                @blk.vector
                def _(vector):
                    for j, (Sd, Td, dh, dl, dnh, dnl, r, c) in enumerate(seq):
                        vector.wait_ge(s["ld"], 32 * (j + 1))
                        a, b = sbuf[2 * (j % 2)], sbuf[2 * (j % 2) + 1]
                        if c == 0:
                            vector.tensor_mul(acc[:], a[:], b[:]).then_inc(s["used"], 1)
                        else:
                            vector.tensor_mul(tmp[:], a[:], b[:]).then_inc(s["used"], 1)
                            vector.tensor_add(acc[:], acc[:], tmp[:])
                        if c == 3:
                            vector.tensor_copy(dh[r][:], acc[:])
                            vector.tensor_sub(acc[:], acc[:], dh[r][:])
                            vector.tensor_copy(dl[r][:], acc[:])
                            if dnh is not None:
                                vector.tensor_scalar(dnh[r][:], dh[r][:], -1.0, None, OP.mult)
                                vector.tensor_scalar(dnl[r][:], dl[r][:], -1.0, None, OP.mult)

    def p_l0_qkv(self):
        nc = self.nc
        BF = mybir.dt.bfloat16
        NOB = 6
        dests = [(self.q_tok, 0), (self.q_tok, 512), (self.k_tok, 0),
                 (self.k_tok, 512), (self.v_tok, 0), (self.v_tok, 512)]
        with contextlib.ExitStack() as st:
            s = self.sems(st, ["wld", "wsp", "mm", "ev", "bq", "bev", "sto"])
            wf32 = self.sb(st, "wf32", [128, 2048])
            wh = [self.sb(st, "wh", [128, 8 * 512], BF) for _ in range(2)]
            wl = [self.sb(st, "wl", [128, 8 * 512], BF) for _ in range(2)]
            obuf = [self.sb(st, "obuf", [128, 512]) for _ in range(2)]
            qkvb = self.sb(st, "qkvb", [128, 3 * D])
            bqrow = self.sb(st, "bqrow", [1, 3 * D])
            psm = [self.ps(st, "psqkv", [128, 512]) for _ in range(2)]
            psb = self.ps(st, "psbias", [128, 512])

            with nc.Block() as blk:
                @blk.sync
                def _(sync):
                    sync.dma_start(out=bqrow[:], in_=self.l0_bqkv.ap()[:, :]).then_inc(s["bq"], 16)
                    for ob in range(NOB):
                        for r in range(2):
                            j = 2 * ob + r
                            if j >= 1:
                                sync.wait_ge(s["wsp"], j)
                            for i2 in range(4):
                                i = 4 * r + i2
                                sync.dma_start(
                                    out=wf32[:, 512 * i2:512 * (i2 + 1)],
                                    in_=self.l0_wqkv.ap()[128 * i:128 * (i + 1),
                                                          512 * ob:512 * (ob + 1)]).then_inc(s["wld"], 16)

                @blk.vector
                def _(vector):
                    for b6 in range(6):
                        vector.wait_ge(s["mm"], b6 + 1)
                        vector.tensor_copy(qkvb[:, 512 * b6:512 * (b6 + 1)],
                                           psb.ap()[:, :]).then_inc(s["bev"], 1)
                    for ob in range(NOB):
                        for r in range(2):
                            j = 2 * ob + r
                            vector.wait_ge(s["wld"], 64 * (j + 1))
                            if ob >= 2:
                                vector.wait_ge(s["ev"], NT * (ob - 1))
                            csl = slice(2048 * r, 2048 * (r + 1))
                            vector.tensor_copy(wh[ob % 2][:, csl], wf32[:])
                            vector.tensor_sub(wf32[:], wf32[:], wh[ob % 2][:, csl])
                            vector.tensor_copy(wl[ob % 2][:, csl], wf32[:]).then_inc(s["wsp"], 1)
                        for t in range(NT):
                            idx = ob * NT + t
                            vector.wait_ge(s["mm"], 6 + idx + 1)
                            if idx >= 2:
                                vector.wait_ge(s["sto"], 16 * (idx - 1))
                            vector.tensor_tensor(
                                obuf[idx % 2][:], psm[idx % 2].ap()[:, :],
                                qkvb[:, 512 * ob:512 * (ob + 1)],
                                OP.add).then_inc(s["ev"], 1)

                @blk.tensor
                def _(tensor):
                    tensor.wait_ge(s["bq"], 16)
                    for b6 in range(6):
                        if b6 >= 1:
                            tensor.wait_ge(s["bev"], b6)
                        tensor.matmul(psb.ap()[:, :], self.ones_r1[:],
                                      bqrow[:, 512 * b6:512 * (b6 + 1)],
                                      start=True, stop=True).then_inc(s["mm"], 1)
                    for ob in range(NOB):
                        tensor.wait_ge(s["wsp"], 2 * (ob + 1))
                        for t in range(NT):
                            idx = ob * NT + t
                            if idx >= 2:
                                tensor.wait_ge(s["ev"], idx - 1)
                            tsl = slice(128 * t, 128 * (t + 1))
                            for i in range(NI):
                                wsl = slice(512 * i, 512 * (i + 1))
                                tensor.matmul(psm[idx % 2].ap()[:, :],
                                              self.xh[i][:, tsl], wh[ob % 2][:, wsl],
                                              start=(i == 0), stop=False)
                                tensor.matmul(psm[idx % 2].ap()[:, :],
                                              self.xh[i][:, tsl], wl[ob % 2][:, wsl],
                                              start=False, stop=False)
                                mmi = tensor.matmul(psm[idx % 2].ap()[:, :],
                                                    self.xl[i][:, tsl], wh[ob % 2][:, wsl],
                                                    start=False, stop=(i == NI - 1))
                            mmi.then_inc(s["mm"], 1)

                @blk.gpsimd
                def _(gpsimd):
                    for ob in range(NOB):
                        dstd, coff = dests[ob]
                        for t in range(NT):
                            idx = ob * NT + t
                            gpsimd.wait_ge(s["ev"], idx + 1)
                            gpsimd.dma_start(
                                out=bass.AP(dstd, 128 * t * D + coff,
                                            [[D, 128], [1, 512]]),
                                in_=obuf[idx % 2][:]).then_inc(s["sto"], 16)

    def p_l0_kv(self):
        nc = self.nc
        self.kv_stack = contextlib.ExitStack()
        self.substacks.append(self.kv_stack)
        big = self.kv_stack
        self.srecip = self.sb(big, "srecip", [1, H])
        with contextlib.ExitStack() as st:
            s = self.sems(st, ["ld", "sq", "vr", "sr", "kh", "mm", "ev", "sg", "rc",
                               "sto"])
            kt = [self.sb(st, "ktile", [128, D]) for _ in range(2)]
            vt = [self.sb(st, "vtile", [128, D]) for _ in range(2)]
            ksq = [self.sb(st, "ksq", [128, D]) for _ in range(2)]
            khat = [self.sb(st, "khat", [128, D]) for _ in range(2)]
            ssq = [self.sb(st, "ssq", [128, H]) for _ in range(2)]
            nrm = [self.sb(st, "nrm", [128, H]) for _ in range(2)]
            kvpart = self.sb(st, "kvpart", [64, D])
            sgr = self.sb(st, "sgr", [1, H])
            sclr = self.sb(st, "sclr", [1, H])
            pskv = self.ps(st, "pskv", [64, D])

            with nc.Block() as blk:
                @blk.sync
                def _(sync):
                    for t in range(NT):
                        if t >= 2:
                            sync.wait_ge(s["mm"], t - 1)
                        sync.dma_start(out=kt[t % 2][:],
                                       in_=self.k_tok.ap()[128 * t:128 * (t + 1), :]).then_inc(s["ld"], 16)
                        sync.dma_start(out=vt[t % 2][:],
                                       in_=self.v_tok.ap()[128 * t:128 * (t + 1), :]).then_inc(s["ld"], 16)

                @blk.scalar
                def _(scalar):
                    scalar.activation(sgr[:], self.m_row[:], AF.Sigmoid)
                    scalar.activation(sclr[:], sgr[:], AF.Exp, scale=LOG_N).then_inc(s["sg"], 1)
                    for t in range(NT):
                        scalar.wait_ge(s["ld"], 32 * (t + 1))
                        if t >= 2:
                            scalar.wait_ge(s["vr"], t - 1)
                        scalar.activation(ksq[t % 2][:], kt[t % 2][:],
                                          AF.Square).then_inc(s["sq"], 1)
                        scalar.wait_ge(s["vr"], t + 1)
                        scalar.activation(nrm[t % 2][:], ssq[t % 2][:],
                                          AF.Sqrt).then_inc(s["sr"], 1)

                @blk.vector
                def _(vector):
                    vector.wait_ge(s["sg"], 1)
                    vector.reciprocal(self.srecip[:], sclr[:])
                    for t in range(NT):
                        vector.wait_ge(s["sq"], t + 1)
                        v3 = bass.AP(ksq[t % 2], 0, [[D, 128], [DH, H], [1, DH]])
                        vector.tensor_reduce(ssq[t % 2][:], v3, AX.X,
                                             OP.add).then_inc(s["vr"], 1)
                        vector.wait_ge(s["sr"], t + 1)
                        vector.tensor_scalar_max(nrm[t % 2][:], nrm[t % 2][:], EPS_L2)
                        vector.reciprocal(nrm[t % 2][:], nrm[t % 2][:]).then_inc(s["rc"], 1)
                        vector.wait_ge(s["rc"], t + 1)
                        if t >= 2:
                            vector.wait_ge(s["mm"], t - 1)
                        for h in range(H):
                            vi = vector.tensor_scalar(
                                khat[t % 2][:, DH * h:DH * (h + 1)],
                                kt[t % 2][:, DH * h:DH * (h + 1)],
                                nrm[t % 2][:, h:h + 1], None, OP.mult)
                        vi.then_inc(s["kh"], 1)
                    vector.wait_ge(s["mm"], NT)
                    vector.tensor_copy(kvpart[:], pskv.ap()[:, :]).then_inc(s["ev"], 1)

                @blk.tensor
                def _(tensor):
                    for t in range(NT):
                        tensor.wait_ge(s["kh"], t + 1)
                        for h in range(H):
                            mmi = tensor.matmul(
                                pskv.ap()[:, DH * h:DH * (h + 1)],
                                khat[t % 2][:, DH * h:DH * (h + 1)],
                                vt[t % 2][:, DH * h:DH * (h + 1)],
                                start=(t == 0 and h % 8 == 0),
                                stop=(t == NT - 1 and h % 8 == 7),
                                skip_group_check=True)
                        mmi.then_inc(s["mm"], 1)

                @blk.gpsimd
                def _(gpsimd):
                    gpsimd.wait_ge(s["ev"], 1)
                    gpsimd.dma_start(out=self.cc_in.ap()[:, :],
                                     in_=kvpart[:]).then_inc(s["sto"], 16)
                    gpsimd.wait_ge(s["sto"], 16)

    def p_l0_cc_qhat(self):
        nc = self.nc
        big = self.kv_stack
        self.kvbar = self.sb(big, "kvbar", [64, D])
        with contextlib.ExitStack() as st:
            s = self.sems(st, ["cc", "ccl", "bc", "bev", "ld", "sq", "vr", "sr",
                              "qh", "tp", "ev", "rc", "sto"])
            qt = [self.sb(st, "qtile", [128, D]) for _ in range(2)]
            qsq = [self.sb(st, "qsq", [128, D]) for _ in range(2)]
            qhat = [self.sb(st, "qhat", [128, D]) for _ in range(2)]
            ssq = [self.sb(st, "ssq2", [128, H]) for _ in range(2)]
            nrm = [self.sb(st, "nrm2", [128, H]) for _ in range(2)]
            qtr = [self.sb(st, "qtr", [64, T]) for _ in range(2)]
            kvld = self.sb(st, "kvld", [64, D])
            srp = self.sb(st, "srp", [64, H])
            pssr = self.ps(st, "pssr", [64, 512])
            pstr = [self.ps(st, "pstr", [64, 512]) for _ in range(4)]

            with nc.Block() as blk:
                @blk.gpsimd
                def _(gpsimd):
                    gpsimd.collective_compute(
                        "AllReduce", OP.add,
                        ins=[self.cc_in.ap()[:, :]],
                        outs=[self.cc_out.ap()[:, :]],
                        replica_groups=[[0, 1, 2, 3], [4, 5, 6, 7]],
                    ).then_inc(s["cc"], 1)
                    gpsimd.wait_ge(s["cc"], 1)
                    gpsimd.dma_start(out=kvld[:], in_=self.cc_out.ap()[:, :]).then_inc(s["ccl"], 16)
                    for t in range(NT):
                        gpsimd.wait_ge(s["ev"], (t + 1) * H)
                        gpsimd.dma_start(
                            out=bass.AP(self.qhat_f, 128 * t,
                                        [[T, 64], [64 * T, H], [1, 128]]),
                            in_=bass.AP(qtr[t % 2], 0, [[T, 64], [128, H], [1, 128]]),
                        ).then_inc(s["sto"], 16)

                @blk.tensor
                def _(tensor):
                    tensor.matmul(pssr.ap()[:, 0:H], self.ones_r1[:, 0:64],
                                  self.srecip[:], start=True, stop=True).then_inc(s["bc"], 1)
                    for t in range(NT):
                        tensor.wait_ge(s["qh"], t + 1)
                        for h in range(H):
                            j = (t * H + h) % 4
                            if t * H + h >= 4:
                                tensor.wait_ge(s["ev"], t * H + h - 3)
                            tensor.transpose(pstr[j].ap()[:, 0:128],
                                             qhat[t % 2][:, DH * h:DH * (h + 1)],
                                             self.ident[:]).then_inc(s["tp"], 1)

                @blk.sync
                def _(sync):
                    for t in range(NT):
                        if t >= 2:
                            sync.wait_ge(s["tp"], (t - 1) * H)
                        sync.dma_start(out=qt[t % 2][:],
                                       in_=self.q_tok.ap()[128 * t:128 * (t + 1), :]).then_inc(s["ld"], 16)

                @blk.scalar
                def _(scalar):
                    for t in range(NT):
                        scalar.wait_ge(s["ld"], 16 * (t + 1))
                        if t >= 2:
                            scalar.wait_ge(s["vr"], t - 1)
                        scalar.activation(qsq[t % 2][:], qt[t % 2][:],
                                          AF.Square).then_inc(s["sq"], 1)
                        scalar.wait_ge(s["vr"], t + 1)
                        scalar.activation(nrm[t % 2][:], ssq[t % 2][:],
                                          AF.Sqrt).then_inc(s["sr"], 1)

                @blk.vector
                def _(vector):
                    vector.wait_ge(s["bc"], 1)
                    vector.tensor_copy(srp[:], pssr.ap()[:, 0:H]).then_inc(s["bev"], 1)
                    vector.wait_ge(s["bev"], 1)
                    vector.wait_ge(s["ccl"], 16)
                    for h in range(H):
                        vector.tensor_scalar(self.kvbar[:, DH * h:DH * (h + 1)],
                                             kvld[:, DH * h:DH * (h + 1)],
                                             srp[:, h:h + 1], None, OP.mult)
                    for t in range(NT):
                        vector.wait_ge(s["sq"], t + 1)
                        v3 = bass.AP(qsq[t % 2], 0, [[D, 128], [DH, H], [1, DH]])
                        vector.tensor_reduce(ssq[t % 2][:], v3, AX.X,
                                             OP.add).then_inc(s["vr"], 1)
                        vector.wait_ge(s["sr"], t + 1)
                        vector.tensor_scalar_max(nrm[t % 2][:], nrm[t % 2][:], EPS_L2)
                        vector.reciprocal(nrm[t % 2][:], nrm[t % 2][:]).then_inc(s["rc"], 1)
                        vector.wait_ge(s["rc"], t + 1)
                        if t >= 2:
                            vector.wait_ge(s["tp"], (t - 1) * H)
                        for h in range(H):
                            vi = vector.tensor_scalar(
                                qhat[t % 2][:, DH * h:DH * (h + 1)],
                                qt[t % 2][:, DH * h:DH * (h + 1)],
                                nrm[t % 2][:, h:h + 1], None, OP.mult)
                        vi.then_inc(s["qh"], 1)
                        if t >= 2:
                            vector.wait_ge(s["sto"], 16 * (t - 1))
                        for h in range(H):
                            vector.wait_ge(s["tp"], t * H + h + 1)
                            vector.tensor_copy(qtr[t % 2][:, 128 * h:128 * (h + 1)],
                                               pstr[(t * H + h) % 4].ap()[:, 0:128]).then_inc(s["ev"], 1)

    def p_l0_s0(self):
        nc = self.nc
        with contextlib.ExitStack() as st:
            s = self.sems(st, ["ld", "mm", "ev", "sto"])
            qhf = [self.sb(st, "qhf", [64, T]) for _ in range(2)]
            s0b = [self.sb(st, "s0b", [64, T]) for _ in range(2)]
            psb = [self.ps(st, "pss0", [64, 512]) for _ in range(2)]
            with nc.Block() as blk:
                @blk.sync
                def _(sync):
                    for h in range(H):
                        if h >= 2:
                            sync.wait_ge(s["mm"], 4 * (h - 1))
                        sync.dma_start(out=qhf[h % 2][:],
                                       in_=self.qhat_f.ap()[64 * h:64 * (h + 1), :]).then_inc(s["ld"], 16)

                @blk.tensor
                def _(tensor):
                    for h in range(H):
                        tensor.wait_ge(s["ld"], 16 * (h + 1))
                        for q in range(4):
                            idx = h * 4 + q
                            if idx >= 2:
                                tensor.wait_ge(s["ev"], idx - 1)
                            tensor.matmul(psb[idx % 2].ap()[:, :],
                                          self.kvbar[:, DH * h:DH * (h + 1)],
                                          qhf[h % 2][:, 512 * q:512 * (q + 1)],
                                          start=True, stop=True).then_inc(s["mm"], 1)

                @blk.vector
                def _(vector):
                    for h in range(H):
                        if h >= 2:
                            vector.wait_ge(s["sto"], 16 * (h - 1))
                        for q in range(4):
                            idx = h * 4 + q
                            vector.wait_ge(s["mm"], idx + 1)
                            vector.tensor_copy(s0b[h % 2][:, 512 * q:512 * (q + 1)],
                                               psb[idx % 2].ap()[:, :]).then_inc(s["ev"], 1)

                @blk.gpsimd
                def _(gpsimd):
                    for h in range(H):
                        gpsimd.wait_ge(s["ev"], 4 * (h + 1))
                        gpsimd.dma_start(out=self.s0_f.ap()[64 * h:64 * (h + 1), :],
                                         in_=s0b[h % 2][:]).then_inc(s["sto"], 16)

    def p_l0_oproj_ffn(self):
        nc = self.nc
        s0t = self.xt
        with contextlib.ExitStack() as st:
            s = self.sems(st, ["l0", "wld", "wfree", "mmS", "evS", "mmG", "gv",
                              "ldx", "ev2", "sto", "stoS"])
            wo_buf = [self.sb(st, "wo_buf", [128, D]) for _ in range(2)]
            schunk = self.sb(st, "schunk", [128, T])
            gchunk = self.sb(st, "gchunk", [128, T])
            xold = [self.sb(st, "xold", [128, T]) for _ in range(2)]
            x2b = [self.sb(st, "x2b", [128, 512]) for _ in range(2)]
            psS = [self.ps(st, "psS", [128, 512]) for _ in range(2)]
            psG = [self.ps(st, "psG", [128, 512]) for _ in range(2)]

            with nc.Block() as blk:
                @blk.sync
                def _(sync):
                    for i in range(NI):
                        sync.dma_start(out=s0t[i][:],
                                       in_=self.s0_f.ap()[128 * i:128 * (i + 1), :]).then_inc(s["l0"], 16)
                    for oc in range(NI):
                        if oc >= 2:
                            sync.wait_ge(s["evS"], 4 * (oc - 1))
                        for i in range(NI):
                            sync.dma_start(
                                out=wo_buf[oc % 2][:, 128 * i:128 * (i + 1)],
                                in_=self.l0_wo.ap()[128 * i:128 * (i + 1),
                                                    128 * oc:128 * (oc + 1)]).then_inc(s["wld"], 16)
                        if oc >= 2:
                            sync.wait_ge(s["ev2"], 4 * (oc - 1))
                        sync.dma_start(out=xold[oc % 2][:],
                                       in_=self.xspill.ap()[128 * oc:128 * (oc + 1), :]).then_inc(s["ldx"], 16)

                @blk.tensor
                def _(tensor):
                    tensor.wait_ge(s["l0"], 16 * NI)
                    for oc in range(NI):
                        tensor.wait_ge(s["wld"], 16 * 8 * (oc + 1))
                        for q in range(4):
                            idx = oc * 4 + q
                            if idx >= 2:
                                tensor.wait_ge(s["evS"], idx - 1)
                            for i in range(NI):
                                mmi = tensor.matmul(
                                    psS[idx % 2].ap()[:, :],
                                    wo_buf[oc % 2][:, 128 * i:128 * (i + 1)],
                                    s0t[i][:, 512 * q:512 * (q + 1)],
                                    start=(i == 0), stop=(i == NI - 1))
                            mmi.then_inc(s["mmS"], 1)
                            if idx >= 2:
                                tensor.wait_ge(s["gv"], idx - 1)
                            oc2 = oc % 4
                            if oc < 4:
                                pairs = [(self.wrTh[ic], self.wrTl[ic], ic) for ic in range(4)] + \
                                        [(self.wiTnh[ic], self.wiTnl[ic], 4 + ic) for ic in range(4)]
                            else:
                                pairs = [(self.wiTh[ic], self.wiTl[ic], ic) for ic in range(4)] + \
                                        [(self.wrTh[ic], self.wrTl[ic], 4 + ic) for ic in range(4)]
                            wsl = slice(128 * oc2, 128 * (oc2 + 1))
                            ssl = slice(512 * q, 512 * (q + 1))
                            for jj, (wth, wtl, xi) in enumerate(pairs):
                                tensor.matmul(psG[idx % 2].ap()[:, :], wth[:, wsl],
                                              self.xh[xi][:, ssl],
                                              start=(jj == 0), stop=False)
                                tensor.matmul(psG[idx % 2].ap()[:, :], wtl[:, wsl],
                                              self.xh[xi][:, ssl],
                                              start=False, stop=False)
                                mmi = tensor.matmul(psG[idx % 2].ap()[:, :], wth[:, wsl],
                                                    self.xl[xi][:, ssl],
                                                    start=False, stop=(jj == 7))
                            mmi.then_inc(s["mmG"], 1)

                @blk.scalar
                def _(scalar):
                    for oc in range(NI):
                        for q in range(4):
                            idx = oc * 4 + q
                            scalar.wait_ge(s["mmG"], idx + 1)
                            if idx >= 1:
                                scalar.wait_ge(s["ev2"], idx)
                            scalar.activation(gchunk[:, 512 * q:512 * (q + 1)],
                                              psG[idx % 2].ap()[:, :], AF.Silu,
                                              bias=self.brbi_cols[:, oc:oc + 1]).then_inc(s["gv"], 1)

                @blk.vector
                def _(vector):
                    for oc in range(NI):
                        vector.wait_ge(s["ldx"], 16 * (oc + 1))
                        if oc >= 1:
                            vector.wait_ge(s["stoS"], 16 * oc)
                        for q in range(4):
                            idx = oc * 4 + q
                            sl = slice(512 * q, 512 * (q + 1))
                            vector.wait_ge(s["mmS"], idx + 1)
                            vector.tensor_scalar(schunk[:, sl], psS[idx % 2].ap()[:, :],
                                                 self.bo_cols[:, oc:oc + 1], None,
                                                 OP.add).then_inc(s["evS"], 1)
                            vector.wait_ge(s["gv"], idx + 1)
                            if idx >= 2:
                                vector.wait_ge(s["sto"], 16 * (idx - 1))
                            vector.tensor_mul(x2b[idx % 2][:],
                                              schunk[:, sl], gchunk[:, sl])
                            vector.tensor_add(x2b[idx % 2][:],
                                              x2b[idx % 2][:],
                                              xold[oc % 2][:, sl]).then_inc(s["ev2"], 1)

                @blk.gpsimd
                def _(gpsimd):
                    for oc in range(NI):
                        for q in range(4):
                            idx = oc * 4 + q
                            gpsimd.wait_ge(s["ev2"], idx + 1)
                            gpsimd.dma_start(
                                out=self.x2spill.ap()[128 * oc:128 * (oc + 1),
                                                      512 * q:512 * (q + 1)],
                                in_=x2b[idx % 2][:]).then_inc(s["sto"], 16)
                        gpsimd.wait_ge(s["evS"], 4 * (oc + 1))
                        gpsimd.dma_start(out=self.s_f.ap()[128 * oc:128 * (oc + 1), :],
                                         in_=schunk[:]).then_inc(s["stoS"], 16)

    def p_l1_qk(self):
        nc = self.nc
        with contextlib.ExitStack() as st:
            s = self.sems(st, ["wld", "wsp", "mm", "sq", "nm", "ev", "sto"])
            BF = mybir.dt.bfloat16
            wf32 = self.sb(st, "wqkf32", [128, D])
            whB = [self.sb(st, "whB", [128, D], BF) for _ in range(2)]
            wlB = [self.sb(st, "wlB", [128, D], BF) for _ in range(2)]
            qkb = [self.sb(st, "qkb", [128, 512]) for _ in range(2)]
            nbounce = self.sb(st, "nbounce", [128, 512])
            sqb = [self.sb(st, "sqb", [128, 512]) for _ in range(2)]
            psm = [self.ps(st, "psqk", [128, 512]) for _ in range(2)]
            psnq = self.ps(st, "psnq", [128, 512])
            psnk = self.ps(st, "psnk", [128, 512])

            with nc.Block() as blk:
                @blk.sync
                def _(sync):
                    for oc in range(16):
                        if oc >= 1:
                            sync.wait_ge(s["wsp"], oc)
                        for i in range(NI):
                            sync.dma_start(
                                out=wf32[:, 128 * i:128 * (i + 1)],
                                in_=self.l1_wqkvT.ap()[128 * i:128 * (i + 1),
                                                       128 * oc:128 * (oc + 1)]).then_inc(s["wld"], 16)

                @blk.tensor
                def _(tensor):
                    for oc in range(16):
                        tensor.wait_ge(s["wsp"], oc + 1)
                        for q in range(4):
                            idx = oc * 4 + q
                            if idx >= 2:
                                tensor.wait_ge(s["ev"], idx - 1)
                                tensor.wait_ge(s["nm"], idx - 1)
                            ssl = slice(512 * q, 512 * (q + 1))
                            for i in range(NI):
                                wsl = slice(128 * i, 128 * (i + 1))
                                tensor.matmul(psm[idx % 2].ap()[:, :],
                                              whB[oc % 2][:, wsl],
                                              self.xh[i][:, ssl],
                                              start=(i == 0), stop=False)
                                tensor.matmul(psm[idx % 2].ap()[:, :],
                                              whB[oc % 2][:, wsl],
                                              self.xl[i][:, ssl],
                                              start=False, stop=False)
                                mmi = tensor.matmul(psm[idx % 2].ap()[:, :],
                                                    wlB[oc % 2][:, wsl],
                                                    self.xh[i][:, ssl],
                                                    start=False, stop=(i == NI - 1))
                            mmi.then_inc(s["mm"], 1)
                            if QK_MODE == "nonorm":
                                tensor.nop().then_inc(s["nm"], 1)
                                continue
                            tensor.wait_ge(s["sq"], idx + 1)
                            pn = psnq if oc < 8 else psnk
                            for half in range(2):
                                hcol = (2 * (oc % 8) + half) * 16
                                for m in range(4):
                                    col = hcol + q * 4 + m
                                    mmi = tensor.matmul(
                                        pn.ap()[:, 2 * col:2 * col + 2],
                                        sqb[idx % 2][64 * half:64 * half + 64,
                                                     128 * m:128 * (m + 1)],
                                        self.ones2[64 * half:64 * half + 64, :],
                                        start=True, stop=True)
                            mmi.then_inc(s["nm"], 1)

                @blk.scalar
                def _(scalar):
                    if QK_MODE == "nonorm":
                        pass
                    else:
                        for oc in range(16):
                            for q in range(4):
                                idx = oc * 4 + q
                                scalar.wait_ge(s["mm"], idx + 1)
                                if idx >= 2:
                                    scalar.wait_ge(s["nm"], idx - 1)
                                    scalar.wait_ge(s["ev"], idx - 1)
                                scalar.activation(sqb[idx % 2][:], psm[idx % 2].ap()[:, :],
                                                  AF.Square).then_inc(s["sq"], 1)

                @blk.vector
                def _(vector):
                    for oc in range(16):
                        vector.wait_ge(s["wld"], 16 * 8 * (oc + 1))
                        if oc >= 2:
                            vector.wait_ge(s["nm"], 4 * (oc - 1))
                        vector.tensor_copy(whB[oc % 2][:], wf32[:])
                        vector.tensor_sub(wf32[:], wf32[:], whB[oc % 2][:])
                        vector.tensor_copy(wlB[oc % 2][:], wf32[:]).then_inc(s["wsp"], 1)
                        for q in range(4):
                            idx = oc * 4 + q
                            vector.wait_ge(s["sq"], idx + 1)
                            if idx >= 2:
                                vector.wait_ge(s["sto"], 16 * (idx - 1))
                            vector.tensor_copy(qkb[idx % 2][:],
                                               psm[idx % 2].ap()[:, :]).then_inc(s["ev"], 1)
                    vector.wait_ge(s["nm"], 64)
                    vector.tensor_copy(nbounce[:], psnq.ap()[:, 0:512])
                    vector.tensor_copy(self.normq[:],
                                       bass.AP(nbounce, 0, [[512, 128], [2, 256]]))
                    vector.tensor_copy(nbounce[:], psnk.ap()[:, 0:512])
                    vector.tensor_copy(self.normk[:],
                                       bass.AP(nbounce, 0, [[512, 128], [2, 256]]))

                @blk.gpsimd
                def _(gpsimd):
                    for oc in range(16):
                        dstd = self.qf_sp if oc < 8 else self.kf_sp
                        for q in range(4):
                            idx = oc * 4 + q
                            gpsimd.wait_ge(s["ev"], idx + 1)
                            gpsimd.dma_start(
                                out=dstd.ap()[128 * (oc % 8):128 * (oc % 8 + 1),
                                              512 * q:512 * (q + 1)],
                                in_=qkb[idx % 2][:]).then_inc(s["sto"], 16)

    def p_l1_lam_bound(self):
        nc = self.nc
        with contextlib.ExitStack() as st:
            s = self.sems(st, ["io", "ln", "mm1", "rl", "mm2", "sgm", "l8",
                              "tp", "ev", "nb", "ld"])
            posrow = self.sb(st, "posrow", [1, T])
            logrow = self.sb(st, "logrow", [1, T])
            lamrhs = self.sb(st, "lamrhs", [2, T])
            h1 = self.sb(st, "h1", [64, T])
            lamf = self.sb(st, "lamf", [H, T])
            lam8f = self.sb(st, "lam8f", [H, T])
            lam1t = self.sb(st, "lam1t", [2, 64])
            lam2Tt = self.sb(st, "lam2Tt", [64, H])
            lam2bt = self.sb(st, "lam2bt", [1, H])
            mqcol = self.sb(st, "mqcol", [128, 2])
            mkcol = self.sb(st, "mkcol", [128, 2])
            mq2 = self.sb(st, "mq2", [1, 32])
            mk2 = self.sb(st, "mk2", [1, 32])
            brow = self.sb(st, "brow", [1, 32])
            ps1 = [self.ps(st, "pslam", [64, 512]) for _ in range(2)]
            ps2 = [self.ps(st, "pslam2", [H, 512]) for _ in range(2)]
            pst = [self.ps(st, "pslt", [128, 512]) for _ in range(2)]
            psr = self.ps(st, "pslr", [1, 128])
            psnb = self.ps(st, "psnb", [128, 32])

            with nc.Block() as blk:
                @blk.sync
                def _(sync):
                    sync.dma_start(out=lam1t[:], in_=self.l1_lam1.ap()[:, :]).then_inc(s["ld"], 16)
                    sync.dma_start(out=lam2Tt[:], in_=self.l1_lam2T.ap()[:, :]).then_inc(s["ld"], 16)
                    sync.dma_start(out=lam2bt[:], in_=self.l1_lam2b.ap()[:, :]).then_inc(s["ld"], 16)

                @blk.gpsimd
                def _(gpsimd):
                    gpsimd.iota(posrow[:], [[1, T]], channel_multiplier=0,
                                allow_small_or_imprecise_dtypes=True).then_inc(s["io"], 1)

                @blk.vector
                def _(vector):
                    vector.wait_ge(s["io"], 1)
                    vector.tensor_scalar(posrow[:], posrow[:], self.posb[:], 1.0,
                                         OP.add, OP.add).then_inc(s["io"], 1)
                    vector.wait_ge(s["ln"], 1)
                    vector.memset(lamrhs[:, :], 1.0)
                    vector.tensor_copy(lamrhs[0:1, :], logrow[:]).then_inc(s["io"], 1)
                    vector.wait_ge(s["sgm"], 4)
                    vector.tensor_scalar(lam8f[:], lamf[:], QSCALE, None,
                                         OP.mult).then_inc(s["l8"], 1)
                    for t in range(NT):
                        vector.wait_ge(s["tp"], t + 1)
                        vector.tensor_copy(self.lam8t[t][:], pst[t % 2].ap()[:, 0:H]).then_inc(s["ev"], 1)
                    for j, (src, dst) in enumerate([(self.normq, mqcol), (self.normk, mkcol)]):
                        for half in range(2):
                            vector.wait_ge(s["tp"], NT + 2 * j + half + 1)
                            vector.reduce_max(dst[:, half:half + 1],
                                              pst[(2 * j + half) % 2].ap()[:, 0:128],
                                              AX.X).then_inc(s["ev"], 1)
                    for j, (colt, dst) in enumerate([(mqcol, mq2), (mkcol, mk2)]):
                        for half in range(2):
                            vector.wait_ge(s["tp"], NT + 4 + 2 * j + half + 1)
                            v4 = bass.AP(psr, 0, [[128, 1], [16, 8], [8, 2], [1, 8]])
                            vector.reduce_max(dst[:, 16 * half:16 * (half + 1)],
                                              v4, AX.X).then_inc(s["ev"], 1)
                    vector.tensor_mul(brow[:], mq2[:], mk2[:]).then_inc(s["nb"], 1)
                    vector.wait_ge(s["nb"], 2)
                    vector.tensor_scalar(brow[:], brow[:], -QSCALE, None,
                                         OP.mult).then_inc(s["nb"], 1)
                    vector.wait_ge(s["nb"], 4)
                    vector.tensor_copy(self.negB[:], psnb.ap()[:, :])

                @blk.scalar
                def _(scalar):
                    scalar.wait_ge(s["io"], 2)
                    scalar.activation(logrow[:], posrow[:], AF.Ln).then_inc(s["ln"], 1)
                    for q in range(4):
                        scalar.wait_ge(s["mm1"], q + 1)
                        scalar.activation(h1[:, 512 * q:512 * (q + 1)],
                                          ps1[q % 2].ap()[:, :], AF.Relu).then_inc(s["rl"], 1)
                    for q in range(4):
                        scalar.wait_ge(s["mm2"], q + 1)
                        scalar.activation(lamf[:, 512 * q:512 * (q + 1)],
                                          ps2[q % 2].ap()[:, :], AF.Sigmoid).then_inc(s["sgm"], 1)
                    scalar.wait_ge(s["nb"], 1)
                    scalar.activation(brow[:], brow[:], AF.Sqrt).then_inc(s["nb"], 1)

                @blk.tensor
                def _(tensor):
                    tensor.wait_ge(s["io"], 3)
                    tensor.wait_ge(s["ld"], 48)
                    for q in range(4):
                        if q >= 2:
                            tensor.wait_ge(s["rl"], q - 1)
                        tensor.matmul(ps1[q % 2].ap()[:, :], lam1t[:],
                                      lamrhs[:, 512 * q:512 * (q + 1)],
                                      start=True, stop=True).then_inc(s["mm1"], 1)
                    for q in range(4):
                        if q >= 2:
                            tensor.wait_ge(s["sgm"], q - 1)
                        tensor.wait_ge(s["rl"], q + 1)
                        tensor.matmul(ps2[q % 2].ap()[:, :], lam2Tt[:],
                                      h1[:, 512 * q:512 * (q + 1)],
                                      start=True, stop=False)
                        tensor.matmul(ps2[q % 2].ap()[:, :], lam2bt[:],
                                      self.ones_row[:, 0:512],
                                      start=False, stop=True).then_inc(s["mm2"], 1)
                    tensor.wait_ge(s["l8"], 1)
                    for t in range(NT):
                        if t >= 2:
                            tensor.wait_ge(s["ev"], t - 1)
                        tensor.transpose(pst[t % 2].ap()[:, 0:H],
                                         lam8f[:, 128 * t:128 * (t + 1)],
                                         self.ident[0:H, 0:H]).then_inc(s["tp"], 1)
                    for j, src in enumerate([self.normq, self.normk]):
                        for half in range(2):
                            k = 2 * j + half
                            tensor.wait_ge(s["ev"], NT - 1 + k)
                            tensor.transpose(pst[k % 2].ap()[:, 0:128],
                                             src[:, 128 * half:128 * (half + 1)],
                                             self.ident[:]).then_inc(s["tp"], 1)
                    for j, colt in enumerate([mqcol, mkcol]):
                        for half in range(2):
                            k = 2 * j + half
                            tensor.wait_ge(s["ev"], NT + 2 + k)
                            tensor.transpose(psr.ap()[:, :], colt[:, half:half + 1],
                                             self.ident[:]).then_inc(s["tp"], 1)
                            tensor.wait_ge(s["ev"], NT + 4 + k + 1)
                    tensor.wait_ge(s["nb"], 3)
                    tensor.matmul(psnb.ap()[:, :], self.ones_r1[:], brow[:],
                                  start=True, stop=True).then_inc(s["nb"], 1)

    def p_l1_v(self):
        nc = self.nc
        with contextlib.ExitStack() as st:
            BF = mybir.dt.bfloat16
            s = self.sems(st, ["wld", "wsp", "mm", "ev", "sto"])
            wvf = self.sb(st, "wvf", [128, D])
            wvh = self.sb(st, "wvh", [128, 8 * D], BF)
            wvl = self.sb(st, "wvl", [128, 8 * D], BF)
            vb = [self.sb(st, "vb", [128, D]) for _ in range(2)]
            psm = [self.ps(st, "psv", [128, D]) for _ in range(2)]
            with nc.Block() as blk:
                @blk.sync
                def _(sync):
                    for i in range(NI):
                        if i >= 1:
                            sync.wait_ge(s["wsp"], i)
                        sync.dma_start(
                            out=wvf[:, :],
                            in_=self.l1_wqkvT.ap()[128 * i:128 * (i + 1), 2 * D:3 * D]).then_inc(s["wld"], 16)

                @blk.tensor
                def _(tensor):
                    tensor.wait_ge(s["wsp"], NI)
                    for t in range(NT):
                        if t >= 2:
                            tensor.wait_ge(s["ev"], t - 1)
                        tsl = slice(128 * t, 128 * (t + 1))
                        for i in range(NI):
                            for half in range(2):
                                psl = slice(512 * half, 512 * (half + 1))
                                wsl = slice(D * i + 512 * half, D * i + 512 * (half + 1))
                                tensor.matmul(psm[t % 2].ap()[:, psl],
                                              self.xh[i][:, tsl], wvh[:, wsl],
                                              start=(i == 0), stop=False)
                                tensor.matmul(psm[t % 2].ap()[:, psl],
                                              self.xh[i][:, tsl], wvl[:, wsl],
                                              start=False, stop=False)
                                mmi = tensor.matmul(psm[t % 2].ap()[:, psl],
                                                    self.xl[i][:, tsl], wvh[:, wsl],
                                                    start=False, stop=(i == NI - 1))
                        mmi.then_inc(s["mm"], 1)

                @blk.vector
                def _(vector):
                    for i in range(NI):
                        vector.wait_ge(s["wld"], 16 * (i + 1))
                        csl = slice(D * i, D * (i + 1))
                        vector.tensor_copy(wvh[:, csl], wvf[:])
                        vector.tensor_sub(wvf[:], wvf[:], wvh[:, csl])
                        vector.tensor_copy(wvl[:, csl], wvf[:]).then_inc(s["wsp"], 1)
                    for t in range(NT):
                        vector.wait_ge(s["mm"], t + 1)
                        if t >= 2:
                            vector.wait_ge(s["sto"], 16 * (t - 1))
                        vector.tensor_copy(vb[t % 2][:], psm[t % 2].ap()[:, :]).then_inc(s["ev"], 1)

                @blk.gpsimd
                def _(gpsimd):
                    for t in range(NT):
                        gpsimd.wait_ge(s["ev"], t + 1)
                        gpsimd.dma_start(out=self.vtok_sp.ap()[128 * t:128 * (t + 1), :],
                                         in_=vb[t % 2][:]).then_inc(s["sto"], 16)

    def p_l1_attn(self):
        nc = self.nc
        NKT = CS // 128
        with contextlib.ExitStack() as st:
            s = self.sems(st, ["sp", "ld", "vst", "mmS", "ex", "mmP", "evp",
                               "tp", "eva", "rc", "sto"])
            qfc = [self.xt[i][:, 0:CS] for i in range(NI)]
            kfc = [self.xt[i][:, CS:2 * CS] for i in range(NI)]
            vc = [self.sb(st, "vc", [128, 17 * DH]) for kt in range(NKT)]
            pt = [self.sb(st, "ptile", [128, CS]) for _ in range(NKT)]
            atk = [self.sb(st, "atk", [128, D]) for _ in range(NKT)]
            atf = [self.sb(st, "atf", [128, CS]) for _ in range(2)]
            rec = self.sb(st, "recip", [128, 1])
            psS = [self.ps(st, "psSc", [128, CS]) for _ in range(2)]
            psP = [self.ps(st, "psPV", [128, 512]) for _ in range(2)]
            psT = [self.ps(st, "psTr", [128, 512]) for _ in range(2)]
            NLD = 16 * (2 * NI + NKT)

            with nc.Block() as blk:
                @blk.gpsimd
                def _(gpsimd):
                    for ch in range(NCH):
                        for fc in range(NI):
                            gpsimd.wait_ge(s["eva"], ch * NI * NKT + (fc + 1) * NKT)
                            gpsimd.dma_start(
                                out=self.attnf_sp.ap()[128 * fc:128 * (fc + 1),
                                                       CS * ch:CS * (ch + 1)],
                                in_=atf[fc % 2][:]).then_inc(s["sto"], 16)

                @blk.sync
                def _(sync):
                    for ch in range(NCH):
                        if ch >= 1:
                            sync.wait_ge(s["mmP"], ch * H * NKT)
                        for i in range(NI):
                            sync.dma_start(out=qfc[i],
                                           in_=self.qf_sp.ap()[128 * i:128 * (i + 1),
                                                               CS * ch:CS * (ch + 1)]).then_inc(s["ld"], 16)
                            sync.dma_start(out=kfc[i],
                                           in_=self.kf_sp.ap()[128 * i:128 * (i + 1),
                                                               CS * ch:CS * (ch + 1)]).then_inc(s["ld"], 16)
                        for kt in range(NKT):
                            sync.dma_start(
                                out=bass.AP(vc[kt], 0, [[17 * DH, 128], [68, H], [1, DH]]),
                                in_=bass.AP(self.vtok_sp, (CS * ch + 128 * kt) * D,
                                            [[D, 128], [DH, H], [1, DH]])).then_inc(s["ld"], 16)

                @blk.vector
                def _(vector):
                    for ch in range(NCH):
                        vector.wait_ge(s["ld"], NLD * (ch + 1))
                        for kt in range(NKT):
                            vector.memset(bass.AP(vc[kt], DH, [[17 * DH, 128], [68, H], [1, 4]]), 0.0)
                            vi = vector.memset(bass.AP(vc[kt], DH, [[17 * DH, 128], [68, H]]), 1.0)
                        vi.then_inc(s["vst"], 1)
                        for h in range(H):
                            for qc in range(NKT):
                                idx = (ch * H + h) * NKT + qc
                                vector.wait_ge(s["mmP"], idx + 1)
                                vector.reciprocal(rec[:], psP[idx % 2].ap()[:, 64:65]).then_inc(s["rc"], 1)
                                vector.wait_ge(s["rc"], idx + 1)
                                vector.tensor_scalar(
                                    atk[qc][:, DH * h:DH * (h + 1)],
                                    psP[idx % 2].ap()[:, 0:DH],
                                    rec[:], None, OP.mult).then_inc(s["evp"], 1)
                        for fc in range(NI):
                            if ch * NI + fc >= 2:
                                vector.wait_ge(s["sto"], 16 * (ch * NI + fc - 1))
                            for qc in range(NKT):
                                tidx = (ch * NI + fc) * NKT + qc
                                vector.wait_ge(s["tp"], tidx + 1)
                                vector.tensor_copy(atf[fc % 2][:, 128 * qc:128 * (qc + 1)],
                                                   psT[tidx % 2].ap()[:, 0:128]).then_inc(s["eva"], 1)

                @blk.scalar
                def _(scalar):
                    for ch in range(NCH):
                        for h in range(H):
                            for kt in range(NKT):
                                sidx = (ch * H + h) * NKT + kt
                                scalar.wait_ge(s["mmS"], sidx + 1)
                                gt = ch * NKT + kt
                                scalar.activation(
                                    pt[kt][:], psS[sidx % 2].ap()[:, :], AF.Exp,
                                    scale=self.lam8t[gt][:, h:h + 1],
                                    bias=self.negB[:, 2 * h + ch:2 * h + ch + 1]).then_inc(s["ex"], 1)

                @blk.tensor
                def _(tensor):
                    for ch in range(NCH):
                        tensor.wait_ge(s["ld"], NLD * (ch + 1))
                        tensor.wait_ge(s["vst"], ch + 1)
                        for h in range(H):
                            oc, half = h // 2, h % 2
                            qsl = slice(64 * half, 64 * half + 64)
                            for kt in range(NKT):
                                sidx = (ch * H + h) * NKT + kt
                                if sidx >= 2:
                                    tensor.wait_ge(s["ex"], sidx - 1)
                                for h2 in range(2):
                                    mmi = tensor.matmul(
                                        psS[sidx % 2].ap()[:, 512 * h2:512 * (h2 + 1)],
                                        kfc[oc][qsl, 128 * kt:128 * (kt + 1)],
                                        qfc[oc][qsl, 512 * h2:512 * (h2 + 1)],
                                        start=True, stop=True)
                                mmi.then_inc(s["mmS"], 1)
                            tensor.wait_ge(s["ex"], (ch * H + h + 1) * NKT)
                            for qc in range(NKT):
                                idx = (ch * H + h) * NKT + qc
                                if idx >= 2:
                                    tensor.wait_ge(s["evp"], idx - 1)
                                for kt in range(NKT):
                                    mmi = tensor.matmul(
                                        psP[idx % 2].ap()[:, 0:66],
                                        pt[kt][:, 128 * qc:128 * (qc + 1)],
                                        bass.AP(vc[kt], 68 * h, [[17 * DH, 128], [1, 66]]),
                                        start=(kt == 0), stop=(kt == NKT - 1))
                                mmi.then_inc(s["mmP"], 1)
                        tensor.wait_ge(s["evp"], (ch + 1) * H * NKT)
                        for fc in range(NI):
                            for qc in range(NKT):
                                tidx = (ch * NI + fc) * NKT + qc
                                if tidx >= 2:
                                    tensor.wait_ge(s["eva"], tidx - 1)
                                tensor.transpose(psT[tidx % 2].ap()[:, 0:128],
                                                 atk[qc][:, 128 * fc:128 * (fc + 1)],
                                                 self.ident[:]).then_inc(s["tp"], 1)

    def p_l1_outproj(self):
        nc = self.nc
        atf = self.xt
        with contextlib.ExitStack() as st:
            s = self.sems(st, ["l0", "wld", "wfree", "mm", "ev", "sto"])
            wbuf = [self.sb(st, "wbuf2", [128, D]) for _ in range(2)]
            s1b = [self.sb(st, "s1b", [128, 512]) for _ in range(2)]
            psm = [self.ps(st, "pss1", [128, 512]) for _ in range(2)]
            with nc.Block() as blk:
                @blk.sync
                def _(sync):
                    for i in range(NI):
                        sync.dma_start(out=atf[i][:],
                                       in_=self.attnf_sp.ap()[128 * i:128 * (i + 1), :]).then_inc(s["l0"], 16)
                    for oc in range(NI):
                        if oc >= 2:
                            sync.wait_ge(s["ev"], 4 * (oc - 1))
                        for i in range(NI):
                            sync.dma_start(
                                out=wbuf[oc % 2][:, 128 * i:128 * (i + 1)],
                                in_=self.l1_woutT.ap()[128 * i:128 * (i + 1),
                                                       128 * oc:128 * (oc + 1)]).then_inc(s["wld"], 16)

                @blk.tensor
                def _(tensor):
                    tensor.wait_ge(s["l0"], 16 * NI)
                    for oc in range(NI):
                        tensor.wait_ge(s["wld"], 16 * 8 * (oc + 1))
                        for q in range(4):
                            idx = oc * 4 + q
                            if idx >= 2:
                                tensor.wait_ge(s["ev"], idx - 1)
                            for i in range(NI):
                                mmi = tensor.matmul(
                                    psm[idx % 2].ap()[:, :],
                                    wbuf[oc % 2][:, 128 * i:128 * (i + 1)],
                                    atf[i][:, 512 * q:512 * (q + 1)],
                                    start=(i == 0), stop=(i == NI - 1))
                            mmi.then_inc(s["mm"], 1)

                @blk.vector
                def _(vector):
                    for idx in range(NI * 4):
                        vector.wait_ge(s["mm"], idx + 1)
                        if idx >= 2:
                            vector.wait_ge(s["sto"], 16 * (idx - 1))
                        oc = idx // 4
                        vector.tensor_scalar(s1b[idx % 2][:], psm[idx % 2].ap()[:, :],
                                             self.bout_cols[:, oc:oc + 1], None,
                                             OP.add).then_inc(s["ev"], 1)

                @blk.gpsimd
                def _(gpsimd):
                    for idx in range(NI * 4):
                        oc, q = idx // 4, idx % 4
                        gpsimd.wait_ge(s["ev"], idx + 1)
                        gpsimd.dma_start(
                            out=self.s1_sp.ap()[128 * oc:128 * (oc + 1),
                                                512 * q:512 * (q + 1)],
                            in_=s1b[idx % 2][:]).then_inc(s["sto"], 16)

    def p_l1_ffn_final(self):
        nc = self.nc
        HC2 = 2
        NHG2 = (HID // 128) // HC2
        SW = 256
        TH = T // 2
        NS = TH // SW
        BF = mybir.dt.bfloat16

        with contextlib.ExitStack() as st:
            s = self.sems(st, ["xl", "spl", "wld", "mmGU", "sl", "hh",
                               "mmD", "evD", "g1", "lds", "fx", "tp", "evy",
                               "outd", "sto"])
            dacc = [self.sb(st, "dacc", [128, T]) for _ in range(NI)]
            xh = [self.sb(st, "fxh", [128, TH], BF) for _ in range(NI)]
            xl = [self.sb(st, "fxl", [128, TH], BF) for _ in range(NI)]
            xhi32 = self.sb(st, "xhi32", [128, TH])
            wgB = self.sb(st, "wgB", [128, 2048], BF)
            wuB = self.sb(st, "wuB", [128, 2048], BF)
            wdB = self.sb(st, "wdB", [128, 2048], BF)
            gsil = self.sb(st, "gsil", [128, SW])
            hh = [self.sb(st, "hh", [128, SW], BF) for _ in range(HC2)]
            hl = [self.sb(st, "hl", [128, SW], BF) for _ in range(HC2)]
            hsc = self.sb(st, "hsc", [128, SW])
            psG = [self.ps(st, "psGate", [128, 512]) for _ in range(2)]
            psU = [self.ps(st, "psUp", [128, 512]) for _ in range(2)]
            psD = [self.ps(st, "psD", [128, 512]) for _ in range(2)]
            psY = [self.ps(st, "psY", [128, 512]) for _ in range(2)]
            WLDG = 16 * (8 + 8 + 2)

            with nc.Block() as blk:
                @blk.sync
                def _(sync):
                    for th in range(2):
                        csl = slice(TH * th, TH * (th + 1))
                        bsl = slice(CS * (1 - th), CS * (1 - th) + CS)
                        s1c = self.xt[0][:, bsl]
                        x2c = self.xt[1][:, bsl]
                        for i in range(NI):
                            if th == 1:
                                sync.wait_ge(s["mmGU"], th * NHG2 * NS * HC2)
                            sync.dma_start(out=xsc[:] if False else xhi32[:],
                                           in_=self.xn1spill.ap()[128 * i:128 * (i + 1), csl]).then_inc(s["xl"], 16)
                            sync.wait_ge(s["spl"], th * NI + i + 1)
                        for oc in range(NI):
                            j = th * NI + oc
                            if j >= 1:
                                sync.wait_ge(s["fx"], j)
                            sync.dma_start(out=s1c[:],
                                           in_=self.s1_sp.ap()[128 * oc:128 * (oc + 1), csl]).then_inc(s["lds"], 16)
                            sync.dma_start(out=x2c[:],
                                           in_=self.x2spill.ap()[128 * oc:128 * (oc + 1), csl]).then_inc(s["lds"], 16)

                @blk.vector
                def _(vector):
                    for th in range(2):
                        bsl = slice(CS * (1 - th), CS * (1 - th) + CS)
                        s1c = self.xt[0][:, bsl]
                        x2c = self.xt[1][:, bsl]
                        g1b = self.xt[2][:, bsl]
                        xfb = self.xt[3][:, bsl]
                        if th == 1:
                            vector.wait_ge(s["outd"], 16 * NI)
                        for i in range(NI):
                            vector.wait_ge(s["xl"], 16 * (th * NI + i + 1))
                            vector.tensor_copy(xh[i][:], xhi32[:])
                            vector.tensor_sub(xhi32[:], xhi32[:], xh[i][:])
                            vector.tensor_copy(xl[i][:], xhi32[:]).then_inc(s["spl"], 1)
                        for hg in range(NHG2):
                            g = th * NHG2 + hg
                            for sp in range(NS):
                                gsp = g * NS + sp
                                if gsp >= 1:
                                    vector.wait_ge(s["mmD"], 4 * gsp)
                                for hc in range(HC2):
                                    j = gsp * HC2 + hc
                                    vector.wait_ge(s["sl"], j + 1)
                                    vector.tensor_tensor(hsc[:], gsil[:],
                                                         psU[j % 2].ap()[:, 0:SW], OP.mult)
                                    vector.tensor_copy(hh[hc][:], hsc[:])
                                    vector.tensor_sub(hsc[:], hsc[:], hh[hc][:])
                                    vector.tensor_copy(hl[hc][:], hsc[:]).then_inc(s["hh"], 1)
                                for g4 in range(4):
                                    didx = gsp * 4 + g4
                                    vector.wait_ge(s["mmD"], didx + 1)
                                    ssl = slice(TH * th + SW * sp, TH * th + SW * (sp + 1))
                                    for o2, oc in enumerate(range(2 * g4, 2 * g4 + 2)):
                                        osl = slice(SW * o2, SW * (o2 + 1))
                                        if hg == 0:
                                            vi = vector.tensor_copy(dacc[oc][:, ssl],
                                                                    psD[didx % 2].ap()[:, osl])
                                        else:
                                            vi = vector.tensor_add(dacc[oc][:, ssl],
                                                                   dacc[oc][:, ssl],
                                                                   psD[didx % 2].ap()[:, osl])
                                    vi.then_inc(s["evD"], 1)
                        for oc in range(NI):
                            j = th * NI + oc
                            vector.wait_ge(s["lds"], 32 * (j + 1))
                            vector.wait_ge(s["g1"], j + 1)
                            if j >= 1:
                                vector.wait_ge(s["tp"], j * (TH // 128))
                            vector.tensor_mul(xfb[:], s1c[:], g1b[:])
                            vector.tensor_add(xfb[:], xfb[:], x2c[:]).then_inc(s["fx"], 1)
                            for t in range(TH // 128):
                                tidx = j * (TH // 128) + t
                                vector.wait_ge(s["tp"], tidx + 1)
                                vector.tensor_copy(
                                    self.xt[t][:, CS * th + 128 * oc:CS * th + 128 * (oc + 1)],
                                    psY[tidx % 2].ap()[:, 0:128]).then_inc(s["evy"], 1)

                @blk.scalar
                def _(scalar):
                    for th in range(2):
                        bsl = slice(CS * (1 - th), CS * (1 - th) + CS)
                        g1b = self.xt[2][:, bsl]
                        for hg in range(NHG2):
                            g = th * NHG2 + hg
                            for sp in range(NS):
                                for hc in range(HC2):
                                    j = (g * NS + sp) * HC2 + hc
                                    scalar.wait_ge(s["mmGU"], j + 1)
                                    if j >= 1:
                                        scalar.wait_ge(s["hh"], j)
                                    scalar.activation(gsil[:], psG[j % 2].ap()[:, 0:SW], AF.Silu,
                                                      scale=1.0 / 127.0).then_inc(s["sl"], 1)
                        for oc in range(NI):
                            j = th * NI + oc
                            scalar.wait_ge(s["evD"], 4 * NS * NHG2 * (th + 1))
                            if j >= 1:
                                scalar.wait_ge(s["fx"], j)
                            scalar.activation(g1b[:], dacc[oc][:, TH * th:TH * (th + 1)],
                                              AF.Silu,
                                              scale=1.0 / (127.0 * 127.0)).then_inc(s["g1"], 1)

                @blk.tensor
                def _(tensor):
                    for th in range(2):
                        bsl = slice(CS * (1 - th), CS * (1 - th) + CS)
                        xfb = self.xt[3][:, bsl]
                        tensor.wait_ge(s["spl"], th * NI + NI)
                        for hg in range(NHG2):
                            g = th * NHG2 + hg
                            for sp in range(NS):
                                gsp = g * NS + sp
                                ssl = slice(SW * sp, SW * (sp + 1))
                                for hc in range(HC2):
                                    j = gsp * HC2 + hc
                                    tensor.wait_ge(s["sto"], 16 * (18 * g + 16))
                                    if j >= 2:
                                        tensor.wait_ge(s["sl"], j - 1)
                                        tensor.wait_ge(s["hh"], j - 1)
                                    for i in range(NI):
                                        wsl = slice(256 * i + 128 * hc, 256 * i + 128 * (hc + 1))
                                        tensor.matmul(psG[j % 2].ap()[:, 0:SW], wgB[:, wsl],
                                                      xh[i][:, ssl],
                                                      start=(i == 0), stop=False)
                                        tensor.matmul(psG[j % 2].ap()[:, 0:SW], wgB[:, wsl],
                                                      xl[i][:, ssl],
                                                      start=False, stop=(i == NI - 1))
                                    for i in range(NI):
                                        wsl = slice(256 * i + 128 * hc, 256 * i + 128 * (hc + 1))
                                        tensor.matmul(psU[j % 2].ap()[:, 0:SW], wuB[:, wsl],
                                                      xh[i][:, ssl],
                                                      start=(i == 0), stop=False)
                                        mmi = tensor.matmul(psU[j % 2].ap()[:, 0:SW], wuB[:, wsl],
                                                            xl[i][:, ssl],
                                                            start=False, stop=(i == NI - 1))
                                    mmi.then_inc(s["mmGU"], 1)
                                tensor.wait_ge(s["sto"], 16 * 18 * (g + 1))
                                tensor.wait_ge(s["hh"], gsp * HC2 + HC2)
                                for g4 in range(4):
                                    didx = gsp * 4 + g4
                                    if didx >= 2:
                                        tensor.wait_ge(s["evD"], didx - 1)
                                    for o2, oc in enumerate(range(2 * g4, 2 * g4 + 2)):
                                        osl = slice(SW * o2, SW * (o2 + 1))
                                        for hc in range(HC2):
                                            wsl = slice(1024 * hc + 128 * oc, 1024 * hc + 128 * (oc + 1))
                                            tensor.matmul(psD[didx % 2].ap()[:, osl],
                                                          wdB[:, wsl], hh[hc][:],
                                                          start=(hc == 0), stop=False,
                                                          skip_group_check=True)
                                            mmi = tensor.matmul(psD[didx % 2].ap()[:, osl],
                                                                wdB[:, wsl], hl[hc][:],
                                                                start=False, stop=(hc == HC2 - 1),
                                                                skip_group_check=True)
                                    mmi.then_inc(s["mmD"], 1)
                        for oc in range(NI):
                            j = th * NI + oc
                            tensor.wait_ge(s["fx"], j + 1)
                            for t in range(TH // 128):
                                tidx = j * (TH // 128) + t
                                if tidx >= 2:
                                    tensor.wait_ge(s["evy"], tidx - 1)
                                tensor.transpose(psY[tidx % 2].ap()[:, 0:128],
                                                 xfb[:, 128 * t:128 * (t + 1)],
                                                 self.ident[:]).then_inc(s["tp"], 1)

                @blk.gpsimd
                def _(gpsimd):
                    for th in range(2):
                        for hg in range(NHG2):
                            g = th * NHG2 + hg
                            if g >= 1:
                                gpsimd.wait_ge(s["mmD"], 4 * NS * g)
                            for i in range(NI):
                                gpsimd.dma_start(
                                    out=wgB[:, 256 * i:256 * (i + 1)],
                                    in_=self.l1_wgT.ap()[128 * i:128 * (i + 1),
                                                         256 * hg:256 * (hg + 1)]).then_inc(s["sto"], 16)
                            for i in range(NI):
                                gpsimd.dma_start(
                                    out=wuB[:, 256 * i:256 * (i + 1)],
                                    in_=self.l1_wuT.ap()[128 * i:128 * (i + 1),
                                                         256 * hg:256 * (hg + 1)]).then_inc(s["sto"], 16)
                            for j in range(HC2):
                                gpsimd.dma_start(
                                    out=wdB[:, 1024 * j:1024 * (j + 1)],
                                    in_=self.l1_wdT.ap()[256 * hg + 128 * j:256 * hg + 128 * (j + 1),
                                                         :]).then_inc(s["sto"], 16)
                        for t in range(TH // 128):
                            gt = th * (TH // 128) + t
                            gpsimd.wait_ge(s["evy"], (th + 1) * NI * (TH // 128))
                            gpsimd.dma_start(
                                out=self.y_out.ap()[128 * gt:128 * (gt + 1), :],
                                in_=self.xt[t][:, CS * th:CS * th + D]).then_inc(s["outd"], 16)
                    gpsimd.wait_ge(s["outd"], 16 * NT)

    def p_taps(self):
        if not self.taps:
            return
        nc = self.nc
        with contextlib.ExitStack() as st:
            s = self.sems(st, ["sto"])
            n = [0]
            with nc.Block() as blk:
                @blk.gpsimd
                def _(gpsimd):
                    for name, (srcd, dst) in self.taps.items():
                        gpsimd.dma_start(out=dst.ap()[:, :], in_=srcd.ap()[:, :]).then_inc(s["sto"], 16)
                        n[0] += 16
                    gpsimd.wait_ge(s["sto"], n[0])


def build(tap_names=(), stop_after=None):
    return Builder(tap_names, stop_after=stop_after).build()


def _prep_shared(inputs):
    f = lambda k: np.asarray(inputs[k], dtype=np.float32)
    sh = {}
    sh["l0_wqkv"] = np.ascontiguousarray(
        np.concatenate([f("L0_attn_Wq"), f("L0_attn_Wk"), f("L0_attn_Wv")], axis=1))
    sh["l0_bqkv"] = np.concatenate(
        [f("L0_attn_bq"), f("L0_attn_bk"), f("L0_attn_bv")]).reshape(1, 3 * D)
    sh["l0_wo"] = np.ascontiguousarray(f("L0_attn_Wo"))
    sh["l0_bo"] = np.ascontiguousarray(f("L0_attn_bo").reshape(8, 128).T)
    sh["l0_m"] = f("L0_attn_m").reshape(1, H)
    G1R, G2R = f("L0_ffn_G1_R"), f("L0_ffn_G2_R")
    G1I, G2I = f("L0_ffn_G1_I"), f("L0_ffn_G2_I")

    def s_arr(G1):
        out = np.empty((4, FF, FF), dtype=np.float32)
        for c in range(4):
            g1c = G1[0, :, c, :]
            out[c] = np.broadcast_to(g1c.T[:, None, :, None],
                                     (32, 16, 32, 16)).reshape(FF, FF)
        return np.ascontiguousarray(out.reshape(4 * FF, FF))

    def t_arr(G2):
        out = np.empty((4, FF, FF), dtype=np.float32)
        for c in range(4):
            g2c = G2[c, :, 0, :]
            out[c] = np.broadcast_to(g2c.T[None, :, None, :],
                                     (32, 16, 32, 16)).reshape(FF, FF)
        return np.ascontiguousarray(out.reshape(4 * FF, FF))

    sh["l0_SR"], sh["l0_TR"] = s_arr(G1R), t_arr(G2R)
    sh["l0_SI"], sh["l0_TI"] = s_arr(G1I), t_arr(G2I)
    sh["l0_brbi"] = np.ascontiguousarray(
        np.concatenate([f("L0_ffn_bias_real"), f("L0_ffn_bias_imag")]).reshape(8, 128).T)
    sh["l1_wqkvT"] = np.ascontiguousarray(f("L1_attn_Wqkv").T)
    sh["l1_woutT"] = np.ascontiguousarray(f("L1_attn_Wout").T)
    sh["l1_bout"] = np.ascontiguousarray(f("L1_attn_bout").reshape(8, 128).T)
    sh["l1_lam1"] = np.ascontiguousarray(
        np.stack([f("L1_attn_Wl1")[:, 0], f("L1_attn_bl1")]))
    sh["l1_lam2T"] = np.ascontiguousarray(f("L1_attn_Wl2").T)
    sh["l1_lam2b"] = f("L1_attn_bl2").reshape(1, H)
    sh["l1_wgT"] = np.ascontiguousarray(np.asarray(inputs["L1_ffn_w_gate"]).T)
    sh["l1_wuT"] = np.ascontiguousarray(np.asarray(inputs["L1_ffn_w_up"]).T)
    sh["l1_wdT"] = np.ascontiguousarray(np.asarray(inputs["L1_ffn_w_down"]).T)
    sh["ident_in"] = np.eye(128, dtype=np.float32)
    return sh


def make_in_maps(inputs):
    shared = _prep_shared(inputs)
    maps = []
    for c in range(8):
        b, t0 = c // 4, (c % 4) * T
        d = dict(shared)
        x = np.asarray(inputs["L0_attn_x"][b, t0:t0 + T, :], dtype=np.float32)
        d["x_in"] = np.ascontiguousarray(x.T)
        d["posb_in"] = np.array([[float(t0)]], dtype=np.float32)
        maps.append(d)
    return maps


def kernel(**inputs):
    in_maps = make_in_maps(inputs)
    nc = build()
    res = run_bass_kernel_spmd(nc, in_maps, core_ids=list(range(8)))
    y = np.stack([np.asarray(res.results[c]["y_out"]) for c in range(8)])
    return np.ascontiguousarray(y.reshape(2, 4 * T, D)).astype(np.float32)
